# version 18
# baseline (speedup 1.0000x reference)
import numpy as np
import concourse.bass as bass
import concourse.mybir as mybir
from concourse.bass_utils import run_bass_kernel_spmd
from concourse.bass_core_rust import IndirectOffsetOnAxis

F32 = mybir.dt.float32
BF16 = mybir.dt.bfloat16
I32 = mybir.dt.int32
AF = mybir.ActivationFunctionType
ALU = mybir.AluOpType
AX = mybir.AxisListType.X

S = 2048
E = 1024
H = 16
D = 64
NT = S // 128
NCH = E // 128
TW = 2176
ETW = 4096
LC = 512
NLC = S // NLC if False else S // LC


class Prog:
    ENGS = ("sync", "vector", "scalar", "tensor", "gpsimd")

    def __init__(self):
        self.items = {e: [] for e in self.ENGS}
        self.counts = {}
        self.pending = {e: [] for e in self.ENGS}
        self.hazard = {}

    def sem(self, name):
        self.counts[name] = 0

    FLUSH = {"vector": "tv", "scalar": "ts"}

    def add(self, eng, fn, waits=(), inc=None, amt=1, hz=False):
        w = self.pending[eng] + list(waits)
        self.pending[eng] = []
        fs = self.FLUSH.get(eng)
        if fs is not None and (hz or self.hazard.get(eng, False)) \
                and self.counts.get(fs, 0) > 0:
            w.append((fs, self.counts[fs]))
        self.hazard[eng] = hz
        self.items[eng].append((fn, w, inc, amt))
        if inc is not None:
            self.counts[inc] += amt
        return self.counts.get(inc)

    def barrier(self, eng, waits):
        self.pending[eng].extend(waits)

    def at(self, sem):
        return self.counts[sem]


class Mem:

    def __init__(self, nc):
        self.nc = nc
        self.live = {}

    def alloc(self, name, shape, dt, side=None):
        if side is None:
            cm = self.nc.sbuf_tensor(name, shape, dt)
        else:
            cm = self.nc.sbuf_tensor(name, shape, dt, side=side)
        h = cm.__enter__()
        self.live[name] = (cm, h)
        return h

    def free(self, *names):
        for n in names:
            cm, _ = self.live.pop(n)
            cm.__exit__(None, None, None)

    def free_all(self):
        for n in reversed(list(self.live)):
            self.free(n)


DEBUG = False


def build_program(debug=False):
    nc = bass.Bass("TRN2", target_bir_lowering=False, debug=False)

    d_x = nc.dram_tensor("x", [S, E], F32, kind="ExternalInput").ap()
    d_wqT = nc.dram_tensor("wqT", [E, E], F32, kind="ExternalInput").ap()
    d_wkT = nc.dram_tensor("wkT", [E, E], F32, kind="ExternalInput").ap()
    d_wvT = nc.dram_tensor("wvT", [E, E], F32, kind="ExternalInput").ap()
    d_bq = nc.dram_tensor("bq", [1, E], F32, kind="ExternalInput").ap()
    d_bk = nc.dram_tensor("bk", [1, E], F32, kind="ExternalInput").ap()
    d_bv = nc.dram_tensor("bv", [1, E], F32, kind="ExternalInput").ap()
    d_etf = nc.dram_tensor("etf", [D, ETW], F32, kind="ExternalInput").ap()
    d_etr = nc.dram_tensor("etr", [D, ETW], F32, kind="ExternalInput").ap()
    d_ffw = nc.dram_tensor("ffw", [E, E], F32, kind="ExternalInput").ap()
    d_ffb = nc.dram_tensor("ffb", [1, E], F32, kind="ExternalInput").ap()
    d_idxA = nc.dram_tensor("idxA", [128, 1], I32, kind="ExternalInput").ap()
    d_idxB = nc.dram_tensor("idxB", [128, 1], I32, kind="ExternalInput").ap()
    d_iden = nc.dram_tensor("idn", [128, 128], F32, kind="ExternalInput").ap()
    d_y = nc.dram_tensor("y", [S, E], F32, kind="ExternalOutput").ap()

    ikind = "ExternalOutput" if debug else "Internal"
    d_qe = nc.dram_tensor("qe_st", [H, NT, 128, TW], BF16, kind=ikind).ap()
    d_ke = nc.dram_tensor("ke_st", [H, NT, 128, TW], BF16, kind=ikind).ap()
    d_qe_flat = bass.AP(d_qe.tensor, 0, [[TW, H * NT * 128], [1, TW]])
    d_ke_flat = bass.AP(d_ke.tensor, 0, [[TW, H * NT * 128], [1, TW]])
    d_ctx = nc.dram_tensor("ctx_st", [H, S, D], F32, kind=ikind).ap()
    d_dbg = {}
    if debug:
        for nm, dt_ in [("xln", BF16), ("xlnT", BF16), ("qT", BF16),
                        ("kT", BF16), ("v65", BF16), ("negB", F32),
                        ("A", F32), ("Bt", BF16), ("scr", F32), ("X", BF16),
                        ("xpa", F32), ("xd", BF16), ("wd", BF16),
                        ("wq0", BF16), ("brow", BF16), ("qTe", BF16)]:
            d_dbg[nm] = nc.dram_tensor("dbg_" + nm, [128, 2048], dt_,
                                       kind="ExternalOutput").ap()

    M = Mem(nc)
    sb_xt = M.alloc("sb_xt", [128, E], F32)
    sb_xt2 = M.alloc("sb_xt2", [128, E], F32)
    sb_brow = M.alloc("sb_brow", [1, 3 * E], BF16)
    sb_ffb = M.alloc("sb_ffb", [1, E], BF16)
    sb_iden = M.alloc("sb_iden", [128, 128], BF16)
    sb_idenf = M.alloc("sb_idenf", [128, 128], F32)
    sb_ones = M.alloc("sb_ones", [128, 512], BF16)
    sb_idxA = M.alloc("sb_idxA", [128, 1], I32)
    sb_idxB = M.alloc("sb_idxB", [128, 1], I32)
    sb_tmp = M.alloc("sb_tmp", [1, 8], F32)
    sb_mv = M.alloc("sb_mv", [128, 4], F32)
    sb_zr = M.alloc("sb_zr", [128, 4], F32)
    sb_negB = M.alloc("sb_negB", [128, H], F32)
    sb_nbh = M.alloc("sb_nbh", [1, 2], BF16)
    sb_dump = M.alloc("sb_dump", [128, E], F32)
    sb_ctv = M.alloc("sb_ctv", [128, 512], F32)
    sb_ctp = M.alloc("sb_ctp", [128, 4 * D], F32)
    sb_edg = M.alloc("sb_edg", [128, 2], BF16)
    sb_qe2 = M.alloc("sb_qe2", [1, 2 * S], BF16)

    psum_cms = []

    def psum(name, shape, dt):
        cm = nc.psum_tensor(name, shape, dt)
        h = cm.__enter__()
        psum_cms.append(cm)
        return h

    ps_a = psum("ps_a", [128, 512], F32)
    ps_b = psum("ps_b", [128, 512], F32)
    ps_c = psum("ps_c", [128, 512], F32)
    ps_d = psum("ps_d", [128, 512], F32)
    ps_big = psum("ps_big", [128, 2048], F32)
    psA_bf = ps_a.bitcast(BF16)
    psB_bf = ps_b.bitcast(BF16)

    sem_cms = []

    def mksem(name):
        cm = nc.semaphore(name)
        h = cm.__enter__()
        sem_cms.append(cm)
        return h

    sems = ["sd", "gd", "tv", "ts", "tt"]
    sem_h = {s: mksem(s) for s in sems}

    block_cm = nc.Block()
    block = block_cm.__enter__()

    P = Prog()
    for s in sems:
        P.sem(s)

    def tap(nm, src_fn, wait_sem="tv"):
        if not debug:
            return
        P.add("sync", lambda e: e.dma_start(out=d_dbg[nm][0:src_fn().shape[0],
                                                          0:src_fn().shape[1]],
                                            in_=src_fn(),
                                            allow_slow_non_contiguous=True),
              waits=[(wait_sem, P.at(wait_sem))], inc="sd", amt=16)

    sb_xlnT = M.alloc("sb_xlnT", [128, NCH * S], BF16, side="right")
    sb_wv = M.alloc("sb_wv", [128, NCH * E], BF16, side="right")
    sb_wk = M.alloc("sb_wk", [128, NCH * E], BF16, side="right")
    sb_wq = M.alloc("sb_wq", [128, NCH * E], BF16, side="right")
    sb_xln = M.alloc("sb_xln", [128, NT * E], BF16, side="right")

    P.add("sync", lambda e: e.dma_start(out=sb_idxA[:], in_=d_idxA), inc="sd", amt=16)
    P.add("sync", lambda e: e.dma_start(out=sb_idxB[:], in_=d_idxB), inc="sd", amt=16)
    P.add("sync", lambda e: e.dma_start(out=sb_idenf[:], in_=d_iden), inc="sd", amt=16)
    P.add("gpsimd", lambda e: e.dma_start(out=sb_iden[:], in_=d_iden), inc="gd", amt=16)
    for ch in range(NCH):
        for wsb, wd_ in ((sb_wq, d_wqT), (sb_wk, d_wkT), (sb_wv, d_wvT)):
            P.add("gpsimd", lambda e, c=ch, wsb=wsb, wd_=wd_: e.dma_start(
                out=wsb[:, E * c:E * c + E], in_=wd_[128 * c:128 * c + 128, :]),
                inc="gd", amt=16)
    P.add("gpsimd", lambda e: e.dma_start(out=sb_brow[:, 0:E], in_=d_bq), inc="gd", amt=16)
    P.add("gpsimd", lambda e: e.dma_start(out=sb_brow[:, E:2 * E], in_=d_bk), inc="gd", amt=16)
    P.add("gpsimd", lambda e: e.dma_start(out=sb_brow[:, 2 * E:3 * E], in_=d_bv), inc="gd", amt=16)
    P.add("gpsimd", lambda e: e.dma_start(out=sb_ffb[:], in_=d_ffb), inc="gd", amt=16)
    gd_loads = P.at("gd")

    P.add("vector", lambda e: e.memset(sb_ones[:], 1.0), inc="tv")

    def emit_ln(dst_ap_fn, src_sb, first_waits=()):
        P.add("vector", lambda e: e.reduce_sum(
            sb_mv[:, 0:1], src_sb[:], axis=AX), waits=first_waits, inc="tv", hz=True)
        P.add("vector", lambda e: e.tensor_scalar(
            sb_mv[:, 0:1], sb_mv[:, 0:1], 1.0 / E, None, ALU.mult), inc="tv")
        P.add("scalar", lambda e: e.activation(
            sb_dump[:], src_sb[:], AF.Square, accum_out=sb_mv[:, 1:2]),
            waits=[("tv", P.at("tv"))], inc="ts")
        P.add("vector", lambda e: e.tensor_mul(
            sb_mv[:, 2:3], sb_mv[:, 0:1], sb_mv[:, 0:1]),
            waits=[("ts", P.at("ts"))], inc="tv")
        P.add("vector", lambda e: e.tensor_scalar(
            sb_mv[:, 1:2], sb_mv[:, 1:2], 1.0 / E, None, ALU.mult), inc="tv")
        P.add("vector", lambda e: e.tensor_sub(
            sb_mv[:, 1:2], sb_mv[:, 1:2], sb_mv[:, 2:3]), inc="tv")
        P.add("vector", lambda e: e.tensor_scalar(
            sb_mv[:, 1:2], sb_mv[:, 1:2], 1e-5, None, ALU.add), inc="tv")
        P.add("scalar", lambda e: e.activation(
            sb_mv[:, 3:4], sb_mv[:, 1:2], AF.Sqrt),
            waits=[("tv", P.at("tv"))], inc="ts")
        P.add("vector", lambda e: e.reciprocal(
            sb_mv[:, 2:3], sb_mv[:, 3:4]), waits=[("ts", P.at("ts"))], inc="tv", hz=True)
        P.add("vector", lambda e: e.tensor_scalar(
            dst_ap_fn(), src_sb[:], sb_mv[:, 0:1], sb_mv[:, 2:3],
            ALU.subtract, ALU.mult), inc="tv")

    for t in range(NT):
        buf = sb_xt if t % 2 == 0 else sb_xt2
        w = [("tv", P.at("tv"))] if t >= 2 else []
        P.add("sync", lambda e, t=t, b=buf: e.dma_start(
            out=b[:], in_=d_x[128 * t:128 * t + 128, :]), waits=w, inc="sd", amt=16)
        emit_ln(lambda t=t: sb_xln[:, E * t:E * t + E], buf,
                first_waits=[("sd", P.at("sd"))])
        if t == 0:
            tap("xln", lambda: sb_xln[:, 0:E])
    ln1_tv = P.at("tv")

    tp_i = 0
    for t in range(NT):
        for ch in range(NCH):
            pb = psA_bf if tp_i % 2 == 0 else psB_bf
            w = [("tv", P.at("tv"))] if tp_i >= 2 else [("tv", ln1_tv)]
            P.add("tensor", lambda e, t=t, c=ch, p=pb: e.transpose(
                p[:, 0:128], sb_xln[:, E * t + 128 * c:E * t + 128 * c + 128],
                sb_iden[:]), waits=w, inc="tt")
            P.add("vector", lambda e, t=t, c=ch, p=pb: e.tensor_copy(
                sb_xlnT[:, S * c + 128 * t:S * c + 128 * t + 128], p[:, 0:128]),
                waits=[("tt", P.at("tt")), ("gd", gd_loads)], inc="tv")
            tp_i += 1
    xlnT_tv = P.at("tv")
    tap("xlnT", lambda: sb_xlnT[:, 0:2048])
    M.free("sb_xln")
    sb_qT = M.alloc("sb_qT", [128, NCH * S], BF16)
    sb_kT = M.alloc("sb_kT", [128, NCH * S], BF16)

    P.barrier("tensor", [("tv", xlnT_tv)])
    for ch in range(NCH):
        for wsb, qsb, boff in ((sb_wq, sb_qT, 0), (sb_wk, sb_kT, E)):
            for ec in range(NCH):
                for ns in range(4):
                    P.add("tensor", lambda e, ec=ec, ns=ns, ch=ch, wsb=wsb: e.matmul(
                        ps_big[:, 512 * ns:512 * ns + 512],
                        wsb[:, E * ec + 128 * ch:E * ec + 128 * ch + 128],
                        sb_xlnT[:, S * ec + 512 * ns:S * ec + 512 * ns + 512],
                        start=(ec == 0), stop=False), inc="tt")
            for ns in range(4):
                P.add("tensor", lambda e, ns=ns, ch=ch, boff=boff: e.matmul(
                    ps_big[:, 512 * ns:512 * ns + 512],
                    sb_brow[0:1, boff + 128 * ch:boff + 128 * ch + 128],
                    sb_ones[0:1, 0:512], start=False, stop=True), inc="tt")
            P.add("vector", lambda e, ch=ch, qsb=qsb: e.tensor_copy(
                qsb[:, S * ch:S * ch + S], ps_big[:, :]),
                waits=[("tt", P.at("tt"))], inc="tv")
            P.barrier("tensor", [("tv", P.at("tv"))])
            if ch == 0 and qsb is sb_qT:
                tap("qTe", lambda: sb_qT[:, 0:2048])
                tap("wq0", lambda: sb_wq[:, 0:2048])
                tap("brow", lambda: sb_brow[0:1, 0:2048])
    M.free("sb_wq", "sb_wk")
    sb_v65 = M.alloc("sb_v65", [128, NT * H * 65], BF16)
    for t in range(NT):
        P.add("vector", lambda e, t=t: e.memset(
            bass.AP(sb_v65, 1040 * t + 64, [[NT * H * 65, 128], [65, H], [1, 1]]),
            1.0), inc="tv")
    for t in range(NT):
        for ec in range(NCH):
            for ns in range(2):
                P.add("tensor", lambda e, ec=ec, ns=ns, t=t: e.matmul(
                    ps_big[:, 512 * ns:512 * ns + 512],
                    sb_xlnT[:, S * ec + 128 * t:S * ec + 128 * t + 128],
                    sb_wv[:, E * ec + 512 * ns:E * ec + 512 * ns + 512],
                    start=(ec == 0), stop=False), inc="tt")
        for ns in range(2):
            P.add("tensor", lambda e, ns=ns: e.matmul(
                ps_big[:, 512 * ns:512 * ns + 512],
                sb_ones[0:1, 0:128],
                sb_brow[0:1, 2 * E + 512 * ns:2 * E + 512 * ns + 512],
                start=False, stop=True), inc="tt")

        def v_evac(e, t=t):
            dst = bass.AP(sb_v65, 1040 * t, [[NT * H * 65, 128], [65, H], [1, 64]])
            src = bass.AP(ps_big, 0, [[2048, 128], [64, H], [1, 64]])
            return e.tensor_copy(dst, src)
        P.add("vector", v_evac, waits=[("tt", P.at("tt"))], inc="tv")
        P.barrier("tensor", [("tv", P.at("tv"))])
    proj_tv = P.at("tv")
    tap("qT", lambda: sb_qT[:, 0:2048])
    tap("kT", lambda: sb_kT[:, 0:2048])
    tap("v65", lambda: sb_v65[:, 0:2048])
    M.free("sb_wv", "sb_xlnT")

    sb_etf = M.alloc("sb_etf", [128, ETW], BF16)
    sb_etr = M.alloc("sb_etr", [128, ETW], BF16)
    sb_stg = M.alloc("sb_stg", [128, 2 * TW], BF16)
    sb_A = M.alloc("sb_A", [128, 8 * S], BF16)
    sb_Ss = M.alloc("sb_Ss", [128, 2 * LC], F32)
    sb_B = M.alloc("sb_B", [128, 8 * LC], BF16)
    sb_X = M.alloc("sb_X", [128, 4 * LC], BF16)
    for half in range(2):
        P.add("gpsimd", lambda e, hf=half: e.dma_start(
            out=sb_etf[64 * hf:64 * hf + 64, :], in_=d_etf),
            waits=[("tv", proj_tv)] if half == 0 else [], inc="gd", amt=16)
        P.add("gpsimd", lambda e, hf=half: e.dma_start(
            out=sb_etr[64 * hf:64 * hf + 64, :], in_=d_etr), inc="gd", amt=16)
    gd_tbl = P.at("gd")

    sb_sq = sb_A

    P.add("vector", lambda e: e.tensor_copy(sb_edg[:, 0:1], sb_etf[:, 0:1]),
          waits=[("gd", gd_tbl)], inc="tv")
    P.add("vector", lambda e: e.tensor_copy(sb_edg[:, 1:2], sb_etf[:, ETW - 1:ETW]),
          inc="tv")

    P.add("vector", lambda e: e.tensor_mul(
        sb_sq[0:64, 0:2048], sb_etf[0:64, 0:2048], sb_etf[0:64, 0:2048]),
        waits=[("gd", gd_tbl), ("tt", P.at("tt"))], inc="tv")
    P.barrier("tensor", [("tv", P.at("tv"))])
    for ns in range(4):
        P.add("tensor", lambda e, ns=ns: e.matmul(
            ps_big[0:1, 512 * ns:512 * ns + 512], sb_ones[0:64, 0:1],
            sb_sq[0:64, 512 * ns:512 * ns + 512], start=True, stop=False), inc="tt")
    P.add("vector", lambda e: e.tensor_mul(
        sb_sq[0:64, 0:2048], sb_etf[0:64, 2048:4096], sb_etf[0:64, 2048:4096]),
        waits=[("tt", P.at("tt"))], inc="tv")
    P.barrier("tensor", [("tv", P.at("tv"))])
    for ns in range(4):
        P.add("tensor", lambda e, ns=ns: e.matmul(
            ps_big[0:1, 512 * ns:512 * ns + 512], sb_ones[0:64, 0:1],
            sb_sq[0:64, 512 * ns:512 * ns + 512], start=False, stop=True), inc="tt")
    P.add("vector", lambda e: e.reduce_max(
        sb_tmp[0:1, 0:1], ps_big[0:1, :], axis=AX),
        waits=[("tt", P.at("tt"))], inc="tv", hz=True)
    P.add("scalar", lambda e: e.activation(
        sb_tmp[0:1, 1:2], sb_tmp[0:1, 0:1], AF.Sqrt),
        waits=[("tv", P.at("tv"))], inc="ts")
    me_ts = P.at("ts")
    P.barrier("tensor", [("tv", P.at("tv"))])

    sq_parity = 0
    for h in range(H):
        ch, half = h // 2, h % 2
        qsl = slice(64 * half, 64 * half + 64)
        tpos = (64 * half, 0)
        for src, col in ((sb_qT, 2), (sb_kT, 3)):
            P.add("vector", lambda e, src=src, ch=ch, qsl=qsl: e.tensor_mul(
                sb_sq[qsl, 0:S], src[qsl, S * ch:S * ch + S],
                src[qsl, S * ch:S * ch + S]),
                waits=[("tv", proj_tv), ("tt", P.at("tt"))], inc="tv")
            P.barrier("tensor", [("tv", P.at("tv"))])
            for ns in range(4):
                P.add("tensor", lambda e, ns=ns, qsl=qsl, tpos=tpos: e.matmul(
                    ps_big[0:1, 512 * ns:512 * ns + 512], sb_ones[qsl, 0:1],
                    sb_sq[qsl, 512 * ns:512 * ns + 512], start=True, stop=True,
                    tile_position=tpos), inc="tt")
            P.add("vector", lambda e, col=col: e.reduce_max(
                sb_tmp[0:1, col:col + 1], ps_big[0:1, :], axis=AX),
                waits=[("tt", P.at("tt"))], inc="tv", hz=True)
            P.barrier("tensor", [("tv", P.at("tv"))])
        P.add("scalar", lambda e: e.activation(
            sb_tmp[0:1, 4:5], sb_tmp[0:1, 2:3], AF.Sqrt),
            waits=[("tv", P.at("tv")), ("ts", me_ts)], inc="ts")
        P.add("scalar", lambda e: e.activation(
            sb_tmp[0:1, 5:6], sb_tmp[0:1, 3:4], AF.Sqrt), inc="ts")
        P.add("vector", lambda e: e.tensor_mul(
            sb_tmp[0:1, 6:7], sb_tmp[0:1, 4:5], sb_tmp[0:1, 5:6]),
            waits=[("ts", P.at("ts"))], inc="tv")
        P.add("vector", lambda e: e.tensor_add(
            sb_tmp[0:1, 7:8], sb_tmp[0:1, 4:5], sb_tmp[0:1, 5:6]), inc="tv")
        P.add("vector", lambda e: e.tensor_mul(
            sb_tmp[0:1, 7:8], sb_tmp[0:1, 7:8], sb_tmp[0:1, 1:2]), inc="tv")
        P.add("vector", lambda e: e.tensor_add(
            sb_tmp[0:1, 6:7], sb_tmp[0:1, 6:7], sb_tmp[0:1, 7:8]), inc="tv")
        P.add("vector", lambda e: e.tensor_scalar(
            sb_nbh[0:1, 0:1], sb_tmp[0:1, 6:7], -1.02 / 8.0, None, ALU.mult),
            inc="tv")
        P.add("tensor", lambda e, h=h: e.matmul(
            ps_d[:, 256 + h:256 + h + 1], sb_ones[0:1, 0:128], sb_nbh[0:1, 0:1],
            start=True, stop=True), waits=[("tv", P.at("tv"))], inc="tt")
        P.add("vector", lambda e, h=h: e.tensor_copy(
            sb_negB[:, h:h + 1], ps_d[:, 256 + h:256 + h + 1]),
            waits=[("tt", P.at("tt"))], inc="tv")
        P.barrier("tensor", [("tv", P.at("tv"))])
    negb_tv = P.at("tv")
    if debug:
        tap("negB", lambda: sb_negB[:, 0:1])

    stg_order = [(0, t) for t in range(4)] + [(1, j) for j in range(NT)] \
        + [(0, t) for t in range(4, NT)]
    rot_par = [0]
    rot_ev = [("tv", P.at("tv")), ("tv", P.at("tv"))]
    stg_tile_idx = [0]
    stg_dma_hist = []
    stg_sd = {hh: {} for hh in range(H)}

    def emit_stage_tile(hs, kind, t):
        ch2, half2 = hs // 2, hs % 2
        qsl2 = slice(64 * half2, 64 * half2 + 64)
        tpos2 = (64 * half2, 0)
        src = sb_qT if kind == 0 else sb_kT
        tbl = sb_etr if kind == 0 else sb_etf
        dstD = d_qe if kind == 0 else d_ke
        voff = 1920 - 128 * t
        idx_t = stg_tile_idx[0]
        stg_tile_idx[0] += 1
        sl = idx_t % 2
        ev_last = {}
        for c in range(5):
            ncols = 512 if c < 4 else 128
            par = rot_par[0]
            rot_par[0] ^= 1
            P.add("tensor", lambda e, c=c, ncols=ncols, par=par, t=t,
                  src=src, tbl=tbl, ch2=ch2, qsl2=qsl2, tpos2=tpos2,
                  voff=voff: e.matmul(
                ps_big[:, 512 * par:512 * par + ncols],
                src[qsl2, S * ch2 + 128 * t:S * ch2 + 128 * t + 128],
                tbl[qsl2, voff + 512 * c:voff + 512 * c + ncols],
                start=True, stop=True, tile_position=tpos2),
                waits=[rot_ev[par]], inc="tt")
            tt_c = P.at("tt")
            eng, sem = ("vector", "tv") if c % 2 == 0 else ("scalar", "ts")
            we = [("tt", tt_c)]
            if c < 2 and idx_t >= 2:
                we.append(("sd", stg_dma_hist[idx_t - 2]))
            if eng == "vector":
                P.add(eng, lambda e, c=c, ncols=ncols, par=par, sl=sl:
                      e.tensor_copy(
                    sb_stg[:, TW * sl + 512 * c:TW * sl + 512 * c + ncols],
                    ps_big[:, 512 * par:512 * par + ncols]),
                    waits=we, inc=sem)
            else:
                P.add(eng, lambda e, c=c, ncols=ncols, par=par, sl=sl:
                      e.copy(
                    sb_stg[:, TW * sl + 512 * c:TW * sl + 512 * c + ncols],
                    ps_big[:, 512 * par:512 * par + ncols]),
                    waits=we, inc=sem)
            rot_ev[par] = (sem, P.at(sem))
            ev_last[sem] = P.at(sem)
        P.add("sync", lambda e, sl=sl, t=t, dstD=dstD, hs=hs: e.dma_start(
            out=dstD[hs, t], in_=sb_stg[:, TW * sl:TW * sl + TW]),
            waits=[(s, v) for s, v in ev_last.items()], inc="sd", amt=16)
        stg_sd[hs][(kind, t)] = P.at("sd")
        stg_dma_hist.append(P.at("sd"))

    def emit_edge_dots(hs):
        ch2, half2 = hs // 2, hs % 2
        qsl2 = slice(64 * half2, 64 * half2 + 64)
        tpos2 = (64 * half2, 0)
        for eid in range(2):
            for ns in range(4):
                par = rot_par[0]
                rot_par[0] ^= 1
                P.add("tensor", lambda e, ns=ns, ch2=ch2, qsl2=qsl2,
                      tpos2=tpos2, eid=eid, par=par: e.matmul(
                    ps_big[0:1, 512 * par:512 * par + 512],
                    sb_edg[qsl2, eid:eid + 1],
                    sb_qT[qsl2, S * ch2 + 512 * ns:S * ch2 + 512 * ns + 512],
                    start=True, stop=True, tile_position=tpos2),
                    waits=[rot_ev[par]], inc="tt")
                P.add("vector", lambda e, eid=eid, ns=ns, par=par:
                      e.tensor_copy(
                    sb_qe2[0:1, S * eid + 512 * ns:S * eid + 512 * ns + 512],
                    ps_big[0:1, 512 * par:512 * par + 512]),
                    waits=[("tt", P.at("tt"))], inc="tv")
                rot_ev[par] = ("tv", P.at("tv"))
        P.barrier("tensor", [("tv", P.at("tv"))])

    for kind, t in stg_order:
        emit_stage_tile(0, kind, t)

    for h in range(H):
        ch, half = h // 2, h % 2
        qsl = slice(64 * half, 64 * half + 64)
        tpos = (64 * half, 0)
        emit_edge_dots(h)

        exp_hist = {}
        stt_hist = {}
        slot_free = {}
        vb1_hist = {}
        big_bf5 = ps_big.bitcast(BF16)
        abf_slot = [big_bf5[:, 2048:2560],
                    big_bf5[:, 3072:3584]]

        def emit_a_gather(lc_t, i):
            half = lc_t % 2
            t = 4 * lc_t + i
            wA = [("sd", stg_sd[h][(0, t)])]
            if lc_t >= 2:
                wA.append(("tt", exp_hist[(lc_t - 2, NT - 1, "pv")]))
            else:
                wA.append(("tt", P.at("tt")))
            P.add("gpsimd", lambda e, t=t, i=i, h=h, half=half:
                  e.indirect_dma_start(
                sb_A[:, S * (4 * half + i):S * (4 * half + i) + S], None,
                d_qe_flat, IndirectOffsetOnAxis(sb_idxA[:], 1),
                element_offset=(h * NT + t) * 128 * TW),
                waits=wA, inc="gd", amt=16)

        for i in range(4):
            emit_a_gather(0, i)
        ga_hist = {0: P.at("gd")}
        for lc in range(NLC):
            half = lc % 2
            slot_free[(lc, 0)] = ("tv", P.at("tv"))
            slot_free[(lc, 1)] = ("tv", P.at("tv"))
            slot_free.pop(0, None)
            slot_free.pop(1, None)
            if h == 0 and lc == 0 and debug:
                tap("A", lambda: sb_A[:, 0:2048], wait_sem="gd")
            for j in range(NT):
                pj = [ps_a[:, :], ps_b[:, :]][j % 2]
                sl = j % 2
                wb = [("sd", stg_sd[h][(1, j)])]
                if (lc, j - 8) in vb1_hist:
                    wb.append(("tv", vb1_hist[(lc, j - 8)]))
                elif lc > 0 and (lc - 1, 8 + j) in vb1_hist:
                    wb.append(("tv", vb1_hist[(lc - 1, 8 + j)]))
                else:
                    wb.append(("tv", P.at("tv")))
                P.add("gpsimd", lambda e, j=j, lc=lc, h=h: e.indirect_dma_start(
                    sb_B[:, LC * (j % 8):LC * (j % 8) + LC], None,
                    d_ke_flat, IndirectOffsetOnAxis(sb_idxB[:], 1),
                    element_offset=(h * NT + j) * 128 * TW + LC * lc),
                    waits=wb, inc="gd", amt=16)
                gb_done = P.at("gd")
                if lc + 1 < NLC and j in (8, 10, 12, 14):
                    emit_a_gather(lc + 1, (j - 8) // 2)
                    if j == 14:
                        ga_hist[lc + 1] = P.at("gd")
                edg_runs = []
                tr_list = []
                for i in range(4):
                    t_l = 4 * lc + i
                    if t_l - j <= -9:
                        eid = 0
                    elif t_l - j >= 9:
                        eid = 1
                    else:
                        tr_list.append(i)
                        continue
                    if edg_runs and edg_runs[-1][0] == eid and edg_runs[-1][2] == i:
                        edg_runs[-1] = (eid, edg_runs[-1][1], i + 1)
                    else:
                        edg_runs.append((eid, i, i + 1))
                wt = []
                if (lc, j - 2) in stt_hist:
                    wt.append(("tv", stt_hist[(lc, j - 2)]))
                elif lc > 0 and (lc - 1, 14 + j) in stt_hist:
                    wt.append(("tv", stt_hist[(lc - 1, 14 + j)]))
                else:
                    wt.append(("tv", P.at("tv")))
                P.add("tensor", lambda e, j=j, lc=lc, ch=ch, qsl=qsl,
                      tpos=tpos, p=pj, st=(not edg_runs): e.matmul(
                    p,
                    sb_kT[qsl, S * ch + 128 * j:S * ch + 128 * j + 128],
                    sb_qT[qsl, S * ch + LC * lc:S * ch + LC * lc + LC],
                    start=True, stop=st, tile_position=tpos),
                    waits=wt, inc="tt")
                for ei, (eid, i0, i1) in enumerate(edg_runs):
                    P.add("tensor", lambda e, p=pj, eid=eid, i0=i0, i1=i1,
                          lc=lc, st=(ei == len(edg_runs) - 1): e.matmul(
                        p[:, 128 * i0:128 * i1],
                        sb_ones[0:1, 0:128],
                        sb_qe2[0:1, S * eid + LC * lc + 128 * i0:
                               S * eid + LC * lc + 128 * i1],
                        start=False, stop=st), inc="tt")
                tt_scores = P.at("tt")
                wtr = [("gd", ga_hist[lc])]
                sf = slot_free.get((lc, sl)) if j < 2 else slot_free.get(sl)
                if sf is None:
                    sf = ("tv", P.at("tv"))
                wtr.append(sf)
                first_tr = True
                for i in tr_list:
                    P.add("tensor", lambda e, i=i, j=j, half=half, sl=sl:
                          e.transpose(
                        abf_slot[sl][:, 128 * i:128 * i + 128],
                        sb_A[:, S * (4 * half + i) + 128 * j:
                             S * (4 * half + i) + 128 * j + 128],
                        sb_iden[:, :]),
                        waits=(wtr if first_tr else []), inc="tt")
                    first_tr = False
                tt_atr = P.at("tt")
                wexp_extra = []
                if (lc, j - 2, "exp") in exp_hist:
                    wexp_extra = [("ts", exp_hist[(lc, j - 2, "exp")])]
                elif lc > 0:
                    wexp_extra = [("ts", exp_hist[(lc - 1, 14 + j, "exp")])]
                if tr_list:
                    P.add("vector", lambda e, j=j, sl=sl: e.tensor_tensor(
                        sb_Ss[:, LC * sl:LC * sl + LC], abf_slot[sl][:, 0:LC],
                        sb_B[:, LC * (j % 8):LC * (j % 8) + LC], ALU.add),
                        waits=[("tt", tt_atr), ("gd", gb_done)] + wexp_extra,
                        inc="tv")
                    vb1_hist[(lc, j)] = P.at("tv")
                    slot_free[sl] = ("tv", P.at("tv"))
                    P.add("vector", lambda e, p=pj, sl=sl: e.scalar_tensor_tensor(
                        sb_Ss[:, LC * sl:LC * sl + LC], p, 1.0,
                        sb_Ss[:, LC * sl:LC * sl + LC], ALU.mult, ALU.add),
                        waits=[("tt", tt_scores)], inc="tv")
                else:
                    P.add("vector", lambda e, p=pj, j=j, sl=sl:
                          e.scalar_tensor_tensor(
                        sb_Ss[:, LC * sl:LC * sl + LC], p, 1.0,
                        sb_B[:, LC * (j % 8):LC * (j % 8) + LC],
                        ALU.mult, ALU.add),
                        waits=[("tt", tt_scores), ("gd", gb_done)] + wexp_extra,
                        inc="tv")
                    vb1_hist[(lc, j)] = P.at("tv")
                stt_hist[(lc, j)] = P.at("tv")
                if debug and h == 0 and lc == 0 and j == 0:
                    tap("Bt", lambda: sb_B[:, 0:LC], wait_sem="gd")
                    tap("scr", lambda: sb_Ss[:, 0:LC], wait_sem="tv")
                P.add("scalar", lambda e, sl=sl, x4=j % 4, h=h: e.activation(
                    sb_X[:, LC * x4:LC * x4 + LC],
                    sb_Ss[:, LC * sl:LC * sl + LC], AF.Exp,
                    bias=sb_negB[:, h:h + 1], scale=0.125),
                    waits=[("tv", stt_hist[(lc, j)]), ("tv", negb_tv)], inc="ts")
                exp_hist[(lc, j, "exp")] = P.at("ts")
                if debug and h == 0 and lc == 0 and j == 0:
                    tap("X", lambda: sb_X[:, 0:LC], wait_sem="ts")

                def emit_pv(jp):
                    wpv = [("ts", exp_hist[(lc, jp, "exp")])]
                    if jp == 0:
                        wpv.append(("tv", P.at("tv")))
                    P.add("tensor", lambda e, jp=jp, h=h: e.matmul(
                        ps_c[0:65, :],
                        sb_v65[:, 1040 * jp + 65 * h:1040 * jp + 65 * h + 65],
                        sb_X[:, LC * (jp % 4):LC * (jp % 4) + LC],
                        start=(jp == 0), stop=(jp == NT - 1)),
                        waits=wpv, inc="tt")
                    exp_hist[(lc, jp, "pv")] = P.at("tt")
                if j >= 2:
                    emit_pv(j - 2)
                gi = lc * NT + j
                if h + 1 < H and gi == 63:
                    for kind2, t2 in stg_order:
                        emit_stage_tile(h + 1, kind2, t2)
                if j == NT - 1:
                    emit_pv(NT - 2)
                    emit_pv(NT - 1)
            P.add("vector", lambda e: e.tensor_copy(
                sb_ctv[0:65, :], ps_c[0:65, :]),
                waits=[("tt", P.at("tt")), ("sd", P.at("sd"))], inc="tv")
            for i in range(4):
                wtr2 = [("tv", P.at("tv"))] if i == 0 else []
                P.add("tensor", lambda e, i=i: e.transpose(
                    ps_d[:, 65 * i:65 * i + 65],
                    sb_ctv[0:65, 128 * i:128 * i + 128],
                    sb_idenf[0:65, 0:65]), waits=wtr2, inc="tt")
            P.add("vector", lambda e: e.reciprocal(
                sb_zr[:, 0:4], bass.AP(ps_d, 64, [[512, 128], [65, 4]])),
                waits=[("tt", P.at("tt"))], inc="tv", hz=True)
            P.add("vector", lambda e: e.tensor_tensor(
                bass.AP(sb_ctp, 0, [[256, 128], [64, 4], [1, 64]]),
                bass.AP(ps_d, 0, [[512, 128], [65, 4], [1, 64]]),
                bass.AP(sb_zr, 0, [[4, 128], [1, 4], [0, 64]]),
                ALU.mult), inc="tv")
            def ctx_store(e, h=h, lc=lc):
                dst = bass.AP(d_ctx.tensor, h * S * D + LC * lc * D,
                              [[D, 128], [128 * D, 4], [1, D]])
                src = bass.AP(sb_ctp, 0, [[4 * D, 128], [D, 4], [1, D]])
                return e.dma_start(out=dst, in_=src)
            P.add("sync", ctx_store, waits=[("tv", P.at("tv"))], inc="sd", amt=16)
    attn_sd = P.at("sd")

    M.free("sb_X", "sb_B", "sb_Ss", "sb_A", "sb_stg", "sb_etr",
           "sb_etf", "sb_v65", "sb_kT", "sb_qT")
    sb_xpa = M.alloc("sb_xpa", [128, NT * E], F32)
    sb_xd = M.alloc("sb_xd", [128, NT * E], BF16)
    sb_wd = M.alloc("sb_wd", [128, NCH * E], BF16)
    sb_ctl = M.alloc("sb_ctl", [128, E], F32)
    sb_ffw = M.alloc("sb_ffw", [128, E], F32)
    sb_q1 = M.alloc("sb_q1", [128, E], F32)
    sb_q2 = M.alloc("sb_q2", [128, E], F32)
    sb_q3 = M.alloc("sb_q3", [128, E], F32)
    sb_sc = M.alloc("sb_sc", [128, 32], F32)
    sb_scr = M.alloc("sb_scr", [128, 32], F32)

    def emit_quant(dst_ap_fn, src_sb):
        def a3(sb, off=0):
            return bass.AP(sb, off, [[E, 128], [32, 32], [1, 32]])

        def scbc(sb):
            return bass.AP(sb, 0, [[32, 128], [1, 32], [0, 32]])
        P.add("vector", lambda e: e.tensor_reduce(
            sb_sc[:], a3(src_sb), axis=AX, op=ALU.max,
            apply_absolute_value=True), inc="tv", hz=True)
        P.add("vector", lambda e: e.tensor_scalar(
            sb_sc[:], sb_sc[:], 1.0 / 7.5, 1e-30, ALU.mult, ALU.max), inc="tv")
        P.add("vector", lambda e: e.reciprocal(sb_scr[:], sb_sc[:]), inc="tv", hz=True)
        P.add("vector", lambda e: e.tensor_tensor(
            a3(sb_q1), a3(src_sb), scbc(sb_scr), ALU.mult), inc="tv")
        P.add("vector", lambda e: e.tensor_mul(sb_q2[:], sb_q1[:], sb_q1[:]), inc="tv")
        P.add("vector", lambda e: e.tensor_scalar(
            sb_q3[:], sb_q2[:], 4.0, None, ALU.is_ge), inc="tv")
        P.add("vector", lambda e: e.tensor_scalar(
            sb_q2[:], sb_q2[:], 16.0, None, ALU.is_ge), inc="tv")
        P.add("vector", lambda e: e.tensor_scalar(
            sb_dump[:], sb_q3[:], 0.125, 0.125, ALU.mult, ALU.add), inc="tv")
        P.add("vector", lambda e: e.scalar_tensor_tensor(
            sb_dump[:], sb_q2[:], 0.25, sb_dump[:], ALU.mult, ALU.add), inc="tv")
        P.add("vector", lambda e: e.tensor_scalar(
            sb_q3[:], sb_q3[:], -4.0, 8.0, ALU.mult, ALU.add), inc="tv")
        P.add("vector", lambda e: e.scalar_tensor_tensor(
            sb_q3[:], sb_q2[:], -2.0, sb_q3[:], ALU.mult, ALU.add), inc="tv")
        P.add("vector", lambda e: e.tensor_mul(sb_q1[:], sb_q1[:], sb_q3[:]), inc="tv")
        P.add("vector", lambda e: e.tensor_copy(
            sb_q2.bitcast(I32)[:], sb_q1[:]), inc="tv")
        P.add("vector", lambda e: e.tensor_copy(
            sb_q1[:], sb_q2.bitcast(I32)[:]), inc="tv")
        P.add("vector", lambda e: e.tensor_mul(sb_q1[:], sb_q1[:], sb_dump[:]), inc="tv")
        P.add("vector", lambda e: e.tensor_tensor(
            dst_ap_fn(), a3(sb_q1), scbc(sb_sc), ALU.mult), inc="tv")

    for t in range(NT):
        buf = sb_xt if t % 2 == 0 else sb_xt2
        w = [("tv", P.at("tv"))] if t >= 2 else []
        P.add("sync", lambda e, t=t, b=buf: e.dma_start(
            out=b[:], in_=d_x[128 * t:128 * t + 128, :]), waits=w, inc="sd", amt=16)

        def ctx_ld(e, t=t):
            src = bass.AP(d_ctx.tensor, 128 * t * D, [[D, 128], [S * D, H], [1, D]])
            dst = bass.AP(sb_ctl, 0, [[E, 128], [D, H], [1, D]])
            return e.dma_start(out=dst, in_=src)
        P.add("sync", ctx_ld, waits=[("sd", attn_sd)], inc="sd", amt=16)
        P.add("vector", lambda e, b=buf: e.tensor_add(b[:], b[:], sb_ctl[:]),
              waits=[("sd", P.at("sd"))], inc="tv")
        P.add("vector", lambda e, t=t, b=buf: e.tensor_copy(
            sb_xpa[:, E * t:E * t + E], b[:]), inc="tv")
        emit_ln(lambda: sb_q3[:], buf)
        P.add("vector", lambda e: e.tensor_copy(sb_ffw[:], sb_q3[:]), inc="tv")
        emit_quant(lambda t=t: bass.AP(sb_xd, E * t, [[NT * E, 128], [32, 32], [1, 32]]),
                   sb_ffw)
        if t == 0:
            tap("xpa", lambda: sb_xpa[:, 0:E])
            tap("xd", lambda: sb_xd[:, 0:E])

    for fc in range(NCH):
        w = [("tv", P.at("tv"))] if fc >= 1 else [("tv", P.at("tv"))]
        P.add("sync", lambda e, fc=fc: e.dma_start(
            out=sb_ffw[:], in_=d_ffw[128 * fc:128 * fc + 128, :]),
            waits=w, inc="sd", amt=16)
        P.barrier("vector", [("sd", P.at("sd"))])
        emit_quant(lambda fc=fc: bass.AP(sb_wd, E * fc, [[NCH * E, 128], [32, 32], [1, 32]]),
                   sb_ffw)
        if fc == 0:
            tap("wd", lambda: sb_wd[:, 0:E])
    quant_tv = P.at("tv")

    M.free("sb_scr", "sb_sc", "sb_q3", "sb_q2", "sb_q1", "sb_ffw", "sb_ctl")
    sb_xdT = M.alloc("sb_xdT", [128, NCH * S], BF16)
    sb_wdT = M.alloc("sb_wdT", [128, NCH * E], BF16)
    sb_yt = M.alloc("sb_yt", [128, E], F32)

    tp_i = 0
    P.barrier("tensor", [("tv", quant_tv)])
    for t in range(NT):
        for kc in range(NCH):
            pb = psA_bf if tp_i % 2 == 0 else psB_bf
            w = [("tv", P.at("tv"))] if tp_i >= 2 else []
            P.add("tensor", lambda e, t=t, kc=kc, p=pb: e.transpose(
                p[:, 0:128], sb_xd[:, E * t + 128 * kc:E * t + 128 * kc + 128],
                sb_iden[:]), waits=w, inc="tt")
            P.add("vector", lambda e, t=t, kc=kc, p=pb: e.tensor_copy(
                sb_xdT[:, S * kc + 128 * t:S * kc + 128 * t + 128], p[:, 0:128]),
                waits=[("tt", P.at("tt"))], inc="tv")
            tp_i += 1
    for fc in range(NCH):
        for kc in range(NCH):
            pb = psA_bf if tp_i % 2 == 0 else psB_bf
            P.add("tensor", lambda e, fc=fc, kc=kc, p=pb: e.transpose(
                p[:, 0:128], sb_wd[:, E * fc + 128 * kc:E * fc + 128 * kc + 128],
                sb_iden[:]), waits=[("tv", P.at("tv"))], inc="tt")
            P.add("vector", lambda e, fc=fc, kc=kc, p=pb: e.tensor_copy(
                sb_wdT[:, E * kc + 128 * fc:E * kc + 128 * fc + 128], p[:, 0:128]),
                waits=[("tt", P.at("tt"))], inc="tv")
            tp_i += 1
    trans_tv = P.at("tv")

    P.barrier("tensor", [("tv", trans_tv)])
    y_sd = {}
    for t in range(NT):
        if t >= 1:
            P.barrier("tensor", [("tv", P.at("tv"))])
        for kc in range(NCH):
            for ns in range(2):
                P.add("tensor", lambda e, t=t, kc=kc, ns=ns: e.matmul(
                    ps_big[:, 512 * ns:512 * ns + 512],
                    sb_xdT[:, S * kc + 128 * t:S * kc + 128 * t + 128],
                    sb_wdT[:, E * kc + 512 * ns:E * kc + 512 * ns + 512],
                    start=(kc == 0), stop=False), inc="tt")
        for ns in range(2):
            P.add("tensor", lambda e, ns=ns: e.matmul(
                ps_big[:, 512 * ns:512 * ns + 512],
                sb_ones[0:1, 0:128], sb_ffb[0:1, 512 * ns:512 * ns + 512],
                start=False, stop=True), inc="tt")
        wv = [("tt", P.at("tt"))]
        if t >= 1:
            wv.append(("sd", y_sd[t - 1]))
        P.add("vector", lambda e, t=t: e.tensor_add(
            sb_yt[:, 0:E], ps_big[:, 0:1024],
            sb_xpa[:, E * t:E * t + E]), waits=wv, inc="tv")
        P.add("sync", lambda e, t=t: e.dma_start(
            out=d_y[128 * t:128 * t + 128, :], in_=sb_yt[:, 0:E]),
            waits=[("tv", P.at("tv"))], inc="sd", amt=16)
        y_sd[t] = P.at("sd")

    P.add("sync", lambda e: None, waits=[("sd", P.at("sd"))])

    def replay(name, eng):
        for fn, waits, inc, amt in P.items[name]:
            for s_, v in waits:
                eng.wait_ge(sem_h[s_], v)
            inst = fn(eng)
            if inc is not None:
                inst.then_inc(sem_h[inc], amt)

    @block.sync
    def _(sync):
        replay("sync", sync)

    @block.vector
    def _(vector):
        replay("vector", vector)

    @block.scalar
    def _(scalar):
        replay("scalar", scalar)

    @block.tensor
    def _(tensor):
        replay("tensor", tensor)

    @block.gpsimd
    def _(gpsimd):
        replay("gpsimd", gpsimd)

    block_cm.__exit__(None, None, None)
    M.free_all()
    for cm in reversed(psum_cms):
        cm.__exit__(None, None, None)
    for cm in reversed(sem_cms):
        cm.__exit__(None, None, None)
    return nc


_NC_CACHE = [None]


def _host_prep(inputs):
    x = np.ascontiguousarray(np.asarray(inputs["L0_attn_hidden_states"], dtype=np.float32))
    Et = np.asarray(inputs["L0_attn_dist_emb"], dtype=np.float32)
    E_ext = np.concatenate([
        np.broadcast_to(Et[0:1], (1024, D)), Et,
        np.broadcast_to(Et[-1:], (1025, D))], axis=0)
    p = np.arange(128, dtype=np.int32).reshape(128, 1)
    base = {
        "wqT": np.ascontiguousarray(np.asarray(inputs["L0_attn_wq"], np.float32).T),
        "wkT": np.ascontiguousarray(np.asarray(inputs["L0_attn_wk"], np.float32).T),
        "wvT": np.ascontiguousarray(np.asarray(inputs["L0_attn_wv"], np.float32).T),
        "bq": np.ascontiguousarray(np.asarray(inputs["L0_attn_bq"], np.float32).reshape(1, E)),
        "bk": np.ascontiguousarray(np.asarray(inputs["L0_attn_bk"], np.float32).reshape(1, E)),
        "bv": np.ascontiguousarray(np.asarray(inputs["L0_attn_bv"], np.float32).reshape(1, E)),
        "etf": np.ascontiguousarray(E_ext.T),
        "etr": np.ascontiguousarray(E_ext[::-1].T),
        "ffw": np.ascontiguousarray(np.asarray(inputs["L0_ffn_weight"], np.float32)),
        "ffb": np.ascontiguousarray(np.asarray(inputs["L0_ffn_bias"], np.float32).reshape(1, E)),
        "idxA": (TW - 1) * p + 128,
        "idxB": (TW - 1) * p + 127,
        "idn": np.ascontiguousarray(np.eye(128, dtype=np.float32)),
    }
    return x, base


def debug_run(inputs, core=0):
    x, base = _host_prep(inputs)
    nc = build_program(debug=True)
    im = dict(base)
    im["x"] = np.ascontiguousarray(x[core])
    res = run_bass_kernel_spmd(nc, [im], core_ids=[0])
    return res.results[0]


def kernel(L0_attn_hidden_states, L0_attn_wq, L0_attn_bq, L0_attn_wk,
           L0_attn_bk, L0_attn_wv, L0_attn_bv, L0_attn_dist_emb,
           L0_ffn_weight, L0_ffn_bias):
    x = np.ascontiguousarray(np.asarray(L0_attn_hidden_states, dtype=np.float32))
    wq = np.asarray(L0_attn_wq, dtype=np.float32)
    wk = np.asarray(L0_attn_wk, dtype=np.float32)
    wv = np.asarray(L0_attn_wv, dtype=np.float32)
    bq = np.ascontiguousarray(np.asarray(L0_attn_bq, dtype=np.float32).reshape(1, E))
    bk = np.ascontiguousarray(np.asarray(L0_attn_bk, dtype=np.float32).reshape(1, E))
    bv = np.ascontiguousarray(np.asarray(L0_attn_bv, dtype=np.float32).reshape(1, E))
    Et = np.asarray(L0_attn_dist_emb, dtype=np.float32)
    ffw = np.ascontiguousarray(np.asarray(L0_ffn_weight, dtype=np.float32))
    ffb = np.ascontiguousarray(np.asarray(L0_ffn_bias, dtype=np.float32).reshape(1, E))

    E_ext = np.concatenate([
        np.broadcast_to(Et[0:1], (1024, D)),
        Et,
        np.broadcast_to(Et[-1:], (1025, D)),
    ], axis=0)
    etf = np.ascontiguousarray(E_ext.T)
    etr = np.ascontiguousarray(E_ext[::-1].T)

    wqT = np.ascontiguousarray(wq.T)
    wkT = np.ascontiguousarray(wk.T)
    wvT = np.ascontiguousarray(wv.T)

    p = np.arange(128, dtype=np.int32).reshape(128, 1)
    idxA = (TW - 1) * p + 128
    idxB = (TW - 1) * p + 127
    iden = np.ascontiguousarray(np.eye(128, dtype=np.float32))

    if _NC_CACHE[0] is None:
        _NC_CACHE[0] = build_program()
    nc = _NC_CACHE[0]

    in_maps = []
    for c in range(8):
        in_maps.append({
            "x": np.ascontiguousarray(x[c]),
            "wqT": wqT, "wkT": wkT, "wvT": wvT,
            "bq": bq, "bk": bk, "bv": bv,
            "etf": etf, "etr": etr,
            "ffw": ffw, "ffb": ffb,
            "idxA": idxA, "idxB": idxB, "idn": iden,
        })
    res = run_bass_kernel_spmd(nc, in_maps, core_ids=list(range(8)))
    return np.stack([r["y"] for r in res.results], axis=0)


# version 19
# speedup vs baseline: 1.0002x; 1.0002x over previous
import numpy as np
import concourse.bass as bass
import concourse.mybir as mybir
from concourse.bass_utils import run_bass_kernel_spmd
from concourse.bass_core_rust import IndirectOffsetOnAxis

F32 = mybir.dt.float32
BF16 = mybir.dt.bfloat16
I32 = mybir.dt.int32
AF = mybir.ActivationFunctionType
ALU = mybir.AluOpType
AX = mybir.AxisListType.X

S = 2048
E = 1024
H = 16
D = 64
NT = S // 128
NCH = E // 128
TW = 2176
ETW = 4096
LC = 512
NLC = S // NLC if False else S // LC


class Prog:
    ENGS = ("sync", "vector", "scalar", "tensor", "gpsimd")

    def __init__(self):
        self.items = {e: [] for e in self.ENGS}
        self.counts = {}
        self.pending = {e: [] for e in self.ENGS}
        self.hazard = {}

    def sem(self, name):
        self.counts[name] = 0

    FLUSH = {"vector": "tv", "scalar": "ts"}

    def add(self, eng, fn, waits=(), inc=None, amt=1, hz=False):
        w = self.pending[eng] + list(waits)
        self.pending[eng] = []
        fs = self.FLUSH.get(eng)
        if fs is not None and (hz or self.hazard.get(eng, False)) \
                and self.counts.get(fs, 0) > 0:
            w.append((fs, self.counts[fs]))
        self.hazard[eng] = hz
        self.items[eng].append((fn, w, inc, amt))
        if inc is not None:
            self.counts[inc] += amt
        return self.counts.get(inc)

    def barrier(self, eng, waits):
        self.pending[eng].extend(waits)

    def at(self, sem):
        return self.counts[sem]


class Mem:

    def __init__(self, nc):
        self.nc = nc
        self.live = {}

    def alloc(self, name, shape, dt, side=None):
        if side is None:
            cm = self.nc.sbuf_tensor(name, shape, dt)
        else:
            cm = self.nc.sbuf_tensor(name, shape, dt, side=side)
        h = cm.__enter__()
        self.live[name] = (cm, h)
        return h

    def free(self, *names):
        for n in names:
            cm, _ = self.live.pop(n)
            cm.__exit__(None, None, None)

    def free_all(self):
        for n in reversed(list(self.live)):
            self.free(n)


DEBUG = False


def build_program(debug=False):
    nc = bass.Bass("TRN2", target_bir_lowering=False, debug=False)

    d_x = nc.dram_tensor("x", [S, E], F32, kind="ExternalInput").ap()
    d_wqT = nc.dram_tensor("wqT", [E, E], F32, kind="ExternalInput").ap()
    d_wkT = nc.dram_tensor("wkT", [E, E], F32, kind="ExternalInput").ap()
    d_wvT = nc.dram_tensor("wvT", [E, E], F32, kind="ExternalInput").ap()
    d_bq = nc.dram_tensor("bq", [1, E], F32, kind="ExternalInput").ap()
    d_bk = nc.dram_tensor("bk", [1, E], F32, kind="ExternalInput").ap()
    d_bv = nc.dram_tensor("bv", [1, E], F32, kind="ExternalInput").ap()
    d_etf = nc.dram_tensor("etf", [D, ETW], F32, kind="ExternalInput").ap()
    d_etr = nc.dram_tensor("etr", [D, ETW], F32, kind="ExternalInput").ap()
    d_ffw = nc.dram_tensor("ffw", [E, E], F32, kind="ExternalInput").ap()
    d_ffb = nc.dram_tensor("ffb", [1, E], F32, kind="ExternalInput").ap()
    d_idxA = nc.dram_tensor("idxA", [128, 1], I32, kind="ExternalInput").ap()
    d_idxB = nc.dram_tensor("idxB", [128, 1], I32, kind="ExternalInput").ap()
    d_iden = nc.dram_tensor("idn", [128, 128], F32, kind="ExternalInput").ap()
    d_y = nc.dram_tensor("y", [S, E], F32, kind="ExternalOutput").ap()

    ikind = "ExternalOutput" if debug else "Internal"
    d_qe = nc.dram_tensor("qe_st", [H, NT, 128, TW], BF16, kind=ikind).ap()
    d_ke = nc.dram_tensor("ke_st", [H, NT, 128, TW], BF16, kind=ikind).ap()
    d_qe_flat = bass.AP(d_qe.tensor, 0, [[TW, H * NT * 128], [1, TW]])
    d_ke_flat = bass.AP(d_ke.tensor, 0, [[TW, H * NT * 128], [1, TW]])
    d_ctx = nc.dram_tensor("ctx_st", [H, S, D], F32, kind=ikind).ap()
    d_dbg = {}
    if debug:
        for nm, dt_ in [("xln", BF16), ("xlnT", BF16), ("qT", BF16),
                        ("kT", BF16), ("v65", BF16), ("negB", F32),
                        ("A", F32), ("Bt", BF16), ("scr", F32), ("X", BF16),
                        ("xpa", F32), ("xd", BF16), ("wd", BF16),
                        ("wq0", BF16), ("brow", BF16), ("qTe", BF16)]:
            d_dbg[nm] = nc.dram_tensor("dbg_" + nm, [128, 2048], dt_,
                                       kind="ExternalOutput").ap()

    M = Mem(nc)
    sb_xt = M.alloc("sb_xt", [128, E], F32)
    sb_xt2 = M.alloc("sb_xt2", [128, E], F32)
    sb_brow = M.alloc("sb_brow", [1, 3 * E], BF16)
    sb_ffb = M.alloc("sb_ffb", [1, E], BF16)
    sb_iden = M.alloc("sb_iden", [128, 128], BF16)
    sb_idenf = M.alloc("sb_idenf", [128, 128], F32)
    sb_ones = M.alloc("sb_ones", [128, 512], BF16)
    sb_idxA = M.alloc("sb_idxA", [128, 1], I32)
    sb_idxB = M.alloc("sb_idxB", [128, 1], I32)
    sb_tmp = M.alloc("sb_tmp", [1, 8], F32)
    sb_mv = M.alloc("sb_mv", [128, 4], F32)
    sb_zr = M.alloc("sb_zr", [128, 4], F32)
    sb_negB = M.alloc("sb_negB", [128, H], F32)
    sb_nbh = M.alloc("sb_nbh", [1, 2], BF16)
    sb_dump = M.alloc("sb_dump", [128, E], F32)
    sb_ctv = M.alloc("sb_ctv", [128, 512], F32)
    sb_ctp = M.alloc("sb_ctp", [128, 4 * D], F32)
    sb_edg = M.alloc("sb_edg", [128, 2], BF16)
    sb_qe2 = M.alloc("sb_qe2", [1, 2 * S], BF16)

    psum_cms = []

    def psum(name, shape, dt):
        cm = nc.psum_tensor(name, shape, dt)
        h = cm.__enter__()
        psum_cms.append(cm)
        return h

    ps_a = psum("ps_a", [128, 512], F32)
    ps_b = psum("ps_b", [128, 512], F32)
    ps_c = psum("ps_c", [128, 512], F32)
    ps_d = psum("ps_d", [128, 512], F32)
    ps_big = psum("ps_big", [128, 2048], F32)
    psA_bf = ps_a.bitcast(BF16)
    psB_bf = ps_b.bitcast(BF16)

    sem_cms = []

    def mksem(name):
        cm = nc.semaphore(name)
        h = cm.__enter__()
        sem_cms.append(cm)
        return h

    sems = ["sd", "gd", "tv", "ts", "tt"]
    sem_h = {s: mksem(s) for s in sems}

    block_cm = nc.Block()
    block = block_cm.__enter__()

    P = Prog()
    for s in sems:
        P.sem(s)

    def tap(nm, src_fn, wait_sem="tv"):
        if not debug:
            return
        P.add("sync", lambda e: e.dma_start(out=d_dbg[nm][0:src_fn().shape[0],
                                                          0:src_fn().shape[1]],
                                            in_=src_fn(),
                                            allow_slow_non_contiguous=True),
              waits=[(wait_sem, P.at(wait_sem))], inc="sd", amt=16)

    sb_xlnT = M.alloc("sb_xlnT", [128, NCH * S], BF16, side="right")
    sb_wv = M.alloc("sb_wv", [128, NCH * E], BF16, side="right")
    sb_wk = M.alloc("sb_wk", [128, NCH * E], BF16, side="right")
    sb_wq = M.alloc("sb_wq", [128, NCH * E], BF16, side="right")
    sb_xln = M.alloc("sb_xln", [128, NT * E], BF16, side="right")

    P.add("sync", lambda e: e.dma_start(out=sb_idxA[:], in_=d_idxA), inc="sd", amt=16)
    P.add("sync", lambda e: e.dma_start(out=sb_idxB[:], in_=d_idxB), inc="sd", amt=16)
    P.add("sync", lambda e: e.dma_start(out=sb_idenf[:], in_=d_iden), inc="sd", amt=16)
    P.add("gpsimd", lambda e: e.dma_start(out=sb_iden[:], in_=d_iden), inc="gd", amt=16)
    for ch in range(NCH):
        for wsb, wd_ in ((sb_wq, d_wqT), (sb_wk, d_wkT), (sb_wv, d_wvT)):
            P.add("gpsimd", lambda e, c=ch, wsb=wsb, wd_=wd_: e.dma_start(
                out=wsb[:, E * c:E * c + E], in_=wd_[128 * c:128 * c + 128, :]),
                inc="gd", amt=16)
    P.add("gpsimd", lambda e: e.dma_start(out=sb_brow[:, 0:E], in_=d_bq), inc="gd", amt=16)
    P.add("gpsimd", lambda e: e.dma_start(out=sb_brow[:, E:2 * E], in_=d_bk), inc="gd", amt=16)
    P.add("gpsimd", lambda e: e.dma_start(out=sb_brow[:, 2 * E:3 * E], in_=d_bv), inc="gd", amt=16)
    P.add("gpsimd", lambda e: e.dma_start(out=sb_ffb[:], in_=d_ffb), inc="gd", amt=16)
    gd_loads = P.at("gd")

    P.add("vector", lambda e: e.memset(sb_ones[:], 1.0), inc="tv")

    def emit_ln(dst_ap_fn, src_sb, first_waits=()):
        P.add("vector", lambda e: e.reduce_sum(
            sb_mv[:, 0:1], src_sb[:], axis=AX), waits=first_waits, inc="tv", hz=True)
        P.add("vector", lambda e: e.tensor_scalar(
            sb_mv[:, 0:1], sb_mv[:, 0:1], 1.0 / E, None, ALU.mult), inc="tv")
        P.add("scalar", lambda e: e.activation(
            sb_dump[:], src_sb[:], AF.Square, accum_out=sb_mv[:, 1:2]),
            waits=[("tv", P.at("tv"))], inc="ts")
        P.add("vector", lambda e: e.tensor_mul(
            sb_mv[:, 2:3], sb_mv[:, 0:1], sb_mv[:, 0:1]),
            waits=[("ts", P.at("ts"))], inc="tv")
        P.add("vector", lambda e: e.tensor_scalar(
            sb_mv[:, 1:2], sb_mv[:, 1:2], 1.0 / E, None, ALU.mult), inc="tv")
        P.add("vector", lambda e: e.tensor_sub(
            sb_mv[:, 1:2], sb_mv[:, 1:2], sb_mv[:, 2:3]), inc="tv")
        P.add("vector", lambda e: e.tensor_scalar(
            sb_mv[:, 1:2], sb_mv[:, 1:2], 1e-5, None, ALU.add), inc="tv")
        P.add("scalar", lambda e: e.activation(
            sb_mv[:, 3:4], sb_mv[:, 1:2], AF.Sqrt),
            waits=[("tv", P.at("tv"))], inc="ts")
        P.add("vector", lambda e: e.reciprocal(
            sb_mv[:, 2:3], sb_mv[:, 3:4]), waits=[("ts", P.at("ts"))], inc="tv", hz=True)
        P.add("vector", lambda e: e.tensor_scalar(
            dst_ap_fn(), src_sb[:], sb_mv[:, 0:1], sb_mv[:, 2:3],
            ALU.subtract, ALU.mult), inc="tv")

    for t in range(NT):
        buf = sb_xt if t % 2 == 0 else sb_xt2
        w = [("tv", P.at("tv"))] if t >= 2 else []
        P.add("sync", lambda e, t=t, b=buf: e.dma_start(
            out=b[:], in_=d_x[128 * t:128 * t + 128, :]), waits=w, inc="sd", amt=16)
        emit_ln(lambda t=t: sb_xln[:, E * t:E * t + E], buf,
                first_waits=[("sd", P.at("sd"))])
        if t == 0:
            tap("xln", lambda: sb_xln[:, 0:E])
    ln1_tv = P.at("tv")

    tp_i = 0
    for t in range(NT):
        for ch in range(NCH):
            pb = psA_bf if tp_i % 2 == 0 else psB_bf
            w = [("tv", P.at("tv"))] if tp_i >= 2 else [("tv", ln1_tv)]
            P.add("tensor", lambda e, t=t, c=ch, p=pb: e.transpose(
                p[:, 0:128], sb_xln[:, E * t + 128 * c:E * t + 128 * c + 128],
                sb_iden[:]), waits=w, inc="tt")
            P.add("vector", lambda e, t=t, c=ch, p=pb: e.tensor_copy(
                sb_xlnT[:, S * c + 128 * t:S * c + 128 * t + 128], p[:, 0:128]),
                waits=[("tt", P.at("tt")), ("gd", gd_loads)], inc="tv")
            tp_i += 1
    xlnT_tv = P.at("tv")
    tap("xlnT", lambda: sb_xlnT[:, 0:2048])
    M.free("sb_xln")
    sb_qT = M.alloc("sb_qT", [128, NCH * S], BF16)
    sb_kT = M.alloc("sb_kT", [128, NCH * S], BF16)

    P.barrier("tensor", [("tv", xlnT_tv)])
    for ch in range(NCH):
        for wsb, qsb, boff in ((sb_wq, sb_qT, 0), (sb_wk, sb_kT, E)):
            for ec in range(NCH):
                for ns in range(4):
                    P.add("tensor", lambda e, ec=ec, ns=ns, ch=ch, wsb=wsb: e.matmul(
                        ps_big[:, 512 * ns:512 * ns + 512],
                        wsb[:, E * ec + 128 * ch:E * ec + 128 * ch + 128],
                        sb_xlnT[:, S * ec + 512 * ns:S * ec + 512 * ns + 512],
                        start=(ec == 0), stop=False), inc="tt")
            for ns in range(4):
                P.add("tensor", lambda e, ns=ns, ch=ch, boff=boff: e.matmul(
                    ps_big[:, 512 * ns:512 * ns + 512],
                    sb_brow[0:1, boff + 128 * ch:boff + 128 * ch + 128],
                    sb_ones[0:1, 0:512], start=False, stop=True), inc="tt")
            P.add("vector", lambda e, ch=ch, qsb=qsb: e.tensor_copy(
                qsb[:, S * ch:S * ch + S], ps_big[:, :]),
                waits=[("tt", P.at("tt"))], inc="tv")
            P.barrier("tensor", [("tv", P.at("tv"))])
            if ch == 0 and qsb is sb_qT:
                tap("qTe", lambda: sb_qT[:, 0:2048])
                tap("wq0", lambda: sb_wq[:, 0:2048])
                tap("brow", lambda: sb_brow[0:1, 0:2048])
    M.free("sb_wq", "sb_wk")
    sb_v65 = M.alloc("sb_v65", [128, NT * H * 65], BF16)
    for t in range(NT):
        P.add("vector", lambda e, t=t: e.memset(
            bass.AP(sb_v65, 1040 * t + 64, [[NT * H * 65, 128], [65, H], [1, 1]]),
            1.0), inc="tv")
    for t in range(NT):
        for ec in range(NCH):
            for ns in range(2):
                P.add("tensor", lambda e, ec=ec, ns=ns, t=t: e.matmul(
                    ps_big[:, 512 * ns:512 * ns + 512],
                    sb_xlnT[:, S * ec + 128 * t:S * ec + 128 * t + 128],
                    sb_wv[:, E * ec + 512 * ns:E * ec + 512 * ns + 512],
                    start=(ec == 0), stop=False), inc="tt")
        for ns in range(2):
            P.add("tensor", lambda e, ns=ns: e.matmul(
                ps_big[:, 512 * ns:512 * ns + 512],
                sb_ones[0:1, 0:128],
                sb_brow[0:1, 2 * E + 512 * ns:2 * E + 512 * ns + 512],
                start=False, stop=True), inc="tt")

        def v_evac(e, t=t):
            dst = bass.AP(sb_v65, 1040 * t, [[NT * H * 65, 128], [65, H], [1, 64]])
            src = bass.AP(ps_big, 0, [[2048, 128], [64, H], [1, 64]])
            return e.tensor_copy(dst, src)
        P.add("vector", v_evac, waits=[("tt", P.at("tt"))], inc="tv")
        P.barrier("tensor", [("tv", P.at("tv"))])
    proj_tv = P.at("tv")
    tap("qT", lambda: sb_qT[:, 0:2048])
    tap("kT", lambda: sb_kT[:, 0:2048])
    tap("v65", lambda: sb_v65[:, 0:2048])
    M.free("sb_wv", "sb_xlnT")

    sb_etf = M.alloc("sb_etf", [128, ETW], BF16)
    sb_etr = M.alloc("sb_etr", [128, ETW], BF16)
    sb_stg = M.alloc("sb_stg", [128, 2 * TW], BF16)
    sb_A = M.alloc("sb_A", [128, 8 * S], BF16)
    sb_Ss = M.alloc("sb_Ss", [128, 2 * LC], F32)
    sb_B = M.alloc("sb_B", [128, 8 * LC], BF16)
    sb_X = M.alloc("sb_X", [128, 4 * LC], BF16)
    for half in range(2):
        P.add("gpsimd", lambda e, hf=half: e.dma_start(
            out=sb_etf[64 * hf:64 * hf + 64, :], in_=d_etf),
            waits=[("tv", proj_tv)] if half == 0 else [], inc="gd", amt=16)
        P.add("gpsimd", lambda e, hf=half: e.dma_start(
            out=sb_etr[64 * hf:64 * hf + 64, :], in_=d_etr), inc="gd", amt=16)
    gd_tbl = P.at("gd")

    sb_sq = sb_A

    P.add("vector", lambda e: e.tensor_copy(sb_edg[:, 0:1], sb_etf[:, 0:1]),
          waits=[("gd", gd_tbl)], inc="tv")
    P.add("vector", lambda e: e.tensor_copy(sb_edg[:, 1:2], sb_etf[:, ETW - 1:ETW]),
          inc="tv")

    P.add("vector", lambda e: e.tensor_mul(
        sb_sq[0:64, 0:2048], sb_etf[0:64, 0:2048], sb_etf[0:64, 0:2048]),
        waits=[("gd", gd_tbl), ("tt", P.at("tt"))], inc="tv")
    P.barrier("tensor", [("tv", P.at("tv"))])
    for ns in range(4):
        P.add("tensor", lambda e, ns=ns: e.matmul(
            ps_big[0:1, 512 * ns:512 * ns + 512], sb_ones[0:64, 0:1],
            sb_sq[0:64, 512 * ns:512 * ns + 512], start=True, stop=False), inc="tt")
    P.add("vector", lambda e: e.tensor_mul(
        sb_sq[0:64, 0:2048], sb_etf[0:64, 2048:4096], sb_etf[0:64, 2048:4096]),
        waits=[("tt", P.at("tt"))], inc="tv")
    P.barrier("tensor", [("tv", P.at("tv"))])
    for ns in range(4):
        P.add("tensor", lambda e, ns=ns: e.matmul(
            ps_big[0:1, 512 * ns:512 * ns + 512], sb_ones[0:64, 0:1],
            sb_sq[0:64, 512 * ns:512 * ns + 512], start=False, stop=True), inc="tt")
    P.add("vector", lambda e: e.reduce_max(
        sb_tmp[0:1, 0:1], ps_big[0:1, :], axis=AX),
        waits=[("tt", P.at("tt"))], inc="tv", hz=True)
    P.add("scalar", lambda e: e.activation(
        sb_tmp[0:1, 1:2], sb_tmp[0:1, 0:1], AF.Sqrt),
        waits=[("tv", P.at("tv"))], inc="ts")
    me_ts = P.at("ts")
    P.barrier("tensor", [("tv", P.at("tv"))])

    sq_parity = 0
    for h in range(H):
        ch, half = h // 2, h % 2
        qsl = slice(64 * half, 64 * half + 64)
        tpos = (64 * half, 0)
        for src, col in ((sb_qT, 2), (sb_kT, 3)):
            P.add("vector", lambda e, src=src, ch=ch, qsl=qsl: e.tensor_mul(
                sb_sq[qsl, 0:S], src[qsl, S * ch:S * ch + S],
                src[qsl, S * ch:S * ch + S]),
                waits=[("tv", proj_tv), ("tt", P.at("tt"))], inc="tv")
            P.barrier("tensor", [("tv", P.at("tv"))])
            for ns in range(4):
                P.add("tensor", lambda e, ns=ns, qsl=qsl, tpos=tpos: e.matmul(
                    ps_big[0:1, 512 * ns:512 * ns + 512], sb_ones[qsl, 0:1],
                    sb_sq[qsl, 512 * ns:512 * ns + 512], start=True, stop=True,
                    tile_position=tpos), inc="tt")
            P.add("vector", lambda e, col=col: e.reduce_max(
                sb_tmp[0:1, col:col + 1], ps_big[0:1, :], axis=AX),
                waits=[("tt", P.at("tt"))], inc="tv", hz=True)
            P.barrier("tensor", [("tv", P.at("tv"))])
        P.add("scalar", lambda e: e.activation(
            sb_tmp[0:1, 4:5], sb_tmp[0:1, 2:3], AF.Sqrt),
            waits=[("tv", P.at("tv")), ("ts", me_ts)], inc="ts")
        P.add("scalar", lambda e: e.activation(
            sb_tmp[0:1, 5:6], sb_tmp[0:1, 3:4], AF.Sqrt), inc="ts")
        P.add("vector", lambda e: e.tensor_mul(
            sb_tmp[0:1, 6:7], sb_tmp[0:1, 4:5], sb_tmp[0:1, 5:6]),
            waits=[("ts", P.at("ts"))], inc="tv")
        P.add("vector", lambda e: e.tensor_add(
            sb_tmp[0:1, 7:8], sb_tmp[0:1, 4:5], sb_tmp[0:1, 5:6]), inc="tv")
        P.add("vector", lambda e: e.tensor_mul(
            sb_tmp[0:1, 7:8], sb_tmp[0:1, 7:8], sb_tmp[0:1, 1:2]), inc="tv")
        P.add("vector", lambda e: e.tensor_add(
            sb_tmp[0:1, 6:7], sb_tmp[0:1, 6:7], sb_tmp[0:1, 7:8]), inc="tv")
        P.add("vector", lambda e: e.tensor_scalar(
            sb_nbh[0:1, 0:1], sb_tmp[0:1, 6:7], -1.02 / 8.0, None, ALU.mult),
            inc="tv")
        P.add("tensor", lambda e, h=h: e.matmul(
            ps_d[:, 256 + h:256 + h + 1], sb_ones[0:1, 0:128], sb_nbh[0:1, 0:1],
            start=True, stop=True), waits=[("tv", P.at("tv"))], inc="tt")
        P.add("vector", lambda e, h=h: e.tensor_copy(
            sb_negB[:, h:h + 1], ps_d[:, 256 + h:256 + h + 1]),
            waits=[("tt", P.at("tt"))], inc="tv")
        P.barrier("tensor", [("tv", P.at("tv"))])
    negb_tv = P.at("tv")
    if debug:
        tap("negB", lambda: sb_negB[:, 0:1])

    stg_order = [(0, t) for t in range(4)] + [(1, j) for j in range(NT)] \
        + [(0, t) for t in range(4, NT)]
    rot_par = [0]
    rot_ev = [("tv", P.at("tv")), ("tv", P.at("tv"))]
    stg_tile_idx = [0]
    stg_dma_hist = []
    stg_sd = {hh: {} for hh in range(H)}

    def emit_stage_tile(hs, kind, t):
        ch2, half2 = hs // 2, hs % 2
        qsl2 = slice(64 * half2, 64 * half2 + 64)
        tpos2 = (64 * half2, 0)
        src = sb_qT if kind == 0 else sb_kT
        tbl = sb_etr if kind == 0 else sb_etf
        dstD = d_qe if kind == 0 else d_ke
        voff = 1920 - 128 * t
        idx_t = stg_tile_idx[0]
        stg_tile_idx[0] += 1
        sl = idx_t % 2
        ev_last = {}
        for c in range(5):
            ncols = 512 if c < 4 else 128
            par = rot_par[0]
            rot_par[0] ^= 1
            P.add("tensor", lambda e, c=c, ncols=ncols, par=par, t=t,
                  src=src, tbl=tbl, ch2=ch2, qsl2=qsl2, tpos2=tpos2,
                  voff=voff: e.matmul(
                ps_big[:, 512 * par:512 * par + ncols],
                src[qsl2, S * ch2 + 128 * t:S * ch2 + 128 * t + 128],
                tbl[qsl2, voff + 512 * c:voff + 512 * c + ncols],
                start=True, stop=True, tile_position=tpos2),
                waits=[rot_ev[par]], inc="tt")
            tt_c = P.at("tt")
            eng, sem = ("vector", "tv") if c % 2 == 0 else ("scalar", "ts")
            we = [("tt", tt_c)]
            if c < 2 and idx_t >= 2:
                we.append(("sd", stg_dma_hist[idx_t - 2]))
            if eng == "vector":
                P.add(eng, lambda e, c=c, ncols=ncols, par=par, sl=sl:
                      e.tensor_copy(
                    sb_stg[:, TW * sl + 512 * c:TW * sl + 512 * c + ncols],
                    ps_big[:, 512 * par:512 * par + ncols]),
                    waits=we, inc=sem)
            else:
                P.add(eng, lambda e, c=c, ncols=ncols, par=par, sl=sl:
                      e.copy(
                    sb_stg[:, TW * sl + 512 * c:TW * sl + 512 * c + ncols],
                    ps_big[:, 512 * par:512 * par + ncols]),
                    waits=we, inc=sem)
            rot_ev[par] = (sem, P.at(sem))
            ev_last[sem] = P.at(sem)
        P.add("sync", lambda e, sl=sl, t=t, dstD=dstD, hs=hs: e.dma_start(
            out=dstD[hs, t], in_=sb_stg[:, TW * sl:TW * sl + TW]),
            waits=[(s, v) for s, v in ev_last.items()], inc="sd", amt=16)
        stg_sd[hs][(kind, t)] = P.at("sd")
        stg_dma_hist.append(P.at("sd"))

    def emit_edge_dots(hs):
        ch2, half2 = hs // 2, hs % 2
        qsl2 = slice(64 * half2, 64 * half2 + 64)
        tpos2 = (64 * half2, 0)
        for eid in range(2):
            for ns in range(4):
                par = rot_par[0]
                rot_par[0] ^= 1
                P.add("tensor", lambda e, ns=ns, ch2=ch2, qsl2=qsl2,
                      tpos2=tpos2, eid=eid, par=par: e.matmul(
                    ps_big[0:1, 512 * par:512 * par + 512],
                    sb_edg[qsl2, eid:eid + 1],
                    sb_qT[qsl2, S * ch2 + 512 * ns:S * ch2 + 512 * ns + 512],
                    start=True, stop=True, tile_position=tpos2),
                    waits=[rot_ev[par]], inc="tt")
                P.add("vector", lambda e, eid=eid, ns=ns, par=par:
                      e.tensor_copy(
                    sb_qe2[0:1, S * eid + 512 * ns:S * eid + 512 * ns + 512],
                    ps_big[0:1, 512 * par:512 * par + 512]),
                    waits=[("tt", P.at("tt"))], inc="tv")
                rot_ev[par] = ("tv", P.at("tv"))
        P.barrier("tensor", [("tv", P.at("tv"))])

    for kind, t in stg_order:
        emit_stage_tile(0, kind, t)

    for h in range(H):
        ch, half = h // 2, h % 2
        qsl = slice(64 * half, 64 * half + 64)
        tpos = (64 * half, 0)
        emit_edge_dots(h)

        exp_hist = {}
        stt_hist = {}
        slot_free = {}
        vb1_hist = {}
        big_bf5 = ps_big.bitcast(BF16)
        abf_slot = [big_bf5[:, 2048:2560],
                    big_bf5[:, 3072:3584]]

        def emit_a_gather(lc_t, i):
            half = lc_t % 2
            t = 4 * lc_t + i
            wA = [("sd", stg_sd[h][(0, t)])]
            if lc_t >= 2:
                wA.append(("tt", exp_hist[(lc_t - 2, NT - 1, "pv")]))
            else:
                wA.append(("tt", P.at("tt")))
            P.add("gpsimd", lambda e, t=t, i=i, h=h, half=half:
                  e.indirect_dma_start(
                sb_A[:, S * (4 * half + i):S * (4 * half + i) + S], None,
                d_qe_flat, IndirectOffsetOnAxis(sb_idxA[:], 1),
                element_offset=(h * NT + t) * 128 * TW),
                waits=wA, inc="gd", amt=16)

        for i in range(4):
            emit_a_gather(0, i)
        ga_hist = {0: P.at("gd")}
        for lc in range(NLC):
            half = lc % 2
            slot_free[(lc, 0)] = ("tv", P.at("tv"))
            slot_free[(lc, 1)] = ("tv", P.at("tv"))
            slot_free.pop(0, None)
            slot_free.pop(1, None)
            if h == 0 and lc == 0 and debug:
                tap("A", lambda: sb_A[:, 0:2048], wait_sem="gd")
            for j in range(NT):
                pj = [ps_a[:, :], ps_b[:, :]][j % 2]
                sl = j % 2
                wb = [("sd", stg_sd[h][(1, j)])]
                if (lc, j - 8) in vb1_hist:
                    wb.append(("tv", vb1_hist[(lc, j - 8)]))
                elif lc > 0 and (lc - 1, 8 + j) in vb1_hist:
                    wb.append(("tv", vb1_hist[(lc - 1, 8 + j)]))
                else:
                    wb.append(("tv", P.at("tv")))
                P.add("gpsimd", lambda e, j=j, lc=lc, h=h: e.indirect_dma_start(
                    sb_B[:, LC * (j % 8):LC * (j % 8) + LC], None,
                    d_ke_flat, IndirectOffsetOnAxis(sb_idxB[:], 1),
                    element_offset=(h * NT + j) * 128 * TW + LC * lc),
                    waits=wb, inc="gd", amt=16)
                gb_done = P.at("gd")
                if lc + 1 < NLC and j in (8, 10, 12, 14):
                    emit_a_gather(lc + 1, (j - 8) // 2)
                    if j == 14:
                        ga_hist[lc + 1] = P.at("gd")
                edg_runs = []
                tr_list = []
                for i in range(4):
                    t_l = 4 * lc + i
                    if t_l - j <= -9:
                        eid = 0
                    elif t_l - j >= 9:
                        eid = 1
                    else:
                        tr_list.append(i)
                        continue
                    if edg_runs and edg_runs[-1][0] == eid and edg_runs[-1][2] == i:
                        edg_runs[-1] = (eid, edg_runs[-1][1], i + 1)
                    else:
                        edg_runs.append((eid, i, i + 1))
                wt = []
                if (lc, j - 2) in stt_hist:
                    wt.append(("tv", stt_hist[(lc, j - 2)]))
                elif lc > 0 and (lc - 1, 14 + j) in stt_hist:
                    wt.append(("tv", stt_hist[(lc - 1, 14 + j)]))
                else:
                    wt.append(("tv", P.at("tv")))
                P.add("tensor", lambda e, j=j, lc=lc, ch=ch, qsl=qsl,
                      tpos=tpos, p=pj, st=(not edg_runs): e.matmul(
                    p,
                    sb_kT[qsl, S * ch + 128 * j:S * ch + 128 * j + 128],
                    sb_qT[qsl, S * ch + LC * lc:S * ch + LC * lc + LC],
                    start=True, stop=st, tile_position=tpos),
                    waits=wt, inc="tt")
                for ei, (eid, i0, i1) in enumerate(edg_runs):
                    P.add("tensor", lambda e, p=pj, eid=eid, i0=i0, i1=i1,
                          lc=lc, st=(ei == len(edg_runs) - 1): e.matmul(
                        p[:, 128 * i0:128 * i1],
                        sb_ones[0:1, 0:128],
                        sb_qe2[0:1, S * eid + LC * lc + 128 * i0:
                               S * eid + LC * lc + 128 * i1],
                        start=False, stop=st), inc="tt")
                tt_scores = P.at("tt")
                wtr = [("gd", ga_hist[lc])]
                sf = slot_free.get((lc, sl)) if j < 2 else slot_free.get(sl)
                if sf is None:
                    sf = ("tv", P.at("tv"))
                wtr.append(sf)
                first_tr = True
                for i in tr_list:
                    P.add("tensor", lambda e, i=i, j=j, half=half, sl=sl:
                          e.transpose(
                        abf_slot[sl][:, 128 * i:128 * i + 128],
                        sb_A[:, S * (4 * half + i) + 128 * j:
                             S * (4 * half + i) + 128 * j + 128],
                        sb_iden[:, :]),
                        waits=(wtr if first_tr else []), inc="tt")
                    first_tr = False
                tt_atr = P.at("tt")
                wexp_extra = []
                if (lc, j - 2, "exp") in exp_hist:
                    wexp_extra = [("ts", exp_hist[(lc, j - 2, "exp")])]
                elif lc > 0:
                    wexp_extra = [("ts", exp_hist[(lc - 1, 14 + j, "exp")])]
                if tr_list:
                    P.add("vector", lambda e, j=j, sl=sl: e.tensor_tensor(
                        sb_Ss[:, LC * sl:LC * sl + LC], abf_slot[sl][:, 0:LC],
                        sb_B[:, LC * (j % 8):LC * (j % 8) + LC], ALU.add),
                        waits=[("tt", tt_atr), ("gd", gb_done)] + wexp_extra,
                        inc="tv")
                    vb1_hist[(lc, j)] = P.at("tv")
                    slot_free[sl] = ("tv", P.at("tv"))
                    P.add("vector", lambda e, p=pj, sl=sl: e.scalar_tensor_tensor(
                        sb_Ss[:, LC * sl:LC * sl + LC], p, 1.0,
                        sb_Ss[:, LC * sl:LC * sl + LC], ALU.mult, ALU.add),
                        waits=[("tt", tt_scores)], inc="tv")
                else:
                    P.add("vector", lambda e, p=pj, j=j, sl=sl:
                          e.scalar_tensor_tensor(
                        sb_Ss[:, LC * sl:LC * sl + LC], p, 1.0,
                        sb_B[:, LC * (j % 8):LC * (j % 8) + LC],
                        ALU.mult, ALU.add),
                        waits=[("tt", tt_scores), ("gd", gb_done)] + wexp_extra,
                        inc="tv")
                    vb1_hist[(lc, j)] = P.at("tv")
                stt_hist[(lc, j)] = P.at("tv")
                if debug and h == 0 and lc == 0 and j == 0:
                    tap("Bt", lambda: sb_B[:, 0:LC], wait_sem="gd")
                    tap("scr", lambda: sb_Ss[:, 0:LC], wait_sem="tv")
                P.add("scalar", lambda e, sl=sl, x4=j % 4, h=h: e.activation(
                    sb_X[:, LC * x4:LC * x4 + LC],
                    sb_Ss[:, LC * sl:LC * sl + LC], AF.Exp,
                    bias=sb_negB[:, h:h + 1], scale=0.125),
                    waits=[("tv", stt_hist[(lc, j)]), ("tv", negb_tv)], inc="ts")
                exp_hist[(lc, j, "exp")] = P.at("ts")
                if debug and h == 0 and lc == 0 and j == 0:
                    tap("X", lambda: sb_X[:, 0:LC], wait_sem="ts")

                def emit_pv(jp):
                    wpv = [("ts", exp_hist[(lc, jp, "exp")])]
                    if jp == 0:
                        wpv.append(("tv", P.at("tv")))
                    P.add("tensor", lambda e, jp=jp, h=h: e.matmul(
                        ps_c[0:65, :],
                        sb_v65[:, 1040 * jp + 65 * h:1040 * jp + 65 * h + 65],
                        sb_X[:, LC * (jp % 4):LC * (jp % 4) + LC],
                        start=(jp == 0), stop=(jp == NT - 1)),
                        waits=wpv, inc="tt")
                    exp_hist[(lc, jp, "pv")] = P.at("tt")
                if j >= 2:
                    emit_pv(j - 2)
                gi = lc * NT + j
                if h + 1 < H and gi % 16 == 0:
                    for kind2, t2 in stg_order[8 * (gi // 16):
                                               8 * (gi // 16) + 8]:
                        emit_stage_tile(h + 1, kind2, t2)
                if j == NT - 1:
                    emit_pv(NT - 2)
                    emit_pv(NT - 1)
            P.add("vector", lambda e: e.tensor_copy(
                sb_ctv[0:65, :], ps_c[0:65, :]),
                waits=[("tt", P.at("tt")), ("sd", P.at("sd"))], inc="tv")
            for i in range(4):
                wtr2 = [("tv", P.at("tv"))] if i == 0 else []
                P.add("tensor", lambda e, i=i: e.transpose(
                    ps_d[:, 65 * i:65 * i + 65],
                    sb_ctv[0:65, 128 * i:128 * i + 128],
                    sb_idenf[0:65, 0:65]), waits=wtr2, inc="tt")
            P.add("vector", lambda e: e.reciprocal(
                sb_zr[:, 0:4], bass.AP(ps_d, 64, [[512, 128], [65, 4]])),
                waits=[("tt", P.at("tt"))], inc="tv", hz=True)
            P.add("vector", lambda e: e.tensor_tensor(
                bass.AP(sb_ctp, 0, [[256, 128], [64, 4], [1, 64]]),
                bass.AP(ps_d, 0, [[512, 128], [65, 4], [1, 64]]),
                bass.AP(sb_zr, 0, [[4, 128], [1, 4], [0, 64]]),
                ALU.mult), inc="tv")
            def ctx_store(e, h=h, lc=lc):
                dst = bass.AP(d_ctx.tensor, h * S * D + LC * lc * D,
                              [[D, 128], [128 * D, 4], [1, D]])
                src = bass.AP(sb_ctp, 0, [[4 * D, 128], [D, 4], [1, D]])
                return e.dma_start(out=dst, in_=src)
            P.add("sync", ctx_store, waits=[("tv", P.at("tv"))], inc="sd", amt=16)
    attn_sd = P.at("sd")

    M.free("sb_X", "sb_B", "sb_Ss", "sb_A", "sb_stg", "sb_etr",
           "sb_etf", "sb_v65", "sb_kT", "sb_qT")
    sb_xpa = M.alloc("sb_xpa", [128, NT * E], F32)
    sb_xd = M.alloc("sb_xd", [128, NT * E], BF16)
    sb_wd = M.alloc("sb_wd", [128, NCH * E], BF16)
    sb_ctl = M.alloc("sb_ctl", [128, E], F32)
    sb_ffw = M.alloc("sb_ffw", [128, E], F32)
    sb_q1 = M.alloc("sb_q1", [128, E], F32)
    sb_q2 = M.alloc("sb_q2", [128, E], F32)
    sb_q3 = M.alloc("sb_q3", [128, E], F32)
    sb_sc = M.alloc("sb_sc", [128, 32], F32)
    sb_scr = M.alloc("sb_scr", [128, 32], F32)

    def emit_quant(dst_ap_fn, src_sb):
        def a3(sb, off=0):
            return bass.AP(sb, off, [[E, 128], [32, 32], [1, 32]])

        def scbc(sb):
            return bass.AP(sb, 0, [[32, 128], [1, 32], [0, 32]])
        P.add("vector", lambda e: e.tensor_reduce(
            sb_sc[:], a3(src_sb), axis=AX, op=ALU.max,
            apply_absolute_value=True), inc="tv", hz=True)
        P.add("vector", lambda e: e.tensor_scalar(
            sb_sc[:], sb_sc[:], 1.0 / 7.5, 1e-30, ALU.mult, ALU.max), inc="tv")
        P.add("vector", lambda e: e.reciprocal(sb_scr[:], sb_sc[:]), inc="tv", hz=True)
        P.add("vector", lambda e: e.tensor_tensor(
            a3(sb_q1), a3(src_sb), scbc(sb_scr), ALU.mult), inc="tv")
        P.add("vector", lambda e: e.tensor_mul(sb_q2[:], sb_q1[:], sb_q1[:]), inc="tv")
        P.add("vector", lambda e: e.tensor_scalar(
            sb_q3[:], sb_q2[:], 4.0, None, ALU.is_ge), inc="tv")
        P.add("vector", lambda e: e.tensor_scalar(
            sb_q2[:], sb_q2[:], 16.0, None, ALU.is_ge), inc="tv")
        P.add("vector", lambda e: e.tensor_scalar(
            sb_dump[:], sb_q3[:], 0.125, 0.125, ALU.mult, ALU.add), inc="tv")
        P.add("vector", lambda e: e.scalar_tensor_tensor(
            sb_dump[:], sb_q2[:], 0.25, sb_dump[:], ALU.mult, ALU.add), inc="tv")
        P.add("vector", lambda e: e.tensor_scalar(
            sb_q3[:], sb_q3[:], -4.0, 8.0, ALU.mult, ALU.add), inc="tv")
        P.add("vector", lambda e: e.scalar_tensor_tensor(
            sb_q3[:], sb_q2[:], -2.0, sb_q3[:], ALU.mult, ALU.add), inc="tv")
        P.add("vector", lambda e: e.tensor_mul(sb_q1[:], sb_q1[:], sb_q3[:]), inc="tv")
        P.add("vector", lambda e: e.tensor_copy(
            sb_q2.bitcast(I32)[:], sb_q1[:]), inc="tv")
        P.add("vector", lambda e: e.tensor_copy(
            sb_q1[:], sb_q2.bitcast(I32)[:]), inc="tv")
        P.add("vector", lambda e: e.tensor_mul(sb_q1[:], sb_q1[:], sb_dump[:]), inc="tv")
        P.add("vector", lambda e: e.tensor_tensor(
            dst_ap_fn(), a3(sb_q1), scbc(sb_sc), ALU.mult), inc="tv")

    for t in range(NT):
        buf = sb_xt if t % 2 == 0 else sb_xt2
        w = [("tv", P.at("tv"))] if t >= 2 else []
        P.add("sync", lambda e, t=t, b=buf: e.dma_start(
            out=b[:], in_=d_x[128 * t:128 * t + 128, :]), waits=w, inc="sd", amt=16)

        def ctx_ld(e, t=t):
            src = bass.AP(d_ctx.tensor, 128 * t * D, [[D, 128], [S * D, H], [1, D]])
            dst = bass.AP(sb_ctl, 0, [[E, 128], [D, H], [1, D]])
            return e.dma_start(out=dst, in_=src)
        P.add("sync", ctx_ld, waits=[("sd", attn_sd)], inc="sd", amt=16)
        P.add("vector", lambda e, b=buf: e.tensor_add(b[:], b[:], sb_ctl[:]),
              waits=[("sd", P.at("sd"))], inc="tv")
        P.add("vector", lambda e, t=t, b=buf: e.tensor_copy(
            sb_xpa[:, E * t:E * t + E], b[:]), inc="tv")
        emit_ln(lambda: sb_q3[:], buf)
        P.add("vector", lambda e: e.tensor_copy(sb_ffw[:], sb_q3[:]), inc="tv")
        emit_quant(lambda t=t: bass.AP(sb_xd, E * t, [[NT * E, 128], [32, 32], [1, 32]]),
                   sb_ffw)
        if t == 0:
            tap("xpa", lambda: sb_xpa[:, 0:E])
            tap("xd", lambda: sb_xd[:, 0:E])

    for fc in range(NCH):
        w = [("tv", P.at("tv"))] if fc >= 1 else [("tv", P.at("tv"))]
        P.add("sync", lambda e, fc=fc: e.dma_start(
            out=sb_ffw[:], in_=d_ffw[128 * fc:128 * fc + 128, :]),
            waits=w, inc="sd", amt=16)
        P.barrier("vector", [("sd", P.at("sd"))])
        emit_quant(lambda fc=fc: bass.AP(sb_wd, E * fc, [[NCH * E, 128], [32, 32], [1, 32]]),
                   sb_ffw)
        if fc == 0:
            tap("wd", lambda: sb_wd[:, 0:E])
    quant_tv = P.at("tv")

    M.free("sb_scr", "sb_sc", "sb_q3", "sb_q2", "sb_q1", "sb_ffw", "sb_ctl")
    sb_xdT = M.alloc("sb_xdT", [128, NCH * S], BF16)
    sb_wdT = M.alloc("sb_wdT", [128, NCH * E], BF16)
    sb_yt = M.alloc("sb_yt", [128, E], F32)

    tp_i = 0
    P.barrier("tensor", [("tv", quant_tv)])
    for t in range(NT):
        for kc in range(NCH):
            pb = psA_bf if tp_i % 2 == 0 else psB_bf
            w = [("tv", P.at("tv"))] if tp_i >= 2 else []
            P.add("tensor", lambda e, t=t, kc=kc, p=pb: e.transpose(
                p[:, 0:128], sb_xd[:, E * t + 128 * kc:E * t + 128 * kc + 128],
                sb_iden[:]), waits=w, inc="tt")
            P.add("vector", lambda e, t=t, kc=kc, p=pb: e.tensor_copy(
                sb_xdT[:, S * kc + 128 * t:S * kc + 128 * t + 128], p[:, 0:128]),
                waits=[("tt", P.at("tt"))], inc="tv")
            tp_i += 1
    for fc in range(NCH):
        for kc in range(NCH):
            pb = psA_bf if tp_i % 2 == 0 else psB_bf
            P.add("tensor", lambda e, fc=fc, kc=kc, p=pb: e.transpose(
                p[:, 0:128], sb_wd[:, E * fc + 128 * kc:E * fc + 128 * kc + 128],
                sb_iden[:]), waits=[("tv", P.at("tv"))], inc="tt")
            P.add("vector", lambda e, fc=fc, kc=kc, p=pb: e.tensor_copy(
                sb_wdT[:, E * kc + 128 * fc:E * kc + 128 * fc + 128], p[:, 0:128]),
                waits=[("tt", P.at("tt"))], inc="tv")
            tp_i += 1
    trans_tv = P.at("tv")

    P.barrier("tensor", [("tv", trans_tv)])
    y_sd = {}
    for t in range(NT):
        if t >= 1:
            P.barrier("tensor", [("tv", P.at("tv"))])
        for kc in range(NCH):
            for ns in range(2):
                P.add("tensor", lambda e, t=t, kc=kc, ns=ns: e.matmul(
                    ps_big[:, 512 * ns:512 * ns + 512],
                    sb_xdT[:, S * kc + 128 * t:S * kc + 128 * t + 128],
                    sb_wdT[:, E * kc + 512 * ns:E * kc + 512 * ns + 512],
                    start=(kc == 0), stop=False), inc="tt")
        for ns in range(2):
            P.add("tensor", lambda e, ns=ns: e.matmul(
                ps_big[:, 512 * ns:512 * ns + 512],
                sb_ones[0:1, 0:128], sb_ffb[0:1, 512 * ns:512 * ns + 512],
                start=False, stop=True), inc="tt")
        wv = [("tt", P.at("tt"))]
        if t >= 1:
            wv.append(("sd", y_sd[t - 1]))
        P.add("vector", lambda e, t=t: e.tensor_add(
            sb_yt[:, 0:E], ps_big[:, 0:1024],
            sb_xpa[:, E * t:E * t + E]), waits=wv, inc="tv")
        P.add("sync", lambda e, t=t: e.dma_start(
            out=d_y[128 * t:128 * t + 128, :], in_=sb_yt[:, 0:E]),
            waits=[("tv", P.at("tv"))], inc="sd", amt=16)
        y_sd[t] = P.at("sd")

    P.add("sync", lambda e: None, waits=[("sd", P.at("sd"))])

    def replay(name, eng):
        for fn, waits, inc, amt in P.items[name]:
            for s_, v in waits:
                eng.wait_ge(sem_h[s_], v)
            inst = fn(eng)
            if inc is not None:
                inst.then_inc(sem_h[inc], amt)

    @block.sync
    def _(sync):
        replay("sync", sync)

    @block.vector
    def _(vector):
        replay("vector", vector)

    @block.scalar
    def _(scalar):
        replay("scalar", scalar)

    @block.tensor
    def _(tensor):
        replay("tensor", tensor)

    @block.gpsimd
    def _(gpsimd):
        replay("gpsimd", gpsimd)

    block_cm.__exit__(None, None, None)
    M.free_all()
    for cm in reversed(psum_cms):
        cm.__exit__(None, None, None)
    for cm in reversed(sem_cms):
        cm.__exit__(None, None, None)
    return nc


_NC_CACHE = [None]


def _host_prep(inputs):
    x = np.ascontiguousarray(np.asarray(inputs["L0_attn_hidden_states"], dtype=np.float32))
    Et = np.asarray(inputs["L0_attn_dist_emb"], dtype=np.float32)
    E_ext = np.concatenate([
        np.broadcast_to(Et[0:1], (1024, D)), Et,
        np.broadcast_to(Et[-1:], (1025, D))], axis=0)
    p = np.arange(128, dtype=np.int32).reshape(128, 1)
    base = {
        "wqT": np.ascontiguousarray(np.asarray(inputs["L0_attn_wq"], np.float32).T),
        "wkT": np.ascontiguousarray(np.asarray(inputs["L0_attn_wk"], np.float32).T),
        "wvT": np.ascontiguousarray(np.asarray(inputs["L0_attn_wv"], np.float32).T),
        "bq": np.ascontiguousarray(np.asarray(inputs["L0_attn_bq"], np.float32).reshape(1, E)),
        "bk": np.ascontiguousarray(np.asarray(inputs["L0_attn_bk"], np.float32).reshape(1, E)),
        "bv": np.ascontiguousarray(np.asarray(inputs["L0_attn_bv"], np.float32).reshape(1, E)),
        "etf": np.ascontiguousarray(E_ext.T),
        "etr": np.ascontiguousarray(E_ext[::-1].T),
        "ffw": np.ascontiguousarray(np.asarray(inputs["L0_ffn_weight"], np.float32)),
        "ffb": np.ascontiguousarray(np.asarray(inputs["L0_ffn_bias"], np.float32).reshape(1, E)),
        "idxA": (TW - 1) * p + 128,
        "idxB": (TW - 1) * p + 127,
        "idn": np.ascontiguousarray(np.eye(128, dtype=np.float32)),
    }
    return x, base


def debug_run(inputs, core=0):
    x, base = _host_prep(inputs)
    nc = build_program(debug=True)
    im = dict(base)
    im["x"] = np.ascontiguousarray(x[core])
    res = run_bass_kernel_spmd(nc, [im], core_ids=[0])
    return res.results[0]


def kernel(L0_attn_hidden_states, L0_attn_wq, L0_attn_bq, L0_attn_wk,
           L0_attn_bk, L0_attn_wv, L0_attn_bv, L0_attn_dist_emb,
           L0_ffn_weight, L0_ffn_bias):
    x = np.ascontiguousarray(np.asarray(L0_attn_hidden_states, dtype=np.float32))
    wq = np.asarray(L0_attn_wq, dtype=np.float32)
    wk = np.asarray(L0_attn_wk, dtype=np.float32)
    wv = np.asarray(L0_attn_wv, dtype=np.float32)
    bq = np.ascontiguousarray(np.asarray(L0_attn_bq, dtype=np.float32).reshape(1, E))
    bk = np.ascontiguousarray(np.asarray(L0_attn_bk, dtype=np.float32).reshape(1, E))
    bv = np.ascontiguousarray(np.asarray(L0_attn_bv, dtype=np.float32).reshape(1, E))
    Et = np.asarray(L0_attn_dist_emb, dtype=np.float32)
    ffw = np.ascontiguousarray(np.asarray(L0_ffn_weight, dtype=np.float32))
    ffb = np.ascontiguousarray(np.asarray(L0_ffn_bias, dtype=np.float32).reshape(1, E))

    E_ext = np.concatenate([
        np.broadcast_to(Et[0:1], (1024, D)),
        Et,
        np.broadcast_to(Et[-1:], (1025, D)),
    ], axis=0)
    etf = np.ascontiguousarray(E_ext.T)
    etr = np.ascontiguousarray(E_ext[::-1].T)

    wqT = np.ascontiguousarray(wq.T)
    wkT = np.ascontiguousarray(wk.T)
    wvT = np.ascontiguousarray(wv.T)

    p = np.arange(128, dtype=np.int32).reshape(128, 1)
    idxA = (TW - 1) * p + 128
    idxB = (TW - 1) * p + 127
    iden = np.ascontiguousarray(np.eye(128, dtype=np.float32))

    if _NC_CACHE[0] is None:
        _NC_CACHE[0] = build_program()
    nc = _NC_CACHE[0]

    in_maps = []
    for c in range(8):
        in_maps.append({
            "x": np.ascontiguousarray(x[c]),
            "wqT": wqT, "wkT": wkT, "wvT": wvT,
            "bq": bq, "bk": bk, "bv": bv,
            "etf": etf, "etr": etr,
            "ffw": ffw, "ffb": ffb,
            "idxA": idxA, "idxB": idxB, "idn": iden,
        })
    res = run_bass_kernel_spmd(nc, in_maps, core_ids=list(range(8)))
    return np.stack([r["y"] for r in res.results], axis=0)


# version 20
# speedup vs baseline: 1.0130x; 1.0127x over previous
import numpy as np
import concourse.bass as bass
import concourse.mybir as mybir
from concourse.bass_utils import run_bass_kernel_spmd
from concourse.bass_core_rust import IndirectOffsetOnAxis

F32 = mybir.dt.float32
BF16 = mybir.dt.bfloat16
I32 = mybir.dt.int32
AF = mybir.ActivationFunctionType
ALU = mybir.AluOpType
AX = mybir.AxisListType.X

S = 2048
E = 1024
H = 16
D = 64
NT = S // 128
NCH = E // 128
TW = 2176
ETW = 4096
LC = 512
NLC = S // NLC if False else S // LC


class Prog:
    ENGS = ("sync", "vector", "scalar", "tensor", "gpsimd")

    def __init__(self):
        self.items = {e: [] for e in self.ENGS}
        self.counts = {}
        self.pending = {e: [] for e in self.ENGS}
        self.hazard = {}

    def sem(self, name):
        self.counts[name] = 0

    FLUSH = {"vector": "tv", "scalar": "ts"}

    def add(self, eng, fn, waits=(), inc=None, amt=1, hz=False):
        w = self.pending[eng] + list(waits)
        self.pending[eng] = []
        fs = self.FLUSH.get(eng)
        if fs is not None and (hz or self.hazard.get(eng, False)) \
                and self.counts.get(fs, 0) > 0:
            w.append((fs, self.counts[fs]))
        self.hazard[eng] = hz
        self.items[eng].append((fn, w, inc, amt))
        if inc is not None:
            self.counts[inc] += amt
        return self.counts.get(inc)

    def barrier(self, eng, waits):
        self.pending[eng].extend(waits)

    def at(self, sem):
        return self.counts[sem]


class Mem:

    def __init__(self, nc):
        self.nc = nc
        self.live = {}

    def alloc(self, name, shape, dt, side=None):
        if side is None:
            cm = self.nc.sbuf_tensor(name, shape, dt)
        else:
            cm = self.nc.sbuf_tensor(name, shape, dt, side=side)
        h = cm.__enter__()
        self.live[name] = (cm, h)
        return h

    def free(self, *names):
        for n in names:
            cm, _ = self.live.pop(n)
            cm.__exit__(None, None, None)

    def free_all(self):
        for n in reversed(list(self.live)):
            self.free(n)


DEBUG = False


def build_program(debug=False):
    nc = bass.Bass("TRN2", target_bir_lowering=False, debug=False)

    d_x = nc.dram_tensor("x", [S, E], F32, kind="ExternalInput").ap()
    d_wqT = nc.dram_tensor("wqT", [E, E], F32, kind="ExternalInput").ap()
    d_wkT = nc.dram_tensor("wkT", [E, E], F32, kind="ExternalInput").ap()
    d_wvT = nc.dram_tensor("wvT", [E, E], F32, kind="ExternalInput").ap()
    d_bq = nc.dram_tensor("bq", [1, E], F32, kind="ExternalInput").ap()
    d_bk = nc.dram_tensor("bk", [1, E], F32, kind="ExternalInput").ap()
    d_bv = nc.dram_tensor("bv", [1, E], F32, kind="ExternalInput").ap()
    d_etf = nc.dram_tensor("etf", [D, ETW], F32, kind="ExternalInput").ap()
    d_etr = nc.dram_tensor("etr", [D, ETW], F32, kind="ExternalInput").ap()
    d_ffw = nc.dram_tensor("ffw", [E, E], F32, kind="ExternalInput").ap()
    d_ffb = nc.dram_tensor("ffb", [1, E], F32, kind="ExternalInput").ap()
    d_idxA = nc.dram_tensor("idxA", [128, 1], I32, kind="ExternalInput").ap()
    d_idxB = nc.dram_tensor("idxB", [128, 1], I32, kind="ExternalInput").ap()
    d_iden = nc.dram_tensor("idn", [128, 128], F32, kind="ExternalInput").ap()
    d_y = nc.dram_tensor("y", [S, E], F32, kind="ExternalOutput").ap()

    ikind = "ExternalOutput" if debug else "Internal"
    d_qe = nc.dram_tensor("qe_st", [H, NT, 128, TW], BF16, kind=ikind).ap()
    d_ke = nc.dram_tensor("ke_st", [H, NT, 128, TW], BF16, kind=ikind).ap()
    d_qe_flat = bass.AP(d_qe.tensor, 0, [[TW, H * NT * 128], [1, TW]])
    d_ke_flat = bass.AP(d_ke.tensor, 0, [[TW, H * NT * 128], [1, TW]])
    d_ctx = nc.dram_tensor("ctx_st", [H, S, D], F32, kind=ikind).ap()
    d_dbg = {}
    if debug:
        for nm, dt_ in [("xln", BF16), ("xlnT", BF16), ("qT", BF16),
                        ("kT", BF16), ("v65", BF16), ("negB", F32),
                        ("A", BF16), ("Bt", BF16), ("scr", F32), ("X", BF16),
                        ("xpa", F32), ("xd", BF16), ("wd", BF16),
                        ("wq0", BF16), ("brow", BF16), ("qTe", BF16)]:
            d_dbg[nm] = nc.dram_tensor("dbg_" + nm, [128, 2048], dt_,
                                       kind="ExternalOutput").ap()

    M = Mem(nc)
    sb_xt = M.alloc("sb_xt", [128, E], F32)
    sb_xt2 = M.alloc("sb_xt2", [128, E], F32)
    sb_brow = M.alloc("sb_brow", [1, 3 * E], BF16)
    sb_ffb = M.alloc("sb_ffb", [1, E], BF16)
    sb_iden = M.alloc("sb_iden", [128, 128], BF16)
    sb_idenf = M.alloc("sb_idenf", [128, 128], F32)
    sb_ones = M.alloc("sb_ones", [128, 512], BF16)
    sb_idxA = M.alloc("sb_idxA", [128, 1], I32)
    sb_idxB = M.alloc("sb_idxB", [128, 1], I32)
    sb_tmp = M.alloc("sb_tmp", [1, 8], F32)
    sb_mv = M.alloc("sb_mv", [128, 4], F32)
    sb_zr = M.alloc("sb_zr", [128, 4], F32)
    sb_negB = M.alloc("sb_negB", [128, H], F32)
    sb_nbh = M.alloc("sb_nbh", [1, 2], BF16)
    sb_dump = M.alloc("sb_dump", [128, E], F32)
    sb_ctv = M.alloc("sb_ctv", [128, 512], F32)
    sb_ctp = M.alloc("sb_ctp", [128, 4 * D], F32)
    sb_edg = M.alloc("sb_edg", [128, 2], BF16)
    sb_qe2 = M.alloc("sb_qe2", [1, 2 * S], BF16)

    psum_cms = []

    def psum(name, shape, dt):
        cm = nc.psum_tensor(name, shape, dt)
        h = cm.__enter__()
        psum_cms.append(cm)
        return h

    ps_a = psum("ps_a", [128, 512], F32)
    ps_b = psum("ps_b", [128, 512], F32)
    ps_c = psum("ps_c", [128, 512], F32)
    ps_d = psum("ps_d", [128, 512], F32)
    ps_big = psum("ps_big", [128, 2048], F32)
    psA_bf = ps_a.bitcast(BF16)
    psB_bf = ps_b.bitcast(BF16)

    sem_cms = []

    def mksem(name):
        cm = nc.semaphore(name)
        h = cm.__enter__()
        sem_cms.append(cm)
        return h

    sems = ["sd", "gd", "tv", "ts", "tt"]
    sem_h = {s: mksem(s) for s in sems}

    block_cm = nc.Block()
    block = block_cm.__enter__()

    P = Prog()
    for s in sems:
        P.sem(s)

    def tap(nm, src_fn, wait_sem="tv"):
        if not debug:
            return
        P.add("sync", lambda e: e.dma_start(out=d_dbg[nm][0:src_fn().shape[0],
                                                          0:src_fn().shape[1]],
                                            in_=src_fn(),
                                            allow_slow_non_contiguous=True),
              waits=[(wait_sem, P.at(wait_sem))], inc="sd", amt=16)

    sb_xlnT = M.alloc("sb_xlnT", [128, NCH * S], BF16, side="right")
    sb_wv = M.alloc("sb_wv", [128, NCH * E], BF16, side="right")
    sb_wk = M.alloc("sb_wk", [128, NCH * E], BF16, side="right")
    sb_wq = M.alloc("sb_wq", [128, NCH * E], BF16, side="right")
    sb_xln = M.alloc("sb_xln", [128, NT * E], BF16, side="right")

    P.add("sync", lambda e: e.dma_start(out=sb_idxA[:], in_=d_idxA), inc="sd", amt=16)
    P.add("sync", lambda e: e.dma_start(out=sb_idxB[:], in_=d_idxB), inc="sd", amt=16)
    P.add("sync", lambda e: e.dma_start(out=sb_idenf[:], in_=d_iden), inc="sd", amt=16)
    P.add("gpsimd", lambda e: e.dma_start(out=sb_iden[:], in_=d_iden), inc="gd", amt=16)
    for ch in range(NCH):
        for wsb, wd_ in ((sb_wq, d_wqT), (sb_wk, d_wkT), (sb_wv, d_wvT)):
            P.add("gpsimd", lambda e, c=ch, wsb=wsb, wd_=wd_: e.dma_start(
                out=wsb[:, E * c:E * c + E], in_=wd_[128 * c:128 * c + 128, :]),
                inc="gd", amt=16)
    P.add("gpsimd", lambda e: e.dma_start(out=sb_brow[:, 0:E], in_=d_bq), inc="gd", amt=16)
    P.add("gpsimd", lambda e: e.dma_start(out=sb_brow[:, E:2 * E], in_=d_bk), inc="gd", amt=16)
    P.add("gpsimd", lambda e: e.dma_start(out=sb_brow[:, 2 * E:3 * E], in_=d_bv), inc="gd", amt=16)
    P.add("gpsimd", lambda e: e.dma_start(out=sb_ffb[:], in_=d_ffb), inc="gd", amt=16)
    gd_loads = P.at("gd")

    P.add("vector", lambda e: e.memset(sb_ones[:], 1.0), inc="tv")

    def emit_ln(dst_ap_fn, src_sb, first_waits=()):
        P.add("vector", lambda e: e.reduce_sum(
            sb_mv[:, 0:1], src_sb[:], axis=AX), waits=first_waits, inc="tv", hz=True)
        P.add("vector", lambda e: e.tensor_scalar(
            sb_mv[:, 0:1], sb_mv[:, 0:1], 1.0 / E, None, ALU.mult), inc="tv")
        P.add("scalar", lambda e: e.activation(
            sb_dump[:], src_sb[:], AF.Square, accum_out=sb_mv[:, 1:2]),
            waits=[("tv", P.at("tv"))], inc="ts")
        P.add("vector", lambda e: e.tensor_mul(
            sb_mv[:, 2:3], sb_mv[:, 0:1], sb_mv[:, 0:1]),
            waits=[("ts", P.at("ts"))], inc="tv")
        P.add("vector", lambda e: e.tensor_scalar(
            sb_mv[:, 1:2], sb_mv[:, 1:2], 1.0 / E, None, ALU.mult), inc="tv")
        P.add("vector", lambda e: e.tensor_sub(
            sb_mv[:, 1:2], sb_mv[:, 1:2], sb_mv[:, 2:3]), inc="tv")
        P.add("vector", lambda e: e.tensor_scalar(
            sb_mv[:, 1:2], sb_mv[:, 1:2], 1e-5, None, ALU.add), inc="tv")
        P.add("scalar", lambda e: e.activation(
            sb_mv[:, 3:4], sb_mv[:, 1:2], AF.Sqrt),
            waits=[("tv", P.at("tv"))], inc="ts")
        P.add("vector", lambda e: e.reciprocal(
            sb_mv[:, 2:3], sb_mv[:, 3:4]), waits=[("ts", P.at("ts"))], inc="tv", hz=True)
        P.add("vector", lambda e: e.tensor_scalar(
            dst_ap_fn(), src_sb[:], sb_mv[:, 0:1], sb_mv[:, 2:3],
            ALU.subtract, ALU.mult), inc="tv")

    for t in range(NT):
        buf = sb_xt if t % 2 == 0 else sb_xt2
        w = [("tv", P.at("tv"))] if t >= 2 else []
        P.add("sync", lambda e, t=t, b=buf: e.dma_start(
            out=b[:], in_=d_x[128 * t:128 * t + 128, :]), waits=w, inc="sd", amt=16)
        emit_ln(lambda t=t: sb_xln[:, E * t:E * t + E], buf,
                first_waits=[("sd", P.at("sd"))])
        if t == 0:
            tap("xln", lambda: sb_xln[:, 0:E])
    ln1_tv = P.at("tv")

    tp_i = 0
    for t in range(NT):
        for ch in range(NCH):
            pb = psA_bf if tp_i % 2 == 0 else psB_bf
            w = [("tv", P.at("tv"))] if tp_i >= 2 else [("tv", ln1_tv)]
            P.add("tensor", lambda e, t=t, c=ch, p=pb: e.transpose(
                p[:, 0:128], sb_xln[:, E * t + 128 * c:E * t + 128 * c + 128],
                sb_iden[:]), waits=w, inc="tt")
            P.add("vector", lambda e, t=t, c=ch, p=pb: e.tensor_copy(
                sb_xlnT[:, S * c + 128 * t:S * c + 128 * t + 128], p[:, 0:128]),
                waits=[("tt", P.at("tt")), ("gd", gd_loads)], inc="tv")
            tp_i += 1
    xlnT_tv = P.at("tv")
    tap("xlnT", lambda: sb_xlnT[:, 0:2048])
    M.free("sb_xln")
    sb_qT = M.alloc("sb_qT", [128, NCH * S], BF16)
    sb_kT = M.alloc("sb_kT", [128, NCH * S], BF16)

    P.barrier("tensor", [("tv", xlnT_tv)])
    for ch in range(NCH):
        for wsb, qsb, boff in ((sb_wq, sb_qT, 0), (sb_wk, sb_kT, E)):
            for ec in range(NCH):
                for ns in range(4):
                    P.add("tensor", lambda e, ec=ec, ns=ns, ch=ch, wsb=wsb: e.matmul(
                        ps_big[:, 512 * ns:512 * ns + 512],
                        wsb[:, E * ec + 128 * ch:E * ec + 128 * ch + 128],
                        sb_xlnT[:, S * ec + 512 * ns:S * ec + 512 * ns + 512],
                        start=(ec == 0), stop=False), inc="tt")
            for ns in range(4):
                P.add("tensor", lambda e, ns=ns, ch=ch, boff=boff: e.matmul(
                    ps_big[:, 512 * ns:512 * ns + 512],
                    sb_brow[0:1, boff + 128 * ch:boff + 128 * ch + 128],
                    sb_ones[0:1, 0:512], start=False, stop=True), inc="tt")
            P.add("vector", lambda e, ch=ch, qsb=qsb: e.tensor_copy(
                qsb[:, S * ch:S * ch + S], ps_big[:, :]),
                waits=[("tt", P.at("tt"))], inc="tv")
            P.barrier("tensor", [("tv", P.at("tv"))])
            if ch == 0 and qsb is sb_qT:
                tap("qTe", lambda: sb_qT[:, 0:2048])
                tap("wq0", lambda: sb_wq[:, 0:2048])
                tap("brow", lambda: sb_brow[0:1, 0:2048])
    M.free("sb_wq", "sb_wk")
    sb_v65 = M.alloc("sb_v65", [128, NT * H * 65], BF16)
    for t in range(NT):
        P.add("vector", lambda e, t=t: e.memset(
            bass.AP(sb_v65, 1040 * t + 64, [[NT * H * 65, 128], [65, H], [1, 1]]),
            1.0), inc="tv")
    for t in range(NT):
        for ec in range(NCH):
            for ns in range(2):
                P.add("tensor", lambda e, ec=ec, ns=ns, t=t: e.matmul(
                    ps_big[:, 512 * ns:512 * ns + 512],
                    sb_xlnT[:, S * ec + 128 * t:S * ec + 128 * t + 128],
                    sb_wv[:, E * ec + 512 * ns:E * ec + 512 * ns + 512],
                    start=(ec == 0), stop=False), inc="tt")
        for ns in range(2):
            P.add("tensor", lambda e, ns=ns: e.matmul(
                ps_big[:, 512 * ns:512 * ns + 512],
                sb_ones[0:1, 0:128],
                sb_brow[0:1, 2 * E + 512 * ns:2 * E + 512 * ns + 512],
                start=False, stop=True), inc="tt")

        def v_evac(e, t=t):
            dst = bass.AP(sb_v65, 1040 * t, [[NT * H * 65, 128], [65, H], [1, 64]])
            src = bass.AP(ps_big, 0, [[2048, 128], [64, H], [1, 64]])
            return e.tensor_copy(dst, src)
        P.add("vector", v_evac, waits=[("tt", P.at("tt"))], inc="tv")
        P.barrier("tensor", [("tv", P.at("tv"))])
    proj_tv = P.at("tv")
    tap("qT", lambda: sb_qT[:, 0:2048])
    tap("kT", lambda: sb_kT[:, 0:2048])
    tap("v65", lambda: sb_v65[:, 0:2048])
    M.free("sb_wv", "sb_xlnT")

    sb_etf = M.alloc("sb_etf", [128, ETW], BF16)
    sb_etr = M.alloc("sb_etr", [128, ETW], BF16)
    sb_stg = M.alloc("sb_stg", [128, 2 * TW], BF16)
    sb_A = M.alloc("sb_A", [128, 8 * S], BF16)
    sb_Ss = M.alloc("sb_Ss", [128, 2 * LC], F32)
    sb_B = M.alloc("sb_B", [128, 8 * LC], BF16)
    sb_X = M.alloc("sb_X", [128, 4 * LC], BF16)
    for half in range(2):
        P.add("gpsimd", lambda e, hf=half: e.dma_start(
            out=sb_etf[64 * hf:64 * hf + 64, :], in_=d_etf),
            waits=[("tv", proj_tv)] if half == 0 else [], inc="gd", amt=16)
        P.add("gpsimd", lambda e, hf=half: e.dma_start(
            out=sb_etr[64 * hf:64 * hf + 64, :], in_=d_etr), inc="gd", amt=16)
    gd_tbl = P.at("gd")

    sb_sq = sb_A

    P.add("vector", lambda e: e.tensor_copy(sb_edg[:, 0:1], sb_etf[:, 0:1]),
          waits=[("gd", gd_tbl)], inc="tv")
    P.add("vector", lambda e: e.tensor_copy(sb_edg[:, 1:2], sb_etf[:, ETW - 1:ETW]),
          inc="tv")

    P.add("vector", lambda e: e.tensor_mul(
        sb_sq[0:64, 0:2048], sb_etf[0:64, 0:2048], sb_etf[0:64, 0:2048]),
        waits=[("gd", gd_tbl), ("tt", P.at("tt"))], inc="tv")
    P.barrier("tensor", [("tv", P.at("tv"))])
    for ns in range(4):
        P.add("tensor", lambda e, ns=ns: e.matmul(
            ps_big[0:1, 512 * ns:512 * ns + 512], sb_ones[0:64, 0:1],
            sb_sq[0:64, 512 * ns:512 * ns + 512], start=True, stop=False), inc="tt")
    P.add("vector", lambda e: e.tensor_mul(
        sb_sq[0:64, 0:2048], sb_etf[0:64, 2048:4096], sb_etf[0:64, 2048:4096]),
        waits=[("tt", P.at("tt"))], inc="tv")
    P.barrier("tensor", [("tv", P.at("tv"))])
    for ns in range(4):
        P.add("tensor", lambda e, ns=ns: e.matmul(
            ps_big[0:1, 512 * ns:512 * ns + 512], sb_ones[0:64, 0:1],
            sb_sq[0:64, 512 * ns:512 * ns + 512], start=False, stop=True), inc="tt")
    P.add("vector", lambda e: e.reduce_max(
        sb_tmp[0:1, 0:1], ps_big[0:1, :], axis=AX),
        waits=[("tt", P.at("tt"))], inc="tv", hz=True)
    P.add("scalar", lambda e: e.activation(
        sb_tmp[0:1, 1:2], sb_tmp[0:1, 0:1], AF.Sqrt),
        waits=[("tv", P.at("tv"))], inc="ts")
    me_ts = P.at("ts")
    P.barrier("tensor", [("tv", P.at("tv"))])

    sq_parity = 0
    for h in range(H):
        ch, half = h // 2, h % 2
        qsl = slice(64 * half, 64 * half + 64)
        tpos = (64 * half, 0)
        for src, col in ((sb_qT, 2), (sb_kT, 3)):
            P.add("vector", lambda e, src=src, ch=ch, qsl=qsl: e.tensor_mul(
                sb_sq[qsl, 0:S], src[qsl, S * ch:S * ch + S],
                src[qsl, S * ch:S * ch + S]),
                waits=[("tv", proj_tv), ("tt", P.at("tt"))], inc="tv")
            P.barrier("tensor", [("tv", P.at("tv"))])
            for ns in range(4):
                P.add("tensor", lambda e, ns=ns, qsl=qsl, tpos=tpos: e.matmul(
                    ps_big[0:1, 512 * ns:512 * ns + 512], sb_ones[qsl, 0:1],
                    sb_sq[qsl, 512 * ns:512 * ns + 512], start=True, stop=True,
                    tile_position=tpos), inc="tt")
            P.add("vector", lambda e, col=col: e.reduce_max(
                sb_tmp[0:1, col:col + 1], ps_big[0:1, :], axis=AX),
                waits=[("tt", P.at("tt"))], inc="tv", hz=True)
            P.barrier("tensor", [("tv", P.at("tv"))])
        P.add("scalar", lambda e: e.activation(
            sb_tmp[0:1, 4:5], sb_tmp[0:1, 2:3], AF.Sqrt),
            waits=[("tv", P.at("tv")), ("ts", me_ts)], inc="ts")
        P.add("scalar", lambda e: e.activation(
            sb_tmp[0:1, 5:6], sb_tmp[0:1, 3:4], AF.Sqrt), inc="ts")
        P.add("vector", lambda e: e.tensor_mul(
            sb_tmp[0:1, 6:7], sb_tmp[0:1, 4:5], sb_tmp[0:1, 5:6]),
            waits=[("ts", P.at("ts"))], inc="tv")
        P.add("vector", lambda e: e.tensor_add(
            sb_tmp[0:1, 7:8], sb_tmp[0:1, 4:5], sb_tmp[0:1, 5:6]), inc="tv")
        P.add("vector", lambda e: e.tensor_mul(
            sb_tmp[0:1, 7:8], sb_tmp[0:1, 7:8], sb_tmp[0:1, 1:2]), inc="tv")
        P.add("vector", lambda e: e.tensor_add(
            sb_tmp[0:1, 6:7], sb_tmp[0:1, 6:7], sb_tmp[0:1, 7:8]), inc="tv")
        P.add("vector", lambda e: e.tensor_scalar(
            sb_nbh[0:1, 0:1], sb_tmp[0:1, 6:7], -1.02 / 8.0, None, ALU.mult),
            inc="tv")
        P.add("tensor", lambda e, h=h: e.matmul(
            ps_d[:, 256 + h:256 + h + 1], sb_ones[0:1, 0:128], sb_nbh[0:1, 0:1],
            start=True, stop=True), waits=[("tv", P.at("tv"))], inc="tt")
        P.add("vector", lambda e, h=h: e.tensor_copy(
            sb_negB[:, h:h + 1], ps_d[:, 256 + h:256 + h + 1]),
            waits=[("tt", P.at("tt"))], inc="tv")
        P.barrier("tensor", [("tv", P.at("tv"))])
    negb_tv = P.at("tv")
    if debug:
        tap("negB", lambda: sb_negB[:, 0:1])

    stg_order = [(0, t) for t in range(4)] + [(1, j) for j in range(NT)] \
        + [(0, t) for t in range(4, NT)]
    rot_par = [0]
    rot_ev = [("tv", P.at("tv")), ("tv", P.at("tv"))]
    stg_tile_idx = [0]
    stg_dma_hist = []
    stg_sd = {hh: {} for hh in range(H)}

    def emit_stage_tile(hs, kind, t):
        ch2, half2 = hs // 2, hs % 2
        qsl2 = slice(64 * half2, 64 * half2 + 64)
        tpos2 = (64 * half2, 0)
        src = sb_qT if kind == 0 else sb_kT
        tbl = sb_etr if kind == 0 else sb_etf
        dstD = d_qe if kind == 0 else d_ke
        voff = 1920 - 128 * t
        idx_t = stg_tile_idx[0]
        stg_tile_idx[0] += 1
        sl = idx_t % 2
        ev_last = {}
        for c in range(5):
            ncols = 512 if c < 4 else 128
            par = rot_par[0]
            rot_par[0] ^= 1
            P.add("tensor", lambda e, c=c, ncols=ncols, par=par, t=t,
                  src=src, tbl=tbl, ch2=ch2, qsl2=qsl2, tpos2=tpos2,
                  voff=voff: e.matmul(
                ps_big[:, 512 * par:512 * par + ncols],
                src[qsl2, S * ch2 + 128 * t:S * ch2 + 128 * t + 128],
                tbl[qsl2, voff + 512 * c:voff + 512 * c + ncols],
                start=True, stop=True, tile_position=tpos2),
                waits=[rot_ev[par]], inc="tt")
            tt_c = P.at("tt")
            eng, sem = ("vector", "tv") if c % 2 == 0 else ("scalar", "ts")
            we = [("tt", tt_c)]
            if c < 2 and idx_t >= 2:
                we.append(("sd", stg_dma_hist[idx_t - 2]))
            if eng == "vector":
                P.add(eng, lambda e, c=c, ncols=ncols, par=par, sl=sl:
                      e.tensor_copy(
                    sb_stg[:, TW * sl + 512 * c:TW * sl + 512 * c + ncols],
                    ps_big[:, 512 * par:512 * par + ncols]),
                    waits=we, inc=sem)
            else:
                P.add(eng, lambda e, c=c, ncols=ncols, par=par, sl=sl:
                      e.copy(
                    sb_stg[:, TW * sl + 512 * c:TW * sl + 512 * c + ncols],
                    ps_big[:, 512 * par:512 * par + ncols]),
                    waits=we, inc=sem)
            rot_ev[par] = (sem, P.at(sem))
            ev_last[sem] = P.at(sem)
        P.add("sync", lambda e, sl=sl, t=t, dstD=dstD, hs=hs: e.dma_start(
            out=dstD[hs, t], in_=sb_stg[:, TW * sl:TW * sl + TW]),
            waits=[(s, v) for s, v in ev_last.items()], inc="sd", amt=16)
        stg_sd[hs][(kind, t)] = P.at("sd")
        stg_dma_hist.append(P.at("sd"))

    def emit_edge_dots(hs):
        ch2, half2 = hs // 2, hs % 2
        qsl2 = slice(64 * half2, 64 * half2 + 64)
        tpos2 = (64 * half2, 0)
        for eid in range(2):
            for ns in range(4):
                par = rot_par[0]
                rot_par[0] ^= 1
                P.add("tensor", lambda e, ns=ns, ch2=ch2, qsl2=qsl2,
                      tpos2=tpos2, eid=eid, par=par: e.matmul(
                    ps_big[0:1, 512 * par:512 * par + 512],
                    sb_edg[qsl2, eid:eid + 1],
                    sb_qT[qsl2, S * ch2 + 512 * ns:S * ch2 + 512 * ns + 512],
                    start=True, stop=True, tile_position=tpos2),
                    waits=[rot_ev[par]], inc="tt")
                P.add("vector", lambda e, eid=eid, ns=ns, par=par:
                      e.tensor_copy(
                    sb_qe2[0:1, S * eid + 512 * ns:S * eid + 512 * ns + 512],
                    ps_big[0:1, 512 * par:512 * par + 512]),
                    waits=[("tt", P.at("tt"))], inc="tv")
                rot_ev[par] = ("tv", P.at("tv"))
        P.barrier("tensor", [("tv", P.at("tv"))])

    for kind, t in stg_order:
        emit_stage_tile(0, kind, t)

    for h in range(H):
        ch, half = h // 2, h % 2
        qsl = slice(64 * half, 64 * half + 64)
        tpos = (64 * half, 0)
        emit_edge_dots(h)

        exp_hist = {}
        stt_hist = {}
        slot_free = {}
        vb1_hist = {}
        big_bf5 = ps_big.bitcast(BF16)
        abf_slot = [big_bf5[:, 2048:2560],
                    big_bf5[:, 3072:3584]]

        def emit_a_gather(lc_t, i):
            half = lc_t % 2
            t = 4 * lc_t + i
            wA = [("sd", stg_sd[h][(0, t)])]
            if lc_t >= 2:
                wA.append(("tt", exp_hist[(lc_t - 2, NT - 1, "pv")]))
            else:
                wA.append(("tt", P.at("tt")))
            P.add("gpsimd", lambda e, t=t, i=i, h=h, half=half:
                  e.indirect_dma_start(
                sb_A[:, S * (4 * half + i):S * (4 * half + i) + S], None,
                d_qe_flat, IndirectOffsetOnAxis(sb_idxA[:], 1),
                element_offset=(h * NT + t) * 128 * TW),
                waits=wA, inc="gd", amt=16)

        for i in range(4):
            emit_a_gather(0, i)
        ga_hist = {0: P.at("gd")}
        for lc in range(NLC):
            half = lc % 2
            slot_free[(lc, 0)] = ("tv", P.at("tv"))
            slot_free[(lc, 1)] = ("tv", P.at("tv"))
            slot_free.pop(0, None)
            slot_free.pop(1, None)
            if h == 0 and lc == 0 and debug:
                tap("A", lambda: sb_A[:, 0:2048], wait_sem="gd")
            for j in range(NT):
                pj = [ps_a[:, :], ps_b[:, :]][j % 2]
                sl = j % 2
                wb = [("sd", stg_sd[h][(1, j)])]
                if (lc, j - 8) in vb1_hist:
                    wb.append(("tv", vb1_hist[(lc, j - 8)]))
                elif lc > 0 and (lc - 1, 8 + j) in vb1_hist:
                    wb.append(("tv", vb1_hist[(lc - 1, 8 + j)]))
                else:
                    wb.append(("tv", P.at("tv")))
                P.add("gpsimd", lambda e, j=j, lc=lc, h=h: e.indirect_dma_start(
                    sb_B[:, LC * (j % 8):LC * (j % 8) + LC], None,
                    d_ke_flat, IndirectOffsetOnAxis(sb_idxB[:], 1),
                    element_offset=(h * NT + j) * 128 * TW + LC * lc),
                    waits=wb, inc="gd", amt=16)
                gb_done = P.at("gd")
                if lc + 1 < NLC and j in (8, 10, 12, 14):
                    emit_a_gather(lc + 1, (j - 8) // 2)
                    if j == 14:
                        ga_hist[lc + 1] = P.at("gd")
                edg_runs = []
                tr_list = []
                for i in range(4):
                    t_l = 4 * lc + i
                    if t_l - j <= -9:
                        eid = 0
                    elif t_l - j >= 9:
                        eid = 1
                    else:
                        tr_list.append(i)
                        continue
                    if edg_runs and edg_runs[-1][0] == eid and edg_runs[-1][2] == i:
                        edg_runs[-1] = (eid, edg_runs[-1][1], i + 1)
                    else:
                        edg_runs.append((eid, i, i + 1))
                wt = []
                if (lc, j - 2) in stt_hist:
                    wt.append(("tv", stt_hist[(lc, j - 2)]))
                elif lc > 0 and (lc - 1, 14 + j) in stt_hist:
                    wt.append(("tv", stt_hist[(lc - 1, 14 + j)]))
                else:
                    wt.append(("tv", P.at("tv")))
                P.add("tensor", lambda e, j=j, lc=lc, ch=ch, qsl=qsl,
                      tpos=tpos, p=pj, st=(not edg_runs): e.matmul(
                    p,
                    sb_kT[qsl, S * ch + 128 * j:S * ch + 128 * j + 128],
                    sb_qT[qsl, S * ch + LC * lc:S * ch + LC * lc + LC],
                    start=True, stop=st, tile_position=tpos),
                    waits=wt, inc="tt")
                for ei, (eid, i0, i1) in enumerate(edg_runs):
                    P.add("tensor", lambda e, p=pj, eid=eid, i0=i0, i1=i1,
                          lc=lc, st=(ei == len(edg_runs) - 1): e.matmul(
                        p[:, 128 * i0:128 * i1],
                        sb_ones[0:1, 0:128],
                        sb_qe2[0:1, S * eid + LC * lc + 128 * i0:
                               S * eid + LC * lc + 128 * i1],
                        start=False, stop=st), inc="tt")
                tt_scores = P.at("tt")
                wtr = [("gd", ga_hist[lc])]
                sf = slot_free.get((lc, sl)) if j < 2 else slot_free.get(sl)
                if sf is None:
                    sf = ("tv", P.at("tv"))
                wtr.append(sf)
                first_tr = True
                for i in tr_list:
                    P.add("tensor", lambda e, i=i, j=j, half=half, sl=sl:
                          e.transpose(
                        abf_slot[sl][:, 128 * i:128 * i + 128],
                        sb_A[:, S * (4 * half + i) + 128 * j:
                             S * (4 * half + i) + 128 * j + 128],
                        sb_iden[:, :]),
                        waits=(wtr if first_tr else []), inc="tt")
                    first_tr = False
                tt_atr = P.at("tt")
                wexp_extra = []
                if (lc, j - 2, "exp") in exp_hist:
                    wexp_extra = [("ts", exp_hist[(lc, j - 2, "exp")])]
                elif lc > 0:
                    wexp_extra = [("ts", exp_hist[(lc - 1, 14 + j, "exp")])]
                if tr_list:
                    P.add("vector", lambda e, j=j, sl=sl: e.tensor_tensor(
                        sb_Ss[:, LC * sl:LC * sl + LC], abf_slot[sl][:, 0:LC],
                        sb_B[:, LC * (j % 8):LC * (j % 8) + LC], ALU.add),
                        waits=[("tt", tt_atr), ("gd", gb_done)] + wexp_extra,
                        inc="tv")
                    vb1_hist[(lc, j)] = P.at("tv")
                    slot_free[sl] = ("tv", P.at("tv"))
                    P.add("vector", lambda e, p=pj, sl=sl: e.scalar_tensor_tensor(
                        sb_Ss[:, LC * sl:LC * sl + LC], p, 1.0,
                        sb_Ss[:, LC * sl:LC * sl + LC], ALU.mult, ALU.add),
                        waits=[("tt", tt_scores)], inc="tv")
                else:
                    P.add("vector", lambda e, p=pj, j=j, sl=sl:
                          e.scalar_tensor_tensor(
                        sb_Ss[:, LC * sl:LC * sl + LC], p, 1.0,
                        sb_B[:, LC * (j % 8):LC * (j % 8) + LC],
                        ALU.mult, ALU.add),
                        waits=[("tt", tt_scores), ("gd", gb_done)] + wexp_extra,
                        inc="tv")
                    vb1_hist[(lc, j)] = P.at("tv")
                stt_hist[(lc, j)] = P.at("tv")
                if debug and h == 0 and lc == 0 and j == 0:
                    tap("Bt", lambda: sb_B[:, 0:LC], wait_sem="gd")
                    tap("scr", lambda: sb_Ss[:, 0:LC], wait_sem="tv")
                P.add("scalar", lambda e, sl=sl, x4=j % 4, h=h: e.activation(
                    sb_X[:, LC * x4:LC * x4 + LC],
                    sb_Ss[:, LC * sl:LC * sl + LC], AF.Exp,
                    bias=sb_negB[:, h:h + 1], scale=0.125),
                    waits=[("tv", stt_hist[(lc, j)]), ("tv", negb_tv)], inc="ts")
                exp_hist[(lc, j, "exp")] = P.at("ts")
                if debug and h == 0 and lc == 0 and j == 0:
                    tap("X", lambda: sb_X[:, 0:LC], wait_sem="ts")

                def emit_pv(jp):
                    wpv = [("ts", exp_hist[(lc, jp, "exp")])]
                    if jp == 0:
                        wpv.append(("tv", P.at("tv")))
                    P.add("tensor", lambda e, jp=jp, h=h: e.matmul(
                        ps_c[0:65, :],
                        sb_v65[:, 1040 * jp + 65 * h:1040 * jp + 65 * h + 65],
                        sb_X[:, LC * (jp % 4):LC * (jp % 4) + LC],
                        start=(jp == 0), stop=(jp == NT - 1)),
                        waits=wpv, inc="tt")
                    exp_hist[(lc, jp, "pv")] = P.at("tt")
                if j >= 2:
                    emit_pv(j - 2)
                gi = lc * NT + j
                if h + 1 < H and gi % 2 == 0 and gi // 2 < len(stg_order):
                    kind2, t2 = stg_order[gi // 2]
                    emit_stage_tile(h + 1, kind2, t2)
                if j == NT - 1:
                    emit_pv(NT - 2)
                    emit_pv(NT - 1)
            P.add("vector", lambda e: e.tensor_copy(
                sb_ctv[0:65, :], ps_c[0:65, :]),
                waits=[("tt", P.at("tt")), ("sd", P.at("sd"))], inc="tv")
            for i in range(4):
                wtr2 = [("tv", P.at("tv"))] if i == 0 else []
                P.add("tensor", lambda e, i=i: e.transpose(
                    ps_d[:, 65 * i:65 * i + 65],
                    sb_ctv[0:65, 128 * i:128 * i + 128],
                    sb_idenf[0:65, 0:65]), waits=wtr2, inc="tt")
            P.add("vector", lambda e: e.reciprocal(
                sb_zr[:, 0:4], bass.AP(ps_d, 64, [[512, 128], [65, 4]])),
                waits=[("tt", P.at("tt"))], inc="tv", hz=True)
            P.add("vector", lambda e: e.tensor_tensor(
                bass.AP(sb_ctp, 0, [[256, 128], [64, 4], [1, 64]]),
                bass.AP(ps_d, 0, [[512, 128], [65, 4], [1, 64]]),
                bass.AP(sb_zr, 0, [[4, 128], [1, 4], [0, 64]]),
                ALU.mult), inc="tv")
            def ctx_store(e, h=h, lc=lc):
                dst = bass.AP(d_ctx.tensor, h * S * D + LC * lc * D,
                              [[D, 128], [128 * D, 4], [1, D]])
                src = bass.AP(sb_ctp, 0, [[4 * D, 128], [D, 4], [1, D]])
                return e.dma_start(out=dst, in_=src)
            P.add("sync", ctx_store, waits=[("tv", P.at("tv"))], inc="sd", amt=16)
    attn_sd = P.at("sd")

    M.free("sb_X", "sb_B", "sb_Ss", "sb_A", "sb_stg", "sb_etr",
           "sb_etf", "sb_v65", "sb_kT", "sb_qT")
    sb_xpa = M.alloc("sb_xpa", [128, NT * E], F32)
    sb_xd = M.alloc("sb_xd", [128, NT * E], BF16)
    sb_wd = M.alloc("sb_wd", [128, NCH * E], BF16)
    sb_ctl = M.alloc("sb_ctl", [128, E], F32)
    sb_ffw = M.alloc("sb_ffw", [128, E], F32)
    sb_q1 = M.alloc("sb_q1", [128, E], F32)
    sb_q2 = M.alloc("sb_q2", [128, E], F32)
    sb_q3 = M.alloc("sb_q3", [128, E], F32)
    sb_sc = M.alloc("sb_sc", [128, 32], F32)
    sb_scr = M.alloc("sb_scr", [128, 32], F32)

    def emit_quant(dst_ap_fn, src_sb):
        def a3(sb, off=0):
            return bass.AP(sb, off, [[E, 128], [32, 32], [1, 32]])

        def scbc(sb):
            return bass.AP(sb, 0, [[32, 128], [1, 32], [0, 32]])
        P.add("vector", lambda e: e.tensor_reduce(
            sb_sc[:], a3(src_sb), axis=AX, op=ALU.max,
            apply_absolute_value=True), inc="tv", hz=True)
        P.add("vector", lambda e: e.tensor_scalar(
            sb_sc[:], sb_sc[:], 1.0 / 7.5, 1e-30, ALU.mult, ALU.max), inc="tv")
        P.add("vector", lambda e: e.reciprocal(sb_scr[:], sb_sc[:]), inc="tv", hz=True)
        P.add("vector", lambda e: e.tensor_tensor(
            a3(sb_q1), a3(src_sb), scbc(sb_scr), ALU.mult), inc="tv")
        P.add("vector", lambda e: e.tensor_mul(sb_q2[:], sb_q1[:], sb_q1[:]), inc="tv")
        P.add("vector", lambda e: e.tensor_scalar(
            sb_q3[:], sb_q2[:], 4.0, None, ALU.is_ge), inc="tv")
        P.add("vector", lambda e: e.tensor_scalar(
            sb_q2[:], sb_q2[:], 16.0, None, ALU.is_ge), inc="tv")
        P.add("vector", lambda e: e.tensor_scalar(
            sb_dump[:], sb_q3[:], 0.125, 0.125, ALU.mult, ALU.add), inc="tv")
        P.add("vector", lambda e: e.scalar_tensor_tensor(
            sb_dump[:], sb_q2[:], 0.25, sb_dump[:], ALU.mult, ALU.add), inc="tv")
        P.add("vector", lambda e: e.tensor_scalar(
            sb_q3[:], sb_q3[:], -4.0, 8.0, ALU.mult, ALU.add), inc="tv")
        P.add("vector", lambda e: e.scalar_tensor_tensor(
            sb_q3[:], sb_q2[:], -2.0, sb_q3[:], ALU.mult, ALU.add), inc="tv")
        P.add("vector", lambda e: e.tensor_mul(sb_q1[:], sb_q1[:], sb_q3[:]), inc="tv")
        P.add("vector", lambda e: e.tensor_copy(
            sb_q2.bitcast(I32)[:], sb_q1[:]), inc="tv")
        P.add("vector", lambda e: e.tensor_copy(
            sb_q1[:], sb_q2.bitcast(I32)[:]), inc="tv")
        P.add("vector", lambda e: e.tensor_mul(sb_q1[:], sb_q1[:], sb_dump[:]), inc="tv")
        P.add("vector", lambda e: e.tensor_tensor(
            dst_ap_fn(), a3(sb_q1), scbc(sb_sc), ALU.mult), inc="tv")

    for t in range(NT):
        buf = sb_xt if t % 2 == 0 else sb_xt2
        w = [("tv", P.at("tv"))] if t >= 2 else []
        P.add("sync", lambda e, t=t, b=buf: e.dma_start(
            out=b[:], in_=d_x[128 * t:128 * t + 128, :]), waits=w, inc="sd", amt=16)

        def ctx_ld(e, t=t):
            src = bass.AP(d_ctx.tensor, 128 * t * D, [[D, 128], [S * D, H], [1, D]])
            dst = bass.AP(sb_ctl, 0, [[E, 128], [D, H], [1, D]])
            return e.dma_start(out=dst, in_=src)
        P.add("sync", ctx_ld, waits=[("sd", attn_sd)], inc="sd", amt=16)
        P.add("vector", lambda e, b=buf: e.tensor_add(b[:], b[:], sb_ctl[:]),
              waits=[("sd", P.at("sd"))], inc="tv")
        P.add("vector", lambda e, t=t, b=buf: e.tensor_copy(
            sb_xpa[:, E * t:E * t + E], b[:]), inc="tv")
        emit_ln(lambda: sb_q3[:], buf)
        P.add("vector", lambda e: e.tensor_copy(sb_ffw[:], sb_q3[:]), inc="tv")
        emit_quant(lambda t=t: bass.AP(sb_xd, E * t, [[NT * E, 128], [32, 32], [1, 32]]),
                   sb_ffw)
        if t == 0:
            tap("xpa", lambda: sb_xpa[:, 0:E])
            tap("xd", lambda: sb_xd[:, 0:E])

    for fc in range(NCH):
        w = [("tv", P.at("tv"))] if fc >= 1 else [("tv", P.at("tv"))]
        P.add("sync", lambda e, fc=fc: e.dma_start(
            out=sb_ffw[:], in_=d_ffw[128 * fc:128 * fc + 128, :]),
            waits=w, inc="sd", amt=16)
        P.barrier("vector", [("sd", P.at("sd"))])
        emit_quant(lambda fc=fc: bass.AP(sb_wd, E * fc, [[NCH * E, 128], [32, 32], [1, 32]]),
                   sb_ffw)
        if fc == 0:
            tap("wd", lambda: sb_wd[:, 0:E])
    quant_tv = P.at("tv")

    M.free("sb_scr", "sb_sc", "sb_q3", "sb_q2", "sb_q1", "sb_ffw", "sb_ctl")
    sb_xdT = M.alloc("sb_xdT", [128, NCH * S], BF16)
    sb_wdT = M.alloc("sb_wdT", [128, NCH * E], BF16)
    sb_yt = M.alloc("sb_yt", [128, E], F32)

    tp_i = 0
    P.barrier("tensor", [("tv", quant_tv)])
    for t in range(NT):
        for kc in range(NCH):
            pb = psA_bf if tp_i % 2 == 0 else psB_bf
            w = [("tv", P.at("tv"))] if tp_i >= 2 else []
            P.add("tensor", lambda e, t=t, kc=kc, p=pb: e.transpose(
                p[:, 0:128], sb_xd[:, E * t + 128 * kc:E * t + 128 * kc + 128],
                sb_iden[:]), waits=w, inc="tt")
            P.add("vector", lambda e, t=t, kc=kc, p=pb: e.tensor_copy(
                sb_xdT[:, S * kc + 128 * t:S * kc + 128 * t + 128], p[:, 0:128]),
                waits=[("tt", P.at("tt"))], inc="tv")
            tp_i += 1
    for fc in range(NCH):
        for kc in range(NCH):
            pb = psA_bf if tp_i % 2 == 0 else psB_bf
            P.add("tensor", lambda e, fc=fc, kc=kc, p=pb: e.transpose(
                p[:, 0:128], sb_wd[:, E * fc + 128 * kc:E * fc + 128 * kc + 128],
                sb_iden[:]), waits=[("tv", P.at("tv"))], inc="tt")
            P.add("vector", lambda e, fc=fc, kc=kc, p=pb: e.tensor_copy(
                sb_wdT[:, E * kc + 128 * fc:E * kc + 128 * fc + 128], p[:, 0:128]),
                waits=[("tt", P.at("tt"))], inc="tv")
            tp_i += 1
    trans_tv = P.at("tv")

    P.barrier("tensor", [("tv", trans_tv)])
    y_sd = {}
    for t in range(NT):
        if t >= 1:
            P.barrier("tensor", [("tv", P.at("tv"))])
        for kc in range(NCH):
            for ns in range(2):
                P.add("tensor", lambda e, t=t, kc=kc, ns=ns: e.matmul(
                    ps_big[:, 512 * ns:512 * ns + 512],
                    sb_xdT[:, S * kc + 128 * t:S * kc + 128 * t + 128],
                    sb_wdT[:, E * kc + 512 * ns:E * kc + 512 * ns + 512],
                    start=(kc == 0), stop=False), inc="tt")
        for ns in range(2):
            P.add("tensor", lambda e, ns=ns: e.matmul(
                ps_big[:, 512 * ns:512 * ns + 512],
                sb_ones[0:1, 0:128], sb_ffb[0:1, 512 * ns:512 * ns + 512],
                start=False, stop=True), inc="tt")
        wv = [("tt", P.at("tt"))]
        if t >= 1:
            wv.append(("sd", y_sd[t - 1]))
        P.add("vector", lambda e, t=t: e.tensor_add(
            sb_yt[:, 0:E], ps_big[:, 0:1024],
            sb_xpa[:, E * t:E * t + E]), waits=wv, inc="tv")
        P.add("sync", lambda e, t=t: e.dma_start(
            out=d_y[128 * t:128 * t + 128, :], in_=sb_yt[:, 0:E]),
            waits=[("tv", P.at("tv"))], inc="sd", amt=16)
        y_sd[t] = P.at("sd")

    P.add("sync", lambda e: None, waits=[("sd", P.at("sd"))])

    def replay(name, eng):
        for fn, waits, inc, amt in P.items[name]:
            for s_, v in waits:
                eng.wait_ge(sem_h[s_], v)
            inst = fn(eng)
            if inc is not None:
                inst.then_inc(sem_h[inc], amt)

    @block.sync
    def _(sync):
        replay("sync", sync)

    @block.vector
    def _(vector):
        replay("vector", vector)

    @block.scalar
    def _(scalar):
        replay("scalar", scalar)

    @block.tensor
    def _(tensor):
        replay("tensor", tensor)

    @block.gpsimd
    def _(gpsimd):
        replay("gpsimd", gpsimd)

    block_cm.__exit__(None, None, None)
    M.free_all()
    for cm in reversed(psum_cms):
        cm.__exit__(None, None, None)
    for cm in reversed(sem_cms):
        cm.__exit__(None, None, None)
    return nc


_NC_CACHE = [None]


def _host_prep(inputs):
    x = np.ascontiguousarray(np.asarray(inputs["L0_attn_hidden_states"], dtype=np.float32))
    Et = np.asarray(inputs["L0_attn_dist_emb"], dtype=np.float32)
    E_ext = np.concatenate([
        np.broadcast_to(Et[0:1], (1024, D)), Et,
        np.broadcast_to(Et[-1:], (1025, D))], axis=0)
    p = np.arange(128, dtype=np.int32).reshape(128, 1)
    base = {
        "wqT": np.ascontiguousarray(np.asarray(inputs["L0_attn_wq"], np.float32).T),
        "wkT": np.ascontiguousarray(np.asarray(inputs["L0_attn_wk"], np.float32).T),
        "wvT": np.ascontiguousarray(np.asarray(inputs["L0_attn_wv"], np.float32).T),
        "bq": np.ascontiguousarray(np.asarray(inputs["L0_attn_bq"], np.float32).reshape(1, E)),
        "bk": np.ascontiguousarray(np.asarray(inputs["L0_attn_bk"], np.float32).reshape(1, E)),
        "bv": np.ascontiguousarray(np.asarray(inputs["L0_attn_bv"], np.float32).reshape(1, E)),
        "etf": np.ascontiguousarray(E_ext.T),
        "etr": np.ascontiguousarray(E_ext[::-1].T),
        "ffw": np.ascontiguousarray(np.asarray(inputs["L0_ffn_weight"], np.float32)),
        "ffb": np.ascontiguousarray(np.asarray(inputs["L0_ffn_bias"], np.float32).reshape(1, E)),
        "idxA": (TW - 1) * p + 128,
        "idxB": (TW - 1) * p + 127,
        "idn": np.ascontiguousarray(np.eye(128, dtype=np.float32)),
    }
    return x, base


def debug_run(inputs, core=0):
    x, base = _host_prep(inputs)
    nc = build_program(debug=True)
    im = dict(base)
    im["x"] = np.ascontiguousarray(x[core])
    res = run_bass_kernel_spmd(nc, [im], core_ids=[0])
    return res.results[0]


def kernel(L0_attn_hidden_states, L0_attn_wq, L0_attn_bq, L0_attn_wk,
           L0_attn_bk, L0_attn_wv, L0_attn_bv, L0_attn_dist_emb,
           L0_ffn_weight, L0_ffn_bias):
    x = np.ascontiguousarray(np.asarray(L0_attn_hidden_states, dtype=np.float32))
    wq = np.asarray(L0_attn_wq, dtype=np.float32)
    wk = np.asarray(L0_attn_wk, dtype=np.float32)
    wv = np.asarray(L0_attn_wv, dtype=np.float32)
    bq = np.ascontiguousarray(np.asarray(L0_attn_bq, dtype=np.float32).reshape(1, E))
    bk = np.ascontiguousarray(np.asarray(L0_attn_bk, dtype=np.float32).reshape(1, E))
    bv = np.ascontiguousarray(np.asarray(L0_attn_bv, dtype=np.float32).reshape(1, E))
    Et = np.asarray(L0_attn_dist_emb, dtype=np.float32)
    ffw = np.ascontiguousarray(np.asarray(L0_ffn_weight, dtype=np.float32))
    ffb = np.ascontiguousarray(np.asarray(L0_ffn_bias, dtype=np.float32).reshape(1, E))

    E_ext = np.concatenate([
        np.broadcast_to(Et[0:1], (1024, D)),
        Et,
        np.broadcast_to(Et[-1:], (1025, D)),
    ], axis=0)
    etf = np.ascontiguousarray(E_ext.T)
    etr = np.ascontiguousarray(E_ext[::-1].T)

    wqT = np.ascontiguousarray(wq.T)
    wkT = np.ascontiguousarray(wk.T)
    wvT = np.ascontiguousarray(wv.T)

    p = np.arange(128, dtype=np.int32).reshape(128, 1)
    idxA = (TW - 1) * p + 128
    idxB = (TW - 1) * p + 127
    iden = np.ascontiguousarray(np.eye(128, dtype=np.float32))

    if _NC_CACHE[0] is None:
        _NC_CACHE[0] = build_program()
    nc = _NC_CACHE[0]

    in_maps = []
    for c in range(8):
        in_maps.append({
            "x": np.ascontiguousarray(x[c]),
            "wqT": wqT, "wkT": wkT, "wvT": wvT,
            "bq": bq, "bk": bk, "bv": bv,
            "etf": etf, "etr": etr,
            "ffw": ffw, "ffb": ffb,
            "idxA": idxA, "idxB": idxB, "idn": iden,
        })
    res = run_bass_kernel_spmd(nc, in_maps, core_ids=list(range(8)))
    return np.stack([r["y"] for r in res.results], axis=0)


# version 21
# speedup vs baseline: 1.1134x; 1.0991x over previous
import numpy as np
import concourse.bass as bass
import concourse.mybir as mybir
from concourse.bass_utils import run_bass_kernel_spmd
from concourse.bass_core_rust import IndirectOffsetOnAxis

F32 = mybir.dt.float32
BF16 = mybir.dt.bfloat16
I32 = mybir.dt.int32
AF = mybir.ActivationFunctionType
ALU = mybir.AluOpType
AX = mybir.AxisListType.X

S = 2048
E = 1024
H = 16
D = 64
NT = S // 128
NCH = E // 128
TW = 2176
ETW = 4096
LC = 512
NLC = S // NLC if False else S // LC


class Prog:
    ENGS = ("sync", "vector", "scalar", "tensor", "gpsimd")

    def __init__(self):
        self.items = {e: [] for e in self.ENGS}
        self.counts = {}
        self.pending = {e: [] for e in self.ENGS}
        self.hazard = {}

    def sem(self, name):
        self.counts[name] = 0

    FLUSH = {"vector": "tv", "scalar": "ts"}

    def add(self, eng, fn, waits=(), inc=None, amt=1, hz=False):
        w = self.pending[eng] + list(waits)
        self.pending[eng] = []
        fs = self.FLUSH.get(eng)
        if fs is not None and (hz or self.hazard.get(eng, False)) \
                and self.counts.get(fs, 0) > 0:
            w.append((fs, self.counts[fs]))
        self.hazard[eng] = hz
        self.items[eng].append((fn, w, inc, amt))
        if inc is not None:
            self.counts[inc] += amt
        return self.counts.get(inc)

    def barrier(self, eng, waits):
        self.pending[eng].extend(waits)

    def at(self, sem):
        return self.counts[sem]


class Mem:

    def __init__(self, nc):
        self.nc = nc
        self.live = {}

    def alloc(self, name, shape, dt, side=None):
        if side is None:
            cm = self.nc.sbuf_tensor(name, shape, dt)
        else:
            cm = self.nc.sbuf_tensor(name, shape, dt, side=side)
        h = cm.__enter__()
        self.live[name] = (cm, h)
        return h

    def free(self, *names):
        for n in names:
            cm, _ = self.live.pop(n)
            cm.__exit__(None, None, None)

    def free_all(self):
        for n in reversed(list(self.live)):
            self.free(n)


DEBUG = False


def build_program(debug=False):
    nc = bass.Bass("TRN2", target_bir_lowering=False, debug=False)

    d_x = nc.dram_tensor("x", [S, E], F32, kind="ExternalInput").ap()
    d_wqT = nc.dram_tensor("wqT", [E, E], F32, kind="ExternalInput").ap()
    d_wkT = nc.dram_tensor("wkT", [E, E], F32, kind="ExternalInput").ap()
    d_wvT = nc.dram_tensor("wvT", [E, E], F32, kind="ExternalInput").ap()
    d_bq = nc.dram_tensor("bq", [1, E], F32, kind="ExternalInput").ap()
    d_bk = nc.dram_tensor("bk", [1, E], F32, kind="ExternalInput").ap()
    d_bv = nc.dram_tensor("bv", [1, E], F32, kind="ExternalInput").ap()
    d_etf = nc.dram_tensor("etf", [D, ETW], F32, kind="ExternalInput").ap()
    d_etr = nc.dram_tensor("etr", [D, ETW], F32, kind="ExternalInput").ap()
    d_ffw = nc.dram_tensor("ffw", [E, E], F32, kind="ExternalInput").ap()
    d_ffb = nc.dram_tensor("ffb", [1, E], F32, kind="ExternalInput").ap()
    d_idxA = nc.dram_tensor("idxA", [128, 1], I32, kind="ExternalInput").ap()
    d_idxB = nc.dram_tensor("idxB", [128, 1], I32, kind="ExternalInput").ap()
    d_iden = nc.dram_tensor("idn", [128, 128], F32, kind="ExternalInput").ap()
    d_y = nc.dram_tensor("y", [S, E], F32, kind="ExternalOutput").ap()

    ikind = "ExternalOutput" if debug else "Internal"
    d_qe = nc.dram_tensor("qe_st", [H, NT, 128, TW], BF16, kind=ikind).ap()
    d_ke = nc.dram_tensor("ke_st", [H, NT, 128, TW], BF16, kind=ikind).ap()
    d_qe_flat = bass.AP(d_qe.tensor, 0, [[TW, H * NT * 128], [1, TW]])
    d_ke_flat = bass.AP(d_ke.tensor, 0, [[TW, H * NT * 128], [1, TW]])
    d_ctx = nc.dram_tensor("ctx_st", [H, S, D], F32, kind=ikind).ap()
    d_dbg = {}
    if debug:
        for nm, dt_ in [("xln", BF16), ("xlnT", BF16), ("qT", BF16),
                        ("kT", BF16), ("v65", BF16), ("negB", F32),
                        ("A", BF16), ("Bt", BF16), ("scr", F32), ("X", BF16),
                        ("xpa", F32), ("xd", BF16), ("wd", BF16),
                        ("wq0", BF16), ("brow", BF16), ("qTe", BF16)]:
            d_dbg[nm] = nc.dram_tensor("dbg_" + nm, [128, 2048], dt_,
                                       kind="ExternalOutput").ap()

    M = Mem(nc)
    sb_ffb = M.alloc("sb_ffb", [1, E], BF16)
    sb_iden = M.alloc("sb_iden", [128, 128], BF16)
    sb_idenf = M.alloc("sb_idenf", [128, 128], F32)
    sb_ones = M.alloc("sb_ones", [128, 512], BF16)
    sb_idxA = M.alloc("sb_idxA", [128, 1], I32)
    sb_idxB = M.alloc("sb_idxB", [128, 1], I32)
    sb_tmp = M.alloc("sb_tmp", [1, 8], F32)
    sb_mv = M.alloc("sb_mv", [128, 4], F32)
    sb_zr = M.alloc("sb_zr", [128, 4], F32)
    sb_negB = M.alloc("sb_negB", [128, H], F32)
    sb_nbh = M.alloc("sb_nbh", [1, 2], BF16)
    sb_ctv = M.alloc("sb_ctv", [128, 512], F32)
    sb_ctp = M.alloc("sb_ctp", [128, 4 * D], F32)
    sb_edg = M.alloc("sb_edg", [128, 2], BF16)
    sb_qe2 = M.alloc("sb_qe2", [1, 2 * S], BF16)

    psum_cms = []

    def psum(name, shape, dt):
        cm = nc.psum_tensor(name, shape, dt)
        h = cm.__enter__()
        psum_cms.append(cm)
        return h

    ps_a = psum("ps_a", [128, 512], F32)
    ps_b = psum("ps_b", [128, 512], F32)
    ps_c = psum("ps_c", [128, 512], F32)
    ps_d = psum("ps_d", [128, 512], F32)
    ps_big = psum("ps_big", [128, 2048], F32)
    psA_bf = ps_a.bitcast(BF16)
    psB_bf = ps_b.bitcast(BF16)

    sem_cms = []

    def mksem(name):
        cm = nc.semaphore(name)
        h = cm.__enter__()
        sem_cms.append(cm)
        return h

    sems = ["sd", "gd", "tv", "ts", "tt"]
    sem_h = {s: mksem(s) for s in sems}

    block_cm = nc.Block()
    block = block_cm.__enter__()

    P = Prog()
    for s in sems:
        P.sem(s)

    def tap(nm, src_fn, wait_sem="tv"):
        if not debug:
            return
        P.add("sync", lambda e: e.dma_start(out=d_dbg[nm][0:src_fn().shape[0],
                                                          0:src_fn().shape[1]],
                                            in_=src_fn(),
                                            allow_slow_non_contiguous=True),
              waits=[(wait_sem, P.at(wait_sem))], inc="sd", amt=16)

    sb_xt = M.alloc("sb_xt", [128, E], F32, side="right")
    sb_xt2 = M.alloc("sb_xt2", [128, E], F32, side="right")
    sb_xlnT = M.alloc("sb_xlnT", [128, NCH * S], BF16, side="right")
    sb_brow = M.alloc("sb_brow", [1, 3 * E], BF16, side="right")
    sb_wv = M.alloc("sb_wv", [128, NCH * E], BF16, side="right")
    sb_wk = M.alloc("sb_wk", [128, NCH * E], BF16, side="right")
    sb_wq = M.alloc("sb_wq", [128, NCH * E], BF16, side="right")
    sb_xln = M.alloc("sb_xln", [128, NT * E], BF16, side="right")

    P.add("sync", lambda e: e.dma_start(out=sb_idxA[:], in_=d_idxA), inc="sd", amt=16)
    P.add("sync", lambda e: e.dma_start(out=sb_idxB[:], in_=d_idxB), inc="sd", amt=16)
    P.add("sync", lambda e: e.dma_start(out=sb_idenf[:], in_=d_iden), inc="sd", amt=16)
    P.add("gpsimd", lambda e: e.dma_start(out=sb_iden[:], in_=d_iden), inc="gd", amt=16)
    for ch in range(NCH):
        for wsb, wd_ in ((sb_wq, d_wqT), (sb_wk, d_wkT), (sb_wv, d_wvT)):
            P.add("gpsimd", lambda e, c=ch, wsb=wsb, wd_=wd_: e.dma_start(
                out=wsb[:, E * c:E * c + E], in_=wd_[128 * c:128 * c + 128, :]),
                inc="gd", amt=16)
    P.add("gpsimd", lambda e: e.dma_start(out=sb_brow[:, 0:E], in_=d_bq), inc="gd", amt=16)
    P.add("gpsimd", lambda e: e.dma_start(out=sb_brow[:, E:2 * E], in_=d_bk), inc="gd", amt=16)
    P.add("gpsimd", lambda e: e.dma_start(out=sb_brow[:, 2 * E:3 * E], in_=d_bv), inc="gd", amt=16)
    P.add("gpsimd", lambda e: e.dma_start(out=sb_ffb[:], in_=d_ffb), inc="gd", amt=16)
    gd_loads = P.at("gd")

    P.add("vector", lambda e: e.memset(sb_ones[:], 1.0), inc="tv")

    def emit_ln(dst_ap_fn, src_sb, first_waits=()):
        P.add("vector", lambda e: e.reduce_sum(
            sb_mv[:, 0:1], src_sb[:], axis=AX), waits=first_waits, inc="tv", hz=True)
        P.add("vector", lambda e: e.tensor_scalar(
            sb_mv[:, 0:1], sb_mv[:, 0:1], 1.0 / E, None, ALU.mult), inc="tv")
        P.add("scalar", lambda e: e.activation(
            ps_big[:, 0:1024], src_sb[:], AF.Square, accum_out=sb_mv[:, 1:2]),
            waits=[("tv", P.at("tv"))], inc="ts")
        P.add("vector", lambda e: e.tensor_mul(
            sb_mv[:, 2:3], sb_mv[:, 0:1], sb_mv[:, 0:1]),
            waits=[("ts", P.at("ts"))], inc="tv")
        P.add("vector", lambda e: e.tensor_scalar(
            sb_mv[:, 1:2], sb_mv[:, 1:2], 1.0 / E, None, ALU.mult), inc="tv")
        P.add("vector", lambda e: e.tensor_sub(
            sb_mv[:, 1:2], sb_mv[:, 1:2], sb_mv[:, 2:3]), inc="tv")
        P.add("vector", lambda e: e.tensor_scalar(
            sb_mv[:, 1:2], sb_mv[:, 1:2], 1e-5, None, ALU.add), inc="tv")
        P.add("scalar", lambda e: e.activation(
            sb_mv[:, 3:4], sb_mv[:, 1:2], AF.Sqrt),
            waits=[("tv", P.at("tv"))], inc="ts")
        P.add("vector", lambda e: e.reciprocal(
            sb_mv[:, 2:3], sb_mv[:, 3:4]), waits=[("ts", P.at("ts"))], inc="tv", hz=True)
        P.add("vector", lambda e: e.tensor_scalar(
            dst_ap_fn(), src_sb[:], sb_mv[:, 0:1], sb_mv[:, 2:3],
            ALU.subtract, ALU.mult), inc="tv")

    for t in range(NT):
        buf = sb_xt if t % 2 == 0 else sb_xt2
        w = [("tv", P.at("tv"))] if t >= 2 else []
        P.add("sync", lambda e, t=t, b=buf: e.dma_start(
            out=b[:], in_=d_x[128 * t:128 * t + 128, :]), waits=w, inc="sd", amt=16)
        emit_ln(lambda t=t: sb_xln[:, E * t:E * t + E], buf,
                first_waits=[("sd", P.at("sd"))])
        if t == 0:
            tap("xln", lambda: sb_xln[:, 0:E])
    ln1_tv = P.at("tv")

    tp_i = 0
    for t in range(NT):
        for ch in range(NCH):
            pb = psA_bf if tp_i % 2 == 0 else psB_bf
            w = [("tv", P.at("tv"))] if tp_i >= 2 else [("tv", ln1_tv)]
            P.add("tensor", lambda e, t=t, c=ch, p=pb: e.transpose(
                p[:, 0:128], sb_xln[:, E * t + 128 * c:E * t + 128 * c + 128],
                sb_iden[:]), waits=w, inc="tt")
            P.add("vector", lambda e, t=t, c=ch, p=pb: e.tensor_copy(
                sb_xlnT[:, S * c + 128 * t:S * c + 128 * t + 128], p[:, 0:128]),
                waits=[("tt", P.at("tt")), ("gd", gd_loads)], inc="tv")
            tp_i += 1
    xlnT_tv = P.at("tv")
    tap("xlnT", lambda: sb_xlnT[:, 0:2048])
    M.free("sb_xln")
    sb_qT = M.alloc("sb_qT", [128, NCH * S], BF16)
    sb_kT = M.alloc("sb_kT", [128, NCH * S], BF16)

    P.barrier("tensor", [("tv", xlnT_tv)])
    for ch in range(NCH):
        for wsb, qsb, boff in ((sb_wq, sb_qT, 0), (sb_wk, sb_kT, E)):
            for ec in range(NCH):
                for ns in range(4):
                    P.add("tensor", lambda e, ec=ec, ns=ns, ch=ch, wsb=wsb: e.matmul(
                        ps_big[:, 512 * ns:512 * ns + 512],
                        wsb[:, E * ec + 128 * ch:E * ec + 128 * ch + 128],
                        sb_xlnT[:, S * ec + 512 * ns:S * ec + 512 * ns + 512],
                        start=(ec == 0), stop=False), inc="tt")
            for ns in range(4):
                P.add("tensor", lambda e, ns=ns, ch=ch, boff=boff: e.matmul(
                    ps_big[:, 512 * ns:512 * ns + 512],
                    sb_brow[0:1, boff + 128 * ch:boff + 128 * ch + 128],
                    sb_ones[0:1, 0:512], start=False, stop=True), inc="tt")
            P.add("vector", lambda e, ch=ch, qsb=qsb: e.tensor_copy(
                qsb[:, S * ch:S * ch + S], ps_big[:, :]),
                waits=[("tt", P.at("tt"))], inc="tv")
            P.barrier("tensor", [("tv", P.at("tv"))])
            if ch == 0 and qsb is sb_qT:
                tap("qTe", lambda: sb_qT[:, 0:2048])
                tap("wq0", lambda: sb_wq[:, 0:2048])
                tap("brow", lambda: sb_brow[0:1, 0:2048])
    M.free("sb_wq", "sb_wk")
    sb_v65 = M.alloc("sb_v65", [128, NT * H * 65], BF16)
    for t in range(NT):
        P.add("vector", lambda e, t=t: e.memset(
            bass.AP(sb_v65, 1040 * t + 64, [[NT * H * 65, 128], [65, H], [1, 1]]),
            1.0), inc="tv")
    for t in range(NT):
        for ec in range(NCH):
            for ns in range(2):
                P.add("tensor", lambda e, ec=ec, ns=ns, t=t: e.matmul(
                    ps_big[:, 512 * ns:512 * ns + 512],
                    sb_xlnT[:, S * ec + 128 * t:S * ec + 128 * t + 128],
                    sb_wv[:, E * ec + 512 * ns:E * ec + 512 * ns + 512],
                    start=(ec == 0), stop=False), inc="tt")
        for ns in range(2):
            P.add("tensor", lambda e, ns=ns: e.matmul(
                ps_big[:, 512 * ns:512 * ns + 512],
                sb_ones[0:1, 0:128],
                sb_brow[0:1, 2 * E + 512 * ns:2 * E + 512 * ns + 512],
                start=False, stop=True), inc="tt")

        def v_evac(e, t=t):
            dst = bass.AP(sb_v65, 1040 * t, [[NT * H * 65, 128], [65, H], [1, 64]])
            src = bass.AP(ps_big, 0, [[2048, 128], [64, H], [1, 64]])
            return e.tensor_copy(dst, src)
        P.add("vector", v_evac, waits=[("tt", P.at("tt"))], inc="tv")
        P.barrier("tensor", [("tv", P.at("tv"))])
    proj_tv = P.at("tv")
    tap("qT", lambda: sb_qT[:, 0:2048])
    tap("kT", lambda: sb_kT[:, 0:2048])
    tap("v65", lambda: sb_v65[:, 0:2048])
    M.free("sb_wv", "sb_brow", "sb_xlnT", "sb_xt2", "sb_xt")

    sb_etf = M.alloc("sb_etf", [128, ETW], BF16)
    sb_etr = M.alloc("sb_etr", [128, ETW], BF16)
    sb_stg = M.alloc("sb_stg", [128, 2 * TW], BF16)
    sb_A = M.alloc("sb_A", [128, 8 * S], BF16)
    sb_Ss = M.alloc("sb_Ss", [128, 2 * LC], F32)
    sb_B = M.alloc("sb_B", [128, 32 * LC], BF16)
    sb_X = M.alloc("sb_X", [128, 3 * LC], BF16)
    for half in range(2):
        P.add("gpsimd", lambda e, hf=half: e.dma_start(
            out=sb_etf[64 * hf:64 * hf + 64, :], in_=d_etf),
            waits=[("tv", proj_tv)] if half == 0 else [], inc="gd", amt=16)
        P.add("gpsimd", lambda e, hf=half: e.dma_start(
            out=sb_etr[64 * hf:64 * hf + 64, :], in_=d_etr), inc="gd", amt=16)
    gd_tbl = P.at("gd")

    sb_sq = sb_A

    P.add("vector", lambda e: e.tensor_copy(sb_edg[:, 0:1], sb_etf[:, 0:1]),
          waits=[("gd", gd_tbl)], inc="tv")
    P.add("vector", lambda e: e.tensor_copy(sb_edg[:, 1:2], sb_etf[:, ETW - 1:ETW]),
          inc="tv")

    P.add("vector", lambda e: e.tensor_mul(
        sb_sq[0:64, 0:2048], sb_etf[0:64, 0:2048], sb_etf[0:64, 0:2048]),
        waits=[("gd", gd_tbl), ("tt", P.at("tt"))], inc="tv")
    P.barrier("tensor", [("tv", P.at("tv"))])
    for ns in range(4):
        P.add("tensor", lambda e, ns=ns: e.matmul(
            ps_big[0:1, 512 * ns:512 * ns + 512], sb_ones[0:64, 0:1],
            sb_sq[0:64, 512 * ns:512 * ns + 512], start=True, stop=False), inc="tt")
    P.add("vector", lambda e: e.tensor_mul(
        sb_sq[0:64, 0:2048], sb_etf[0:64, 2048:4096], sb_etf[0:64, 2048:4096]),
        waits=[("tt", P.at("tt"))], inc="tv")
    P.barrier("tensor", [("tv", P.at("tv"))])
    for ns in range(4):
        P.add("tensor", lambda e, ns=ns: e.matmul(
            ps_big[0:1, 512 * ns:512 * ns + 512], sb_ones[0:64, 0:1],
            sb_sq[0:64, 512 * ns:512 * ns + 512], start=False, stop=True), inc="tt")
    P.add("vector", lambda e: e.reduce_max(
        sb_tmp[0:1, 0:1], ps_big[0:1, :], axis=AX),
        waits=[("tt", P.at("tt"))], inc="tv", hz=True)
    P.add("scalar", lambda e: e.activation(
        sb_tmp[0:1, 1:2], sb_tmp[0:1, 0:1], AF.Sqrt),
        waits=[("tv", P.at("tv"))], inc="ts")
    me_ts = P.at("ts")
    P.barrier("tensor", [("tv", P.at("tv"))])

    sq_parity = 0
    for h in range(H):
        ch, half = h // 2, h % 2
        qsl = slice(64 * half, 64 * half + 64)
        tpos = (64 * half, 0)
        for src, col in ((sb_qT, 2), (sb_kT, 3)):
            P.add("vector", lambda e, src=src, ch=ch, qsl=qsl: e.tensor_mul(
                sb_sq[qsl, 0:S], src[qsl, S * ch:S * ch + S],
                src[qsl, S * ch:S * ch + S]),
                waits=[("tv", proj_tv), ("tt", P.at("tt"))], inc="tv")
            P.barrier("tensor", [("tv", P.at("tv"))])
            for ns in range(4):
                P.add("tensor", lambda e, ns=ns, qsl=qsl, tpos=tpos: e.matmul(
                    ps_big[0:1, 512 * ns:512 * ns + 512], sb_ones[qsl, 0:1],
                    sb_sq[qsl, 512 * ns:512 * ns + 512], start=True, stop=True,
                    tile_position=tpos), inc="tt")
            P.add("vector", lambda e, col=col: e.reduce_max(
                sb_tmp[0:1, col:col + 1], ps_big[0:1, :], axis=AX),
                waits=[("tt", P.at("tt"))], inc="tv", hz=True)
            P.barrier("tensor", [("tv", P.at("tv"))])
        P.add("scalar", lambda e: e.activation(
            sb_tmp[0:1, 4:5], sb_tmp[0:1, 2:3], AF.Sqrt),
            waits=[("tv", P.at("tv")), ("ts", me_ts)], inc="ts")
        P.add("scalar", lambda e: e.activation(
            sb_tmp[0:1, 5:6], sb_tmp[0:1, 3:4], AF.Sqrt), inc="ts")
        P.add("vector", lambda e: e.tensor_mul(
            sb_tmp[0:1, 6:7], sb_tmp[0:1, 4:5], sb_tmp[0:1, 5:6]),
            waits=[("ts", P.at("ts"))], inc="tv")
        P.add("vector", lambda e: e.tensor_add(
            sb_tmp[0:1, 7:8], sb_tmp[0:1, 4:5], sb_tmp[0:1, 5:6]), inc="tv")
        P.add("vector", lambda e: e.tensor_mul(
            sb_tmp[0:1, 7:8], sb_tmp[0:1, 7:8], sb_tmp[0:1, 1:2]), inc="tv")
        P.add("vector", lambda e: e.tensor_add(
            sb_tmp[0:1, 6:7], sb_tmp[0:1, 6:7], sb_tmp[0:1, 7:8]), inc="tv")
        P.add("vector", lambda e: e.tensor_scalar(
            sb_nbh[0:1, 0:1], sb_tmp[0:1, 6:7], -1.02 / 8.0, None, ALU.mult),
            inc="tv")
        P.add("tensor", lambda e, h=h: e.matmul(
            ps_d[:, 256 + h:256 + h + 1], sb_ones[0:1, 0:128], sb_nbh[0:1, 0:1],
            start=True, stop=True), waits=[("tv", P.at("tv"))], inc="tt")
        P.add("vector", lambda e, h=h: e.tensor_copy(
            sb_negB[:, h:h + 1], ps_d[:, 256 + h:256 + h + 1]),
            waits=[("tt", P.at("tt"))], inc="tv")
        P.barrier("tensor", [("tv", P.at("tv"))])
    negb_tv = P.at("tv")
    if debug:
        tap("negB", lambda: sb_negB[:, 0:1])

    stg_order = [(0, t) for t in range(4)] + [(1, j) for j in range(NT)] \
        + [(0, t) for t in range(4, NT)]
    rot_par = [0]
    rot_ev = [("tv", P.at("tv")), ("tv", P.at("tv"))]
    stg_tile_idx = [0]
    stg_dma_hist = []
    stg_sd = {hh: {} for hh in range(H)}

    def emit_stage_tile(hs, kind, t):
        ch2, half2 = hs // 2, hs % 2
        qsl2 = slice(64 * half2, 64 * half2 + 64)
        tpos2 = (64 * half2, 0)
        src = sb_qT if kind == 0 else sb_kT
        tbl = sb_etr if kind == 0 else sb_etf
        dstD = d_qe if kind == 0 else d_ke
        voff = 1920 - 128 * t
        idx_t = stg_tile_idx[0]
        stg_tile_idx[0] += 1
        sl = idx_t % 2
        ev_last = {}
        for c in range(5):
            ncols = 512 if c < 4 else 128
            par = rot_par[0]
            rot_par[0] ^= 1
            P.add("tensor", lambda e, c=c, ncols=ncols, par=par, t=t,
                  src=src, tbl=tbl, ch2=ch2, qsl2=qsl2, tpos2=tpos2,
                  voff=voff: e.matmul(
                ps_big[:, 512 * par:512 * par + ncols],
                src[qsl2, S * ch2 + 128 * t:S * ch2 + 128 * t + 128],
                tbl[qsl2, voff + 512 * c:voff + 512 * c + ncols],
                start=True, stop=True, tile_position=tpos2),
                waits=[rot_ev[par]], inc="tt")
            tt_c = P.at("tt")
            eng, sem = ("vector", "tv") if c % 2 == 0 else ("scalar", "ts")
            we = [("tt", tt_c)]
            if c < 2 and idx_t >= 2:
                we.append(("sd", stg_dma_hist[idx_t - 2]))
            if eng == "vector":
                P.add(eng, lambda e, c=c, ncols=ncols, par=par, sl=sl:
                      e.tensor_copy(
                    sb_stg[:, TW * sl + 512 * c:TW * sl + 512 * c + ncols],
                    ps_big[:, 512 * par:512 * par + ncols]),
                    waits=we, inc=sem)
            else:
                P.add(eng, lambda e, c=c, ncols=ncols, par=par, sl=sl:
                      e.copy(
                    sb_stg[:, TW * sl + 512 * c:TW * sl + 512 * c + ncols],
                    ps_big[:, 512 * par:512 * par + ncols]),
                    waits=we, inc=sem)
            rot_ev[par] = (sem, P.at(sem))
            ev_last[sem] = P.at(sem)
        P.add("sync", lambda e, sl=sl, t=t, dstD=dstD, hs=hs: e.dma_start(
            out=dstD[hs, t], in_=sb_stg[:, TW * sl:TW * sl + TW]),
            waits=[(s, v) for s, v in ev_last.items()], inc="sd", amt=16)
        stg_sd[hs][(kind, t)] = P.at("sd")
        stg_dma_hist.append(P.at("sd"))

    def emit_edge_dots(hs):
        ch2, half2 = hs // 2, hs % 2
        qsl2 = slice(64 * half2, 64 * half2 + 64)
        tpos2 = (64 * half2, 0)
        for eid in range(2):
            for ns in range(4):
                par = rot_par[0]
                rot_par[0] ^= 1
                P.add("tensor", lambda e, ns=ns, ch2=ch2, qsl2=qsl2,
                      tpos2=tpos2, eid=eid, par=par: e.matmul(
                    ps_big[0:1, 512 * par:512 * par + 512],
                    sb_edg[qsl2, eid:eid + 1],
                    sb_qT[qsl2, S * ch2 + 512 * ns:S * ch2 + 512 * ns + 512],
                    start=True, stop=True, tile_position=tpos2),
                    waits=[rot_ev[par]], inc="tt")
                P.add("vector", lambda e, eid=eid, ns=ns, par=par:
                      e.tensor_copy(
                    sb_qe2[0:1, S * eid + 512 * ns:S * eid + 512 * ns + 512],
                    ps_big[0:1, 512 * par:512 * par + 512]),
                    waits=[("tt", P.at("tt"))], inc="tv")
                rot_ev[par] = ("tv", P.at("tv"))
        P.barrier("tensor", [("tv", P.at("tv"))])

    for kind, t in stg_order:
        emit_stage_tile(0, kind, t)

    ga0_carry = None
    bgd_carry = {}
    atr_lc2_end = None
    for h in range(H):
        ch, half = h // 2, h % 2
        qsl = slice(64 * half, 64 * half + 64)
        tpos = (64 * half, 0)
        emit_edge_dots(h)

        exp_hist = {}
        stt_hist = {}
        slot_free = {}
        vb1_hist = {}
        big_bf5 = ps_big.bitcast(BF16)
        abf_slot = [big_bf5[:, 2048:2560],
                    big_bf5[:, 3072:3584]]

        def emit_a_gather(hs, lc_t, i, wA):
            half = lc_t % 2
            t = 4 * lc_t + i
            P.add("gpsimd", lambda e, t=t, i=i, hs=hs, half=half:
                  e.indirect_dma_start(
                sb_A[:, S * (4 * half + i):S * (4 * half + i) + S], None,
                d_qe_flat, IndirectOffsetOnAxis(sb_idxA[:], 1),
                element_offset=(hs * NT + t) * 128 * TW),
                waits=wA + [("sd", stg_sd[hs][(0, t)])], inc="gd", amt=16)

        def emit_b_pair(hs, pr, j, wtv):
            P.add("gpsimd", lambda e, j=j, pr=pr, hs=hs: e.indirect_dma_start(
                sb_B[:, 2 * LC * j:2 * LC * j + 2 * LC], None,
                d_ke_flat, IndirectOffsetOnAxis(sb_idxB[:], 1),
                element_offset=(hs * NT + j) * 128 * TW + 2 * LC * pr),
                waits=[wtv, ("sd", stg_sd[hs][(1, j)])], inc="gd", amt=16)

        if h == 0:
            for i in range(4):
                emit_a_gather(0, 0, i, [("tt", P.at("tt"))])
            ga_hist = {0: P.at("gd")}
            bgd = {}
            for j in range(NT):
                emit_b_pair(0, 0, j, ("tv", P.at("tv")))
                bgd[(0, j)] = P.at("gd")
        else:
            ga_hist = {0: ga0_carry}
            bgd = bgd_carry
        bgd_carry = {}
        for lc in range(NLC):
            half = lc % 2
            slot_free[(lc, 0)] = ("tv", P.at("tv"))
            slot_free[(lc, 1)] = ("tv", P.at("tv"))
            slot_free.pop(0, None)
            slot_free.pop(1, None)
            if h == 0 and lc == 0 and debug:
                tap("A", lambda: sb_A[:, 0:2048], wait_sem="gd")
            for j in range(NT):
                pj = [ps_a[:, :], ps_b[:, :]][j % 2]
                sl = j % 2
                gb_done = bgd[(lc - lc % 2, j)]
                if lc + 1 < NLC and j in (8, 10, 12, 14):
                    lt = lc + 1
                    wA = [("tt", exp_hist[(lt - 2, NT - 1, "pv")])] \
                        if lt >= 2 else [("tt", P.at("tt"))]
                    emit_a_gather(h, lt, (j - 8) // 2, wA)
                    if j == 14:
                        ga_hist[lt] = P.at("gd")
                edg_runs = []
                tr_list = []
                for i in range(4):
                    t_l = 4 * lc + i
                    if t_l - j <= -9:
                        eid = 0
                    elif t_l - j >= 9:
                        eid = 1
                    else:
                        tr_list.append(i)
                        continue
                    if edg_runs and edg_runs[-1][0] == eid and edg_runs[-1][2] == i:
                        edg_runs[-1] = (eid, edg_runs[-1][1], i + 1)
                    else:
                        edg_runs.append((eid, i, i + 1))
                wt = []
                if (lc, j - 2) in stt_hist:
                    wt.append(("tv", stt_hist[(lc, j - 2)]))
                elif lc > 0 and (lc - 1, 14 + j) in stt_hist:
                    wt.append(("tv", stt_hist[(lc - 1, 14 + j)]))
                else:
                    wt.append(("tv", P.at("tv")))
                P.add("tensor", lambda e, j=j, lc=lc, ch=ch, qsl=qsl,
                      tpos=tpos, p=pj, st=(not edg_runs): e.matmul(
                    p,
                    sb_kT[qsl, S * ch + 128 * j:S * ch + 128 * j + 128],
                    sb_qT[qsl, S * ch + LC * lc:S * ch + LC * lc + LC],
                    start=True, stop=st, tile_position=tpos),
                    waits=wt, inc="tt")
                for ei, (eid, i0, i1) in enumerate(edg_runs):
                    P.add("tensor", lambda e, p=pj, eid=eid, i0=i0, i1=i1,
                          lc=lc, st=(ei == len(edg_runs) - 1): e.matmul(
                        p[:, 128 * i0:128 * i1],
                        sb_ones[0:1, 0:128],
                        sb_qe2[0:1, S * eid + LC * lc + 128 * i0:
                               S * eid + LC * lc + 128 * i1],
                        start=False, stop=st), inc="tt")
                tt_scores = P.at("tt")
                wtr = [("gd", ga_hist[lc])]
                sf = slot_free.get((lc, sl)) if j < 2 else slot_free.get(sl)
                if sf is None:
                    sf = ("tv", P.at("tv"))
                wtr.append(sf)
                first_tr = True
                for i in tr_list:
                    P.add("tensor", lambda e, i=i, j=j, half=half, sl=sl:
                          e.transpose(
                        abf_slot[sl][:, 128 * i:128 * i + 128],
                        sb_A[:, S * (4 * half + i) + 128 * j:
                             S * (4 * half + i) + 128 * j + 128],
                        sb_iden[:, :]),
                        waits=(wtr if first_tr else []), inc="tt")
                    first_tr = False
                tt_atr = P.at("tt")
                wexp_extra = []
                if (lc, j - 2, "exp") in exp_hist:
                    wexp_extra = [("ts", exp_hist[(lc, j - 2, "exp")])]
                elif lc > 0:
                    wexp_extra = [("ts", exp_hist[(lc - 1, 14 + j, "exp")])]
                boff = 2 * LC * j + LC * (lc % 2)
                if tr_list:
                    P.add("vector", lambda e, boff=boff, sl=sl: e.tensor_tensor(
                        sb_Ss[:, LC * sl:LC * sl + LC], abf_slot[sl][:, 0:LC],
                        sb_B[:, boff:boff + LC], ALU.add),
                        waits=[("tt", tt_atr), ("gd", gb_done)] + wexp_extra,
                        inc="tv")
                    vb1_hist[(lc, j)] = P.at("tv")
                    slot_free[sl] = ("tv", P.at("tv"))
                    P.add("vector", lambda e, p=pj, sl=sl: e.scalar_tensor_tensor(
                        sb_Ss[:, LC * sl:LC * sl + LC], p, 1.0,
                        sb_Ss[:, LC * sl:LC * sl + LC], ALU.mult, ALU.add),
                        waits=[("tt", tt_scores)], inc="tv")
                else:
                    P.add("vector", lambda e, p=pj, boff=boff, sl=sl:
                          e.scalar_tensor_tensor(
                        sb_Ss[:, LC * sl:LC * sl + LC], p, 1.0,
                        sb_B[:, boff:boff + LC],
                        ALU.mult, ALU.add),
                        waits=[("tt", tt_scores), ("gd", gb_done)] + wexp_extra,
                        inc="tv")
                    vb1_hist[(lc, j)] = P.at("tv")
                stt_hist[(lc, j)] = P.at("tv")
                if lc == 1:
                    emit_b_pair(h, 1, j, ("tv", vb1_hist[(1, j)]))
                    bgd[(2, j)] = P.at("gd")
                if lc == 3 and h + 1 < H:
                    emit_b_pair(h + 1, 0, j, ("tv", vb1_hist[(3, j)]))
                    bgd_carry[(0, j)] = P.at("gd")
                    if j in (1, 3, 5, 7):
                        emit_a_gather(h + 1, 0, (j - 1) // 2,
                                      [("tt", atr_lc2_end)])
                    if j == 7:
                        ga0_carry = P.at("gd")
                if debug and h == 0 and lc == 0 and j == 0:
                    tap("Bt", lambda: sb_B[:, 0:LC], wait_sem="gd")
                    tap("scr", lambda: sb_Ss[:, 0:LC], wait_sem="tv")
                P.add("scalar", lambda e, sl=sl, x4=j % 3, h=h: e.activation(
                    sb_X[:, LC * x4:LC * x4 + LC],
                    sb_Ss[:, LC * sl:LC * sl + LC], AF.Exp,
                    bias=sb_negB[:, h:h + 1], scale=0.125),
                    waits=[("tv", stt_hist[(lc, j)]), ("tv", negb_tv)], inc="ts")
                exp_hist[(lc, j, "exp")] = P.at("ts")
                if debug and h == 0 and lc == 0 and j == 0:
                    tap("X", lambda: sb_X[:, 0:LC], wait_sem="ts")

                def emit_pv(jp):
                    wpv = [("ts", exp_hist[(lc, jp, "exp")])]
                    if jp == 0:
                        wpv.append(("tv", P.at("tv")))
                    P.add("tensor", lambda e, jp=jp, h=h: e.matmul(
                        ps_c[0:65, :],
                        sb_v65[:, 1040 * jp + 65 * h:1040 * jp + 65 * h + 65],
                        sb_X[:, LC * (jp % 3):LC * (jp % 3) + LC],
                        start=(jp == 0), stop=(jp == NT - 1)),
                        waits=wpv, inc="tt")
                    exp_hist[(lc, jp, "pv")] = P.at("tt")
                if j >= 2:
                    emit_pv(j - 2)
                gi = lc * NT + j
                if h + 1 < H and gi % 2 == 0 and gi // 2 < len(stg_order):
                    kind2, t2 = stg_order[gi // 2]
                    emit_stage_tile(h + 1, kind2, t2)
                if j == NT - 1:
                    emit_pv(NT - 2)
                    emit_pv(NT - 1)
            if lc == 2:
                atr_lc2_end = P.at("tt")
            P.add("vector", lambda e: e.tensor_copy(
                sb_ctv[0:65, :], ps_c[0:65, :]),
                waits=[("tt", P.at("tt")), ("sd", P.at("sd"))], inc="tv")
            for i in range(4):
                wtr2 = [("tv", P.at("tv"))] if i == 0 else []
                P.add("tensor", lambda e, i=i: e.transpose(
                    ps_d[:, 65 * i:65 * i + 65],
                    sb_ctv[0:65, 128 * i:128 * i + 128],
                    sb_idenf[0:65, 0:65]), waits=wtr2, inc="tt")
            P.add("vector", lambda e: e.reciprocal(
                sb_zr[:, 0:4], bass.AP(ps_d, 64, [[512, 128], [65, 4]])),
                waits=[("tt", P.at("tt"))], inc="tv", hz=True)
            P.add("vector", lambda e: e.tensor_tensor(
                bass.AP(sb_ctp, 0, [[256, 128], [64, 4], [1, 64]]),
                bass.AP(ps_d, 0, [[512, 128], [65, 4], [1, 64]]),
                bass.AP(sb_zr, 0, [[4, 128], [1, 4], [0, 64]]),
                ALU.mult), inc="tv")
            def ctx_store(e, h=h, lc=lc):
                dst = bass.AP(d_ctx.tensor, h * S * D + LC * lc * D,
                              [[D, 128], [128 * D, 4], [1, D]])
                src = bass.AP(sb_ctp, 0, [[4 * D, 128], [D, 4], [1, D]])
                return e.dma_start(out=dst, in_=src)
            P.add("sync", ctx_store, waits=[("tv", P.at("tv"))], inc="sd", amt=16)
    attn_sd = P.at("sd")

    M.free("sb_X", "sb_B", "sb_Ss", "sb_A", "sb_stg", "sb_etr",
           "sb_etf", "sb_v65", "sb_kT", "sb_qT")
    sb_xpa = M.alloc("sb_xpa", [128, NT * E], F32)
    sb_xd = M.alloc("sb_xd", [128, NT * E], BF16)
    sb_wd = M.alloc("sb_wd", [128, NCH * E], BF16)
    sb_ctl = M.alloc("sb_ctl", [128, E], F32)
    sb_ffw = M.alloc("sb_ffw", [128, E], F32)
    sb_q1 = M.alloc("sb_q1", [128, E], F32)
    sb_q2 = M.alloc("sb_q2", [128, E], F32)
    sb_q3 = M.alloc("sb_q3", [128, E], F32)
    sb_sc = M.alloc("sb_sc", [128, 32], F32)
    sb_scr = M.alloc("sb_scr", [128, 32], F32)
    sb_dump = M.alloc("sb_dump", [128, E], F32)

    def emit_quant(dst_ap_fn, src_sb):
        def a3(sb, off=0):
            return bass.AP(sb, off, [[E, 128], [32, 32], [1, 32]])

        def scbc(sb):
            return bass.AP(sb, 0, [[32, 128], [1, 32], [0, 32]])
        P.add("vector", lambda e: e.tensor_reduce(
            sb_sc[:], a3(src_sb), axis=AX, op=ALU.max,
            apply_absolute_value=True), inc="tv", hz=True)
        P.add("vector", lambda e: e.tensor_scalar(
            sb_sc[:], sb_sc[:], 1.0 / 7.5, 1e-30, ALU.mult, ALU.max), inc="tv")
        P.add("vector", lambda e: e.reciprocal(sb_scr[:], sb_sc[:]), inc="tv", hz=True)
        P.add("vector", lambda e: e.tensor_tensor(
            a3(sb_q1), a3(src_sb), scbc(sb_scr), ALU.mult), inc="tv")
        P.add("vector", lambda e: e.tensor_mul(sb_q2[:], sb_q1[:], sb_q1[:]), inc="tv")
        P.add("vector", lambda e: e.tensor_scalar(
            sb_q3[:], sb_q2[:], 4.0, None, ALU.is_ge), inc="tv")
        P.add("vector", lambda e: e.tensor_scalar(
            sb_q2[:], sb_q2[:], 16.0, None, ALU.is_ge), inc="tv")
        P.add("vector", lambda e: e.tensor_scalar(
            sb_dump[:], sb_q3[:], 0.125, 0.125, ALU.mult, ALU.add), inc="tv")
        P.add("vector", lambda e: e.scalar_tensor_tensor(
            sb_dump[:], sb_q2[:], 0.25, sb_dump[:], ALU.mult, ALU.add), inc="tv")
        P.add("vector", lambda e: e.tensor_scalar(
            sb_q3[:], sb_q3[:], -4.0, 8.0, ALU.mult, ALU.add), inc="tv")
        P.add("vector", lambda e: e.scalar_tensor_tensor(
            sb_q3[:], sb_q2[:], -2.0, sb_q3[:], ALU.mult, ALU.add), inc="tv")
        P.add("vector", lambda e: e.tensor_mul(sb_q1[:], sb_q1[:], sb_q3[:]), inc="tv")
        P.add("vector", lambda e: e.tensor_copy(
            sb_q2.bitcast(I32)[:], sb_q1[:]), inc="tv")
        P.add("vector", lambda e: e.tensor_copy(
            sb_q1[:], sb_q2.bitcast(I32)[:]), inc="tv")
        P.add("vector", lambda e: e.tensor_mul(sb_q1[:], sb_q1[:], sb_dump[:]), inc="tv")
        P.add("vector", lambda e: e.tensor_tensor(
            dst_ap_fn(), a3(sb_q1), scbc(sb_sc), ALU.mult), inc="tv")

    for t in range(NT):
        buf = sb_xt if t % 2 == 0 else sb_xt2
        w = [("tv", P.at("tv"))] if t >= 2 else []
        P.add("sync", lambda e, t=t, b=buf: e.dma_start(
            out=b[:], in_=d_x[128 * t:128 * t + 128, :]), waits=w, inc="sd", amt=16)

        def ctx_ld(e, t=t):
            src = bass.AP(d_ctx.tensor, 128 * t * D, [[D, 128], [S * D, H], [1, D]])
            dst = bass.AP(sb_ctl, 0, [[E, 128], [D, H], [1, D]])
            return e.dma_start(out=dst, in_=src)
        P.add("sync", ctx_ld, waits=[("sd", attn_sd)], inc="sd", amt=16)
        P.add("vector", lambda e, b=buf: e.tensor_add(b[:], b[:], sb_ctl[:]),
              waits=[("sd", P.at("sd"))], inc="tv")
        P.add("vector", lambda e, t=t, b=buf: e.tensor_copy(
            sb_xpa[:, E * t:E * t + E], b[:]), inc="tv")
        emit_ln(lambda: sb_q3[:], buf)
        P.add("vector", lambda e: e.tensor_copy(sb_ffw[:], sb_q3[:]), inc="tv")
        emit_quant(lambda t=t: bass.AP(sb_xd, E * t, [[NT * E, 128], [32, 32], [1, 32]]),
                   sb_ffw)
        if t == 0:
            tap("xpa", lambda: sb_xpa[:, 0:E])
            tap("xd", lambda: sb_xd[:, 0:E])

    for fc in range(NCH):
        w = [("tv", P.at("tv"))] if fc >= 1 else [("tv", P.at("tv"))]
        P.add("sync", lambda e, fc=fc: e.dma_start(
            out=sb_ffw[:], in_=d_ffw[128 * fc:128 * fc + 128, :]),
            waits=w, inc="sd", amt=16)
        P.barrier("vector", [("sd", P.at("sd"))])
        emit_quant(lambda fc=fc: bass.AP(sb_wd, E * fc, [[NCH * E, 128], [32, 32], [1, 32]]),
                   sb_ffw)
        if fc == 0:
            tap("wd", lambda: sb_wd[:, 0:E])
    quant_tv = P.at("tv")

    M.free("sb_dump", "sb_scr", "sb_sc", "sb_q3", "sb_q2", "sb_q1",
           "sb_ffw", "sb_ctl")
    sb_xdT = M.alloc("sb_xdT", [128, NCH * S], BF16)
    sb_wdT = M.alloc("sb_wdT", [128, NCH * E], BF16)
    sb_yt = M.alloc("sb_yt", [128, E], F32)

    tp_i = 0
    P.barrier("tensor", [("tv", quant_tv)])
    for t in range(NT):
        for kc in range(NCH):
            pb = psA_bf if tp_i % 2 == 0 else psB_bf
            w = [("tv", P.at("tv"))] if tp_i >= 2 else []
            P.add("tensor", lambda e, t=t, kc=kc, p=pb: e.transpose(
                p[:, 0:128], sb_xd[:, E * t + 128 * kc:E * t + 128 * kc + 128],
                sb_iden[:]), waits=w, inc="tt")
            P.add("vector", lambda e, t=t, kc=kc, p=pb: e.tensor_copy(
                sb_xdT[:, S * kc + 128 * t:S * kc + 128 * t + 128], p[:, 0:128]),
                waits=[("tt", P.at("tt"))], inc="tv")
            tp_i += 1
    for fc in range(NCH):
        for kc in range(NCH):
            pb = psA_bf if tp_i % 2 == 0 else psB_bf
            P.add("tensor", lambda e, fc=fc, kc=kc, p=pb: e.transpose(
                p[:, 0:128], sb_wd[:, E * fc + 128 * kc:E * fc + 128 * kc + 128],
                sb_iden[:]), waits=[("tv", P.at("tv"))], inc="tt")
            P.add("vector", lambda e, fc=fc, kc=kc, p=pb: e.tensor_copy(
                sb_wdT[:, E * kc + 128 * fc:E * kc + 128 * fc + 128], p[:, 0:128]),
                waits=[("tt", P.at("tt"))], inc="tv")
            tp_i += 1
    trans_tv = P.at("tv")

    P.barrier("tensor", [("tv", trans_tv)])
    y_sd = {}
    for t in range(NT):
        if t >= 1:
            P.barrier("tensor", [("tv", P.at("tv"))])
        for kc in range(NCH):
            for ns in range(2):
                P.add("tensor", lambda e, t=t, kc=kc, ns=ns: e.matmul(
                    ps_big[:, 512 * ns:512 * ns + 512],
                    sb_xdT[:, S * kc + 128 * t:S * kc + 128 * t + 128],
                    sb_wdT[:, E * kc + 512 * ns:E * kc + 512 * ns + 512],
                    start=(kc == 0), stop=False), inc="tt")
        for ns in range(2):
            P.add("tensor", lambda e, ns=ns: e.matmul(
                ps_big[:, 512 * ns:512 * ns + 512],
                sb_ones[0:1, 0:128], sb_ffb[0:1, 512 * ns:512 * ns + 512],
                start=False, stop=True), inc="tt")
        wv = [("tt", P.at("tt"))]
        if t >= 1:
            wv.append(("sd", y_sd[t - 1]))
        P.add("vector", lambda e, t=t: e.tensor_add(
            sb_yt[:, 0:E], ps_big[:, 0:1024],
            sb_xpa[:, E * t:E * t + E]), waits=wv, inc="tv")
        P.add("sync", lambda e, t=t: e.dma_start(
            out=d_y[128 * t:128 * t + 128, :], in_=sb_yt[:, 0:E]),
            waits=[("tv", P.at("tv"))], inc="sd", amt=16)
        y_sd[t] = P.at("sd")

    P.add("sync", lambda e: None, waits=[("sd", P.at("sd"))])

    def replay(name, eng):
        for fn, waits, inc, amt in P.items[name]:
            for s_, v in waits:
                eng.wait_ge(sem_h[s_], v)
            inst = fn(eng)
            if inc is not None:
                inst.then_inc(sem_h[inc], amt)

    @block.sync
    def _(sync):
        replay("sync", sync)

    @block.vector
    def _(vector):
        replay("vector", vector)

    @block.scalar
    def _(scalar):
        replay("scalar", scalar)

    @block.tensor
    def _(tensor):
        replay("tensor", tensor)

    @block.gpsimd
    def _(gpsimd):
        replay("gpsimd", gpsimd)

    block_cm.__exit__(None, None, None)
    M.free_all()
    for cm in reversed(psum_cms):
        cm.__exit__(None, None, None)
    for cm in reversed(sem_cms):
        cm.__exit__(None, None, None)
    return nc


_NC_CACHE = [None]


def _host_prep(inputs):
    x = np.ascontiguousarray(np.asarray(inputs["L0_attn_hidden_states"], dtype=np.float32))
    Et = np.asarray(inputs["L0_attn_dist_emb"], dtype=np.float32)
    E_ext = np.concatenate([
        np.broadcast_to(Et[0:1], (1024, D)), Et,
        np.broadcast_to(Et[-1:], (1025, D))], axis=0)
    p = np.arange(128, dtype=np.int32).reshape(128, 1)
    base = {
        "wqT": np.ascontiguousarray(np.asarray(inputs["L0_attn_wq"], np.float32).T),
        "wkT": np.ascontiguousarray(np.asarray(inputs["L0_attn_wk"], np.float32).T),
        "wvT": np.ascontiguousarray(np.asarray(inputs["L0_attn_wv"], np.float32).T),
        "bq": np.ascontiguousarray(np.asarray(inputs["L0_attn_bq"], np.float32).reshape(1, E)),
        "bk": np.ascontiguousarray(np.asarray(inputs["L0_attn_bk"], np.float32).reshape(1, E)),
        "bv": np.ascontiguousarray(np.asarray(inputs["L0_attn_bv"], np.float32).reshape(1, E)),
        "etf": np.ascontiguousarray(E_ext.T),
        "etr": np.ascontiguousarray(E_ext[::-1].T),
        "ffw": np.ascontiguousarray(np.asarray(inputs["L0_ffn_weight"], np.float32)),
        "ffb": np.ascontiguousarray(np.asarray(inputs["L0_ffn_bias"], np.float32).reshape(1, E)),
        "idxA": (TW - 1) * p + 128,
        "idxB": (TW - 1) * p + 127,
        "idn": np.ascontiguousarray(np.eye(128, dtype=np.float32)),
    }
    return x, base


def debug_run(inputs, core=0):
    x, base = _host_prep(inputs)
    nc = build_program(debug=True)
    im = dict(base)
    im["x"] = np.ascontiguousarray(x[core])
    res = run_bass_kernel_spmd(nc, [im], core_ids=[0])
    return res.results[0]


def kernel(L0_attn_hidden_states, L0_attn_wq, L0_attn_bq, L0_attn_wk,
           L0_attn_bk, L0_attn_wv, L0_attn_bv, L0_attn_dist_emb,
           L0_ffn_weight, L0_ffn_bias):
    x = np.ascontiguousarray(np.asarray(L0_attn_hidden_states, dtype=np.float32))
    wq = np.asarray(L0_attn_wq, dtype=np.float32)
    wk = np.asarray(L0_attn_wk, dtype=np.float32)
    wv = np.asarray(L0_attn_wv, dtype=np.float32)
    bq = np.ascontiguousarray(np.asarray(L0_attn_bq, dtype=np.float32).reshape(1, E))
    bk = np.ascontiguousarray(np.asarray(L0_attn_bk, dtype=np.float32).reshape(1, E))
    bv = np.ascontiguousarray(np.asarray(L0_attn_bv, dtype=np.float32).reshape(1, E))
    Et = np.asarray(L0_attn_dist_emb, dtype=np.float32)
    ffw = np.ascontiguousarray(np.asarray(L0_ffn_weight, dtype=np.float32))
    ffb = np.ascontiguousarray(np.asarray(L0_ffn_bias, dtype=np.float32).reshape(1, E))

    E_ext = np.concatenate([
        np.broadcast_to(Et[0:1], (1024, D)),
        Et,
        np.broadcast_to(Et[-1:], (1025, D)),
    ], axis=0)
    etf = np.ascontiguousarray(E_ext.T)
    etr = np.ascontiguousarray(E_ext[::-1].T)

    wqT = np.ascontiguousarray(wq.T)
    wkT = np.ascontiguousarray(wk.T)
    wvT = np.ascontiguousarray(wv.T)

    p = np.arange(128, dtype=np.int32).reshape(128, 1)
    idxA = (TW - 1) * p + 128
    idxB = (TW - 1) * p + 127
    iden = np.ascontiguousarray(np.eye(128, dtype=np.float32))

    if _NC_CACHE[0] is None:
        _NC_CACHE[0] = build_program()
    nc = _NC_CACHE[0]

    in_maps = []
    for c in range(8):
        in_maps.append({
            "x": np.ascontiguousarray(x[c]),
            "wqT": wqT, "wkT": wkT, "wvT": wvT,
            "bq": bq, "bk": bk, "bv": bv,
            "etf": etf, "etr": etr,
            "ffw": ffw, "ffb": ffb,
            "idxA": idxA, "idxB": idxB, "idn": iden,
        })
    res = run_bass_kernel_spmd(nc, in_maps, core_ids=list(range(8)))
    return np.stack([r["y"] for r in res.results], axis=0)


# version 22
# speedup vs baseline: 1.1346x; 1.0190x over previous
import numpy as np
import concourse.bass as bass
import concourse.mybir as mybir
from concourse.bass_utils import run_bass_kernel_spmd
from concourse.bass_core_rust import IndirectOffsetOnAxis

F32 = mybir.dt.float32
BF16 = mybir.dt.bfloat16
I32 = mybir.dt.int32
AF = mybir.ActivationFunctionType
ALU = mybir.AluOpType
AX = mybir.AxisListType.X

S = 2048
E = 1024
H = 16
D = 64
NT = S // 128
NCH = E // 128
TW = 2176
ETW = 4096
LC = 512
NLC = S // NLC if False else S // LC


class Prog:
    ENGS = ("sync", "vector", "scalar", "tensor", "gpsimd")

    def __init__(self):
        self.items = {e: [] for e in self.ENGS}
        self.counts = {}
        self.pending = {e: [] for e in self.ENGS}
        self.hazard = {}

    def sem(self, name):
        self.counts[name] = 0

    FLUSH = {"vector": "tv", "scalar": "ts"}

    def add(self, eng, fn, waits=(), inc=None, amt=1, hz=False):
        w = self.pending[eng] + list(waits)
        self.pending[eng] = []
        fs = self.FLUSH.get(eng)
        if fs is not None and (hz or self.hazard.get(eng, False)) \
                and self.counts.get(fs, 0) > 0:
            w.append((fs, self.counts[fs]))
        self.hazard[eng] = hz
        self.items[eng].append((fn, w, inc, amt))
        if inc is not None:
            self.counts[inc] += amt
        return self.counts.get(inc)

    def barrier(self, eng, waits):
        self.pending[eng].extend(waits)

    def at(self, sem):
        return self.counts[sem]


class Mem:

    def __init__(self, nc):
        self.nc = nc
        self.live = {}

    def alloc(self, name, shape, dt, side=None):
        if side is None:
            cm = self.nc.sbuf_tensor(name, shape, dt)
        else:
            cm = self.nc.sbuf_tensor(name, shape, dt, side=side)
        h = cm.__enter__()
        self.live[name] = (cm, h)
        return h

    def free(self, *names):
        for n in names:
            cm, _ = self.live.pop(n)
            cm.__exit__(None, None, None)

    def free_all(self):
        for n in reversed(list(self.live)):
            self.free(n)


DEBUG = False


def build_program(debug=False):
    nc = bass.Bass("TRN2", target_bir_lowering=False, debug=False)

    d_x = nc.dram_tensor("x", [S, E], F32, kind="ExternalInput").ap()
    d_wqT = nc.dram_tensor("wqT", [E, E], F32, kind="ExternalInput").ap()
    d_wkT = nc.dram_tensor("wkT", [E, E], F32, kind="ExternalInput").ap()
    d_wvT = nc.dram_tensor("wvT", [E, E], F32, kind="ExternalInput").ap()
    d_bq = nc.dram_tensor("bq", [1, E], F32, kind="ExternalInput").ap()
    d_bk = nc.dram_tensor("bk", [1, E], F32, kind="ExternalInput").ap()
    d_bv = nc.dram_tensor("bv", [1, E], F32, kind="ExternalInput").ap()
    d_etf = nc.dram_tensor("etf", [D, ETW], F32, kind="ExternalInput").ap()
    d_etr = nc.dram_tensor("etr", [D, ETW], F32, kind="ExternalInput").ap()
    d_ffw = nc.dram_tensor("ffw", [E, E], F32, kind="ExternalInput").ap()
    d_ffb = nc.dram_tensor("ffb", [1, E], F32, kind="ExternalInput").ap()
    d_idxA = nc.dram_tensor("idxA", [128, 1], I32, kind="ExternalInput").ap()
    d_idxB = nc.dram_tensor("idxB", [128, 1], I32, kind="ExternalInput").ap()
    d_iden = nc.dram_tensor("idn", [128, 128], F32, kind="ExternalInput").ap()
    d_y = nc.dram_tensor("y", [S, E], F32, kind="ExternalOutput").ap()

    ikind = "ExternalOutput" if debug else "Internal"
    d_qe = nc.dram_tensor("qe_st", [H, NT, 128, TW], BF16, kind=ikind).ap()
    d_ke = nc.dram_tensor("ke_st", [H, NT, 128, TW], BF16, kind=ikind).ap()
    d_qe_flat = bass.AP(d_qe.tensor, 0, [[TW, H * NT * 128], [1, TW]])
    d_ke_flat = bass.AP(d_ke.tensor, 0, [[TW, H * NT * 128], [1, TW]])
    d_ctx = nc.dram_tensor("ctx_st", [H, S, D], F32, kind=ikind).ap()
    d_dbg = {}
    if debug:
        for nm, dt_ in [("xln", BF16), ("xlnT", BF16), ("qT", BF16),
                        ("kT", BF16), ("v65", BF16), ("negB", F32),
                        ("A", BF16), ("Bt", BF16), ("scr", F32), ("X", BF16),
                        ("xpa", F32), ("xd", BF16), ("wd", BF16),
                        ("wq0", BF16), ("brow", BF16), ("qTe", BF16)]:
            d_dbg[nm] = nc.dram_tensor("dbg_" + nm, [128, 2048], dt_,
                                       kind="ExternalOutput").ap()

    M = Mem(nc)
    sb_ffb = M.alloc("sb_ffb", [1, E], BF16)
    sb_iden = M.alloc("sb_iden", [128, 128], BF16)
    sb_idenf = M.alloc("sb_idenf", [128, 128], F32)
    sb_ones = M.alloc("sb_ones", [128, 512], BF16)
    sb_idxA = M.alloc("sb_idxA", [128, 1], I32)
    sb_idxB = M.alloc("sb_idxB", [128, 1], I32)
    sb_tmp = M.alloc("sb_tmp", [1, 8], F32)
    sb_mv = M.alloc("sb_mv", [128, 4], F32)
    sb_zr = M.alloc("sb_zr", [128, 4], F32)
    sb_negB = M.alloc("sb_negB", [128, H], F32)
    sb_nbh = M.alloc("sb_nbh", [1, 2], BF16)
    sb_ctv = M.alloc("sb_ctv", [128, 512], F32)
    sb_ctp = M.alloc("sb_ctp", [128, 4 * D], F32)
    sb_edg = M.alloc("sb_edg", [128, 2], BF16)
    sb_qe2 = M.alloc("sb_qe2", [1, 2 * S], BF16)

    psum_cms = []

    def psum(name, shape, dt):
        cm = nc.psum_tensor(name, shape, dt)
        h = cm.__enter__()
        psum_cms.append(cm)
        return h

    ps_a = psum("ps_a", [128, 512], F32)
    ps_b = psum("ps_b", [128, 512], F32)
    ps_c = psum("ps_c", [128, 512], F32)
    ps_d = psum("ps_d", [128, 512], F32)
    ps_big = psum("ps_big", [128, 2048], F32)
    psA_bf = ps_a.bitcast(BF16)
    psB_bf = ps_b.bitcast(BF16)

    sem_cms = []

    def mksem(name):
        cm = nc.semaphore(name)
        h = cm.__enter__()
        sem_cms.append(cm)
        return h

    sems = ["sd", "gd", "tv", "ts", "tt"]
    sem_h = {s: mksem(s) for s in sems}

    block_cm = nc.Block()
    block = block_cm.__enter__()

    P = Prog()
    for s in sems:
        P.sem(s)

    def tap(nm, src_fn, wait_sem="tv"):
        if not debug:
            return
        P.add("sync", lambda e: e.dma_start(out=d_dbg[nm][0:src_fn().shape[0],
                                                          0:src_fn().shape[1]],
                                            in_=src_fn(),
                                            allow_slow_non_contiguous=True),
              waits=[(wait_sem, P.at(wait_sem))], inc="sd", amt=16)

    sb_xt = M.alloc("sb_xt", [128, E], F32, side="right")
    sb_xt2 = M.alloc("sb_xt2", [128, E], F32, side="right")
    sb_xlnT = M.alloc("sb_xlnT", [128, NCH * S], BF16, side="right")
    sb_brow = M.alloc("sb_brow", [1, 3 * E], BF16, side="right")
    sb_wv = M.alloc("sb_wv", [128, NCH * E], BF16, side="right")
    sb_wk = M.alloc("sb_wk", [128, NCH * E], BF16, side="right")
    sb_wq = M.alloc("sb_wq", [128, NCH * E], BF16, side="right")
    sb_xln = M.alloc("sb_xln", [128, NT * E], BF16, side="right")

    P.add("sync", lambda e: e.dma_start(out=sb_idxA[:], in_=d_idxA), inc="sd", amt=16)
    P.add("sync", lambda e: e.dma_start(out=sb_idxB[:], in_=d_idxB), inc="sd", amt=16)
    P.add("sync", lambda e: e.dma_start(out=sb_idenf[:], in_=d_iden), inc="sd", amt=16)
    P.add("gpsimd", lambda e: e.dma_start(out=sb_iden[:], in_=d_iden), inc="gd", amt=16)
    for ch in range(NCH):
        for wsb, wd_ in ((sb_wq, d_wqT), (sb_wk, d_wkT), (sb_wv, d_wvT)):
            P.add("gpsimd", lambda e, c=ch, wsb=wsb, wd_=wd_: e.dma_start(
                out=wsb[:, E * c:E * c + E], in_=wd_[128 * c:128 * c + 128, :]),
                inc="gd", amt=16)
    P.add("gpsimd", lambda e: e.dma_start(out=sb_brow[:, 0:E], in_=d_bq), inc="gd", amt=16)
    P.add("gpsimd", lambda e: e.dma_start(out=sb_brow[:, E:2 * E], in_=d_bk), inc="gd", amt=16)
    P.add("gpsimd", lambda e: e.dma_start(out=sb_brow[:, 2 * E:3 * E], in_=d_bv), inc="gd", amt=16)
    P.add("gpsimd", lambda e: e.dma_start(out=sb_ffb[:], in_=d_ffb), inc="gd", amt=16)
    gd_loads = P.at("gd")

    P.add("vector", lambda e: e.memset(sb_ones[:], 1.0), inc="tv")

    def emit_ln(dst_ap_fn, src_sb, first_waits=()):
        def src():
            return src_sb() if callable(src_sb) else src_sb[:]
        P.add("vector", lambda e: e.reduce_sum(
            sb_mv[:, 0:1], src(), axis=AX), waits=first_waits, inc="tv", hz=True)
        P.add("vector", lambda e: e.tensor_scalar(
            sb_mv[:, 0:1], sb_mv[:, 0:1], 1.0 / E, None, ALU.mult), inc="tv")
        P.add("scalar", lambda e: e.activation(
            ps_big[:, 1024:2048], src(), AF.Square, accum_out=sb_mv[:, 1:2]),
            waits=[("tv", P.at("tv"))], inc="ts")
        P.add("vector", lambda e: e.tensor_mul(
            sb_mv[:, 2:3], sb_mv[:, 0:1], sb_mv[:, 0:1]),
            waits=[("ts", P.at("ts"))], inc="tv")
        P.add("vector", lambda e: e.tensor_scalar(
            sb_mv[:, 1:2], sb_mv[:, 1:2], 1.0 / E, None, ALU.mult), inc="tv")
        P.add("vector", lambda e: e.tensor_sub(
            sb_mv[:, 1:2], sb_mv[:, 1:2], sb_mv[:, 2:3]), inc="tv")
        P.add("vector", lambda e: e.tensor_scalar(
            sb_mv[:, 1:2], sb_mv[:, 1:2], 1e-5, None, ALU.add), inc="tv")
        P.add("scalar", lambda e: e.activation(
            sb_mv[:, 3:4], sb_mv[:, 1:2], AF.Sqrt),
            waits=[("tv", P.at("tv"))], inc="ts")
        P.add("vector", lambda e: e.reciprocal(
            sb_mv[:, 2:3], sb_mv[:, 3:4]), waits=[("ts", P.at("ts"))], inc="tv", hz=True)
        P.add("vector", lambda e: e.tensor_scalar(
            dst_ap_fn(), src(), sb_mv[:, 0:1], sb_mv[:, 2:3],
            ALU.subtract, ALU.mult), inc="tv")

    for t in range(NT):
        buf = sb_xt if t % 2 == 0 else sb_xt2
        w = [("tv", P.at("tv"))] if t >= 2 else []
        P.add("sync", lambda e, t=t, b=buf: e.dma_start(
            out=b[:], in_=d_x[128 * t:128 * t + 128, :]), waits=w, inc="sd", amt=16)
        emit_ln(lambda t=t: sb_xln[:, E * t:E * t + E], buf,
                first_waits=[("sd", P.at("sd"))])
        if t == 0:
            tap("xln", lambda: sb_xln[:, 0:E])
    ln1_tv = P.at("tv")

    tp_i = 0
    for t in range(NT):
        for ch in range(NCH):
            pb = psA_bf if tp_i % 2 == 0 else psB_bf
            w = [("tv", P.at("tv"))] if tp_i >= 2 else [("tv", ln1_tv)]
            P.add("tensor", lambda e, t=t, c=ch, p=pb: e.transpose(
                p[:, 0:128], sb_xln[:, E * t + 128 * c:E * t + 128 * c + 128],
                sb_iden[:]), waits=w, inc="tt")
            P.add("vector", lambda e, t=t, c=ch, p=pb: e.tensor_copy(
                sb_xlnT[:, S * c + 128 * t:S * c + 128 * t + 128], p[:, 0:128]),
                waits=[("tt", P.at("tt")), ("gd", gd_loads)], inc="tv")
            tp_i += 1
    xlnT_tv = P.at("tv")
    tap("xlnT", lambda: sb_xlnT[:, 0:2048])
    M.free("sb_xln")
    sb_qT = M.alloc("sb_qT", [128, NCH * S], BF16)
    sb_kT = M.alloc("sb_kT", [128, NCH * S], BF16)

    P.barrier("tensor", [("tv", xlnT_tv)])
    for ch in range(NCH):
        for wsb, qsb, boff in ((sb_wq, sb_qT, 0), (sb_wk, sb_kT, E)):
            for ec in range(NCH):
                for ns in range(4):
                    P.add("tensor", lambda e, ec=ec, ns=ns, ch=ch, wsb=wsb: e.matmul(
                        ps_big[:, 512 * ns:512 * ns + 512],
                        wsb[:, E * ec + 128 * ch:E * ec + 128 * ch + 128],
                        sb_xlnT[:, S * ec + 512 * ns:S * ec + 512 * ns + 512],
                        start=(ec == 0), stop=False), inc="tt")
            for ns in range(4):
                P.add("tensor", lambda e, ns=ns, ch=ch, boff=boff: e.matmul(
                    ps_big[:, 512 * ns:512 * ns + 512],
                    sb_brow[0:1, boff + 128 * ch:boff + 128 * ch + 128],
                    sb_ones[0:1, 0:512], start=False, stop=True), inc="tt")
            P.add("vector", lambda e, ch=ch, qsb=qsb: e.tensor_copy(
                qsb[:, S * ch:S * ch + S], ps_big[:, :]),
                waits=[("tt", P.at("tt"))], inc="tv")
            P.barrier("tensor", [("tv", P.at("tv"))])
            if ch == 0 and qsb is sb_qT:
                tap("qTe", lambda: sb_qT[:, 0:2048])
                tap("wq0", lambda: sb_wq[:, 0:2048])
                tap("brow", lambda: sb_brow[0:1, 0:2048])
    M.free("sb_wq", "sb_wk")
    sb_v65 = M.alloc("sb_v65", [128, NT * H * 65], BF16)
    for t in range(NT):
        P.add("vector", lambda e, t=t: e.memset(
            bass.AP(sb_v65, 1040 * t + 64, [[NT * H * 65, 128], [65, H], [1, 1]]),
            1.0), inc="tv")
    for t in range(NT):
        for ec in range(NCH):
            for ns in range(2):
                P.add("tensor", lambda e, ec=ec, ns=ns, t=t: e.matmul(
                    ps_big[:, 512 * ns:512 * ns + 512],
                    sb_xlnT[:, S * ec + 128 * t:S * ec + 128 * t + 128],
                    sb_wv[:, E * ec + 512 * ns:E * ec + 512 * ns + 512],
                    start=(ec == 0), stop=False), inc="tt")
        for ns in range(2):
            P.add("tensor", lambda e, ns=ns: e.matmul(
                ps_big[:, 512 * ns:512 * ns + 512],
                sb_ones[0:1, 0:128],
                sb_brow[0:1, 2 * E + 512 * ns:2 * E + 512 * ns + 512],
                start=False, stop=True), inc="tt")

        def v_evac(e, t=t):
            dst = bass.AP(sb_v65, 1040 * t, [[NT * H * 65, 128], [65, H], [1, 64]])
            src = bass.AP(ps_big, 0, [[2048, 128], [64, H], [1, 64]])
            return e.tensor_copy(dst, src)
        P.add("vector", v_evac, waits=[("tt", P.at("tt"))], inc="tv")
        P.barrier("tensor", [("tv", P.at("tv"))])
    proj_tv = P.at("tv")
    tap("qT", lambda: sb_qT[:, 0:2048])
    tap("kT", lambda: sb_kT[:, 0:2048])
    tap("v65", lambda: sb_v65[:, 0:2048])
    M.free("sb_wv", "sb_brow", "sb_xlnT", "sb_xt2", "sb_xt")

    sb_etf = M.alloc("sb_etf", [128, ETW], BF16)
    sb_etr = M.alloc("sb_etr", [128, ETW], BF16)
    sb_stg = M.alloc("sb_stg", [128, 2 * TW], BF16)
    sb_A = M.alloc("sb_A", [128, 8 * S], BF16)
    sb_Ss = M.alloc("sb_Ss", [128, 2 * LC], F32)
    sb_B = M.alloc("sb_B", [128, 32 * LC], BF16)
    sb_X = M.alloc("sb_X", [128, 3 * LC], BF16)
    for half in range(2):
        P.add("gpsimd", lambda e, hf=half: e.dma_start(
            out=sb_etf[64 * hf:64 * hf + 64, :], in_=d_etf),
            waits=[("tv", proj_tv)] if half == 0 else [], inc="gd", amt=16)
        P.add("gpsimd", lambda e, hf=half: e.dma_start(
            out=sb_etr[64 * hf:64 * hf + 64, :], in_=d_etr), inc="gd", amt=16)
    gd_tbl = P.at("gd")

    sb_sq = sb_A

    P.add("vector", lambda e: e.tensor_copy(sb_edg[:, 0:1], sb_etf[:, 0:1]),
          waits=[("gd", gd_tbl)], inc="tv")
    P.add("vector", lambda e: e.tensor_copy(sb_edg[:, 1:2], sb_etf[:, ETW - 1:ETW]),
          inc="tv")

    P.add("vector", lambda e: e.tensor_mul(
        sb_sq[0:64, 0:2048], sb_etf[0:64, 0:2048], sb_etf[0:64, 0:2048]),
        waits=[("gd", gd_tbl), ("tt", P.at("tt"))], inc="tv")
    P.barrier("tensor", [("tv", P.at("tv"))])
    for ns in range(4):
        P.add("tensor", lambda e, ns=ns: e.matmul(
            ps_big[0:1, 512 * ns:512 * ns + 512], sb_ones[0:64, 0:1],
            sb_sq[0:64, 512 * ns:512 * ns + 512], start=True, stop=False), inc="tt")
    P.add("vector", lambda e: e.tensor_mul(
        sb_sq[0:64, 0:2048], sb_etf[0:64, 2048:4096], sb_etf[0:64, 2048:4096]),
        waits=[("tt", P.at("tt"))], inc="tv")
    P.barrier("tensor", [("tv", P.at("tv"))])
    for ns in range(4):
        P.add("tensor", lambda e, ns=ns: e.matmul(
            ps_big[0:1, 512 * ns:512 * ns + 512], sb_ones[0:64, 0:1],
            sb_sq[0:64, 512 * ns:512 * ns + 512], start=False, stop=True), inc="tt")
    P.add("vector", lambda e: e.reduce_max(
        sb_tmp[0:1, 0:1], ps_big[0:1, :], axis=AX),
        waits=[("tt", P.at("tt"))], inc="tv", hz=True)
    P.add("scalar", lambda e: e.activation(
        sb_tmp[0:1, 1:2], sb_tmp[0:1, 0:1], AF.Sqrt),
        waits=[("tv", P.at("tv"))], inc="ts")
    me_ts = P.at("ts")
    P.barrier("tensor", [("tv", P.at("tv"))])

    sq_parity = 0
    for h in range(H):
        ch, half = h // 2, h % 2
        qsl = slice(64 * half, 64 * half + 64)
        tpos = (64 * half, 0)
        for src, col in ((sb_qT, 2), (sb_kT, 3)):
            P.add("vector", lambda e, src=src, ch=ch, qsl=qsl: e.tensor_mul(
                sb_sq[qsl, 0:S], src[qsl, S * ch:S * ch + S],
                src[qsl, S * ch:S * ch + S]),
                waits=[("tv", proj_tv), ("tt", P.at("tt"))], inc="tv")
            P.barrier("tensor", [("tv", P.at("tv"))])
            for ns in range(4):
                P.add("tensor", lambda e, ns=ns, qsl=qsl, tpos=tpos: e.matmul(
                    ps_big[0:1, 512 * ns:512 * ns + 512], sb_ones[qsl, 0:1],
                    sb_sq[qsl, 512 * ns:512 * ns + 512], start=True, stop=True,
                    tile_position=tpos), inc="tt")
            P.add("vector", lambda e, col=col: e.reduce_max(
                sb_tmp[0:1, col:col + 1], ps_big[0:1, :], axis=AX),
                waits=[("tt", P.at("tt"))], inc="tv", hz=True)
            P.barrier("tensor", [("tv", P.at("tv"))])
        P.add("scalar", lambda e: e.activation(
            sb_tmp[0:1, 4:5], sb_tmp[0:1, 2:3], AF.Sqrt),
            waits=[("tv", P.at("tv")), ("ts", me_ts)], inc="ts")
        P.add("scalar", lambda e: e.activation(
            sb_tmp[0:1, 5:6], sb_tmp[0:1, 3:4], AF.Sqrt), inc="ts")
        P.add("vector", lambda e: e.tensor_mul(
            sb_tmp[0:1, 6:7], sb_tmp[0:1, 4:5], sb_tmp[0:1, 5:6]),
            waits=[("ts", P.at("ts"))], inc="tv")
        P.add("vector", lambda e: e.tensor_add(
            sb_tmp[0:1, 7:8], sb_tmp[0:1, 4:5], sb_tmp[0:1, 5:6]), inc="tv")
        P.add("vector", lambda e: e.tensor_mul(
            sb_tmp[0:1, 7:8], sb_tmp[0:1, 7:8], sb_tmp[0:1, 1:2]), inc="tv")
        P.add("vector", lambda e: e.tensor_add(
            sb_tmp[0:1, 6:7], sb_tmp[0:1, 6:7], sb_tmp[0:1, 7:8]), inc="tv")
        P.add("vector", lambda e: e.tensor_scalar(
            sb_nbh[0:1, 0:1], sb_tmp[0:1, 6:7], -1.02 / 8.0, None, ALU.mult),
            inc="tv")
        P.add("tensor", lambda e, h=h: e.matmul(
            ps_d[:, 256 + h:256 + h + 1], sb_ones[0:1, 0:128], sb_nbh[0:1, 0:1],
            start=True, stop=True), waits=[("tv", P.at("tv"))], inc="tt")
        P.add("vector", lambda e, h=h: e.tensor_copy(
            sb_negB[:, h:h + 1], ps_d[:, 256 + h:256 + h + 1]),
            waits=[("tt", P.at("tt"))], inc="tv")
        P.barrier("tensor", [("tv", P.at("tv"))])
    negb_tv = P.at("tv")
    if debug:
        tap("negB", lambda: sb_negB[:, 0:1])

    stg_order = [(0, t) for t in range(4)] + [(1, j) for j in range(NT)] \
        + [(0, t) for t in range(4, NT)]
    rot_par = [0]
    rot_ev = [("tv", P.at("tv")), ("tv", P.at("tv"))]
    stg_tile_idx = [0]
    stg_dma_hist = []
    stg_sd = {hh: {} for hh in range(H)}

    def emit_stage_tile(hs, kind, t):
        ch2, half2 = hs // 2, hs % 2
        qsl2 = slice(64 * half2, 64 * half2 + 64)
        tpos2 = (64 * half2, 0)
        src = sb_qT if kind == 0 else sb_kT
        tbl = sb_etr if kind == 0 else sb_etf
        dstD = d_qe if kind == 0 else d_ke
        voff = 1920 - 128 * t
        idx_t = stg_tile_idx[0]
        stg_tile_idx[0] += 1
        sl = idx_t % 2
        ev_last = {}
        for c in range(5):
            ncols = 512 if c < 4 else 128
            par = rot_par[0]
            rot_par[0] ^= 1
            P.add("tensor", lambda e, c=c, ncols=ncols, par=par, t=t,
                  src=src, tbl=tbl, ch2=ch2, qsl2=qsl2, tpos2=tpos2,
                  voff=voff: e.matmul(
                ps_big[:, 512 * par:512 * par + ncols],
                src[qsl2, S * ch2 + 128 * t:S * ch2 + 128 * t + 128],
                tbl[qsl2, voff + 512 * c:voff + 512 * c + ncols],
                start=True, stop=True, tile_position=tpos2),
                waits=[rot_ev[par]], inc="tt")
            tt_c = P.at("tt")
            eng, sem = ("vector", "tv") if c % 2 == 0 else ("scalar", "ts")
            we = [("tt", tt_c)]
            if c < 2 and idx_t >= 2:
                we.append(("sd", stg_dma_hist[idx_t - 2]))
            if eng == "vector":
                P.add(eng, lambda e, c=c, ncols=ncols, par=par, sl=sl:
                      e.tensor_copy(
                    sb_stg[:, TW * sl + 512 * c:TW * sl + 512 * c + ncols],
                    ps_big[:, 512 * par:512 * par + ncols]),
                    waits=we, inc=sem)
            else:
                P.add(eng, lambda e, c=c, ncols=ncols, par=par, sl=sl:
                      e.copy(
                    sb_stg[:, TW * sl + 512 * c:TW * sl + 512 * c + ncols],
                    ps_big[:, 512 * par:512 * par + ncols]),
                    waits=we, inc=sem)
            rot_ev[par] = (sem, P.at(sem))
            ev_last[sem] = P.at(sem)
        P.add("sync", lambda e, sl=sl, t=t, dstD=dstD, hs=hs: e.dma_start(
            out=dstD[hs, t], in_=sb_stg[:, TW * sl:TW * sl + TW]),
            waits=[(s, v) for s, v in ev_last.items()], inc="sd", amt=16)
        stg_sd[hs][(kind, t)] = P.at("sd")
        stg_dma_hist.append(P.at("sd"))

    def emit_edge_dots(hs):
        ch2, half2 = hs // 2, hs % 2
        qsl2 = slice(64 * half2, 64 * half2 + 64)
        tpos2 = (64 * half2, 0)
        for eid in range(2):
            for ns in range(4):
                par = rot_par[0]
                rot_par[0] ^= 1
                P.add("tensor", lambda e, ns=ns, ch2=ch2, qsl2=qsl2,
                      tpos2=tpos2, eid=eid, par=par: e.matmul(
                    ps_big[0:1, 512 * par:512 * par + 512],
                    sb_edg[qsl2, eid:eid + 1],
                    sb_qT[qsl2, S * ch2 + 512 * ns:S * ch2 + 512 * ns + 512],
                    start=True, stop=True, tile_position=tpos2),
                    waits=[rot_ev[par]], inc="tt")
                P.add("vector", lambda e, eid=eid, ns=ns, par=par:
                      e.tensor_copy(
                    sb_qe2[0:1, S * eid + 512 * ns:S * eid + 512 * ns + 512],
                    ps_big[0:1, 512 * par:512 * par + 512]),
                    waits=[("tt", P.at("tt"))], inc="tv")
                rot_ev[par] = ("tv", P.at("tv"))
        P.barrier("tensor", [("tv", P.at("tv"))])

    for kind, t in stg_order:
        emit_stage_tile(0, kind, t)

    ga0_carry = None
    bgd_carry = {}
    atr_lc2_end = None
    for h in range(H):
        ch, half = h // 2, h % 2
        qsl = slice(64 * half, 64 * half + 64)
        tpos = (64 * half, 0)
        emit_edge_dots(h)

        exp_hist = {}
        stt_hist = {}
        slot_free = {}
        vb1_hist = {}
        big_bf5 = ps_big.bitcast(BF16)
        abf_slot = [big_bf5[:, 2048:2560],
                    big_bf5[:, 3072:3584]]

        def emit_a_gather(hs, lc_t, i, wA):
            half = lc_t % 2
            t = 4 * lc_t + i
            P.add("gpsimd", lambda e, t=t, i=i, hs=hs, half=half:
                  e.indirect_dma_start(
                sb_A[:, S * (4 * half + i):S * (4 * half + i) + S], None,
                d_qe_flat, IndirectOffsetOnAxis(sb_idxA[:], 1),
                element_offset=(hs * NT + t) * 128 * TW),
                waits=wA + [("sd", stg_sd[hs][(0, t)])], inc="gd", amt=16)

        def emit_b_pair(hs, pr, j, wtv):
            P.add("gpsimd", lambda e, j=j, pr=pr, hs=hs: e.indirect_dma_start(
                sb_B[:, 2 * LC * j:2 * LC * j + 2 * LC], None,
                d_ke_flat, IndirectOffsetOnAxis(sb_idxB[:], 1),
                element_offset=(hs * NT + j) * 128 * TW + 2 * LC * pr),
                waits=[wtv, ("sd", stg_sd[hs][(1, j)])], inc="gd", amt=16)

        if h == 0:
            for i in range(4):
                emit_a_gather(0, 0, i, [("tt", P.at("tt"))])
            ga_hist = {0: P.at("gd")}
            bgd = {}
            for j in range(NT):
                emit_b_pair(0, 0, j, ("tv", P.at("tv")))
                bgd[(0, j)] = P.at("gd")
        else:
            ga_hist = {0: ga0_carry}
            bgd = bgd_carry
        bgd_carry = {}
        for lc in range(NLC):
            half = lc % 2
            slot_free[(lc, 0)] = ("tv", P.at("tv"))
            slot_free[(lc, 1)] = ("tv", P.at("tv"))
            slot_free.pop(0, None)
            slot_free.pop(1, None)
            if h == 0 and lc == 0 and debug:
                tap("A", lambda: sb_A[:, 0:2048], wait_sem="gd")
            for j in range(NT):
                pj = [ps_a[:, :], ps_b[:, :]][j % 2]
                sl = j % 2
                gb_done = bgd[(lc - lc % 2, j)]
                if lc + 1 < NLC and j in (8, 10, 12, 14):
                    lt = lc + 1
                    wA = [("tt", exp_hist[(lt - 2, NT - 1, "pv")])] \
                        if lt >= 2 else [("tt", P.at("tt"))]
                    emit_a_gather(h, lt, (j - 8) // 2, wA)
                    if j == 14:
                        ga_hist[lt] = P.at("gd")
                edg_runs = []
                tr_list = []
                for i in range(4):
                    t_l = 4 * lc + i
                    if t_l - j <= -9:
                        eid = 0
                    elif t_l - j >= 9:
                        eid = 1
                    else:
                        tr_list.append(i)
                        continue
                    if edg_runs and edg_runs[-1][0] == eid and edg_runs[-1][2] == i:
                        edg_runs[-1] = (eid, edg_runs[-1][1], i + 1)
                    else:
                        edg_runs.append((eid, i, i + 1))
                wt = []
                if (lc, j - 2) in stt_hist:
                    wt.append(("tv", stt_hist[(lc, j - 2)]))
                elif lc > 0 and (lc - 1, 14 + j) in stt_hist:
                    wt.append(("tv", stt_hist[(lc - 1, 14 + j)]))
                else:
                    wt.append(("tv", P.at("tv")))
                P.add("tensor", lambda e, j=j, lc=lc, ch=ch, qsl=qsl,
                      tpos=tpos, p=pj, st=(not edg_runs): e.matmul(
                    p,
                    sb_kT[qsl, S * ch + 128 * j:S * ch + 128 * j + 128],
                    sb_qT[qsl, S * ch + LC * lc:S * ch + LC * lc + LC],
                    start=True, stop=st, tile_position=tpos),
                    waits=wt, inc="tt")
                for ei, (eid, i0, i1) in enumerate(edg_runs):
                    P.add("tensor", lambda e, p=pj, eid=eid, i0=i0, i1=i1,
                          lc=lc, st=(ei == len(edg_runs) - 1): e.matmul(
                        p[:, 128 * i0:128 * i1],
                        sb_ones[0:1, 0:128],
                        sb_qe2[0:1, S * eid + LC * lc + 128 * i0:
                               S * eid + LC * lc + 128 * i1],
                        start=False, stop=st), inc="tt")
                tt_scores = P.at("tt")
                wtr = [("gd", ga_hist[lc])]
                sf = slot_free.get((lc, sl)) if j < 2 else slot_free.get(sl)
                if sf is None:
                    sf = ("tv", P.at("tv"))
                wtr.append(sf)
                first_tr = True
                for i in tr_list:
                    P.add("tensor", lambda e, i=i, j=j, half=half, sl=sl:
                          e.transpose(
                        abf_slot[sl][:, 128 * i:128 * i + 128],
                        sb_A[:, S * (4 * half + i) + 128 * j:
                             S * (4 * half + i) + 128 * j + 128],
                        sb_iden[:, :]),
                        waits=(wtr if first_tr else []), inc="tt")
                    first_tr = False
                tt_atr = P.at("tt")
                wexp_extra = []
                if (lc, j - 2, "exp") in exp_hist:
                    wexp_extra = [("ts", exp_hist[(lc, j - 2, "exp")])]
                elif lc > 0:
                    wexp_extra = [("ts", exp_hist[(lc - 1, 14 + j, "exp")])]
                boff = 2 * LC * j + LC * (lc % 2)
                if tr_list:
                    P.add("vector", lambda e, boff=boff, sl=sl: e.tensor_tensor(
                        sb_Ss[:, LC * sl:LC * sl + LC], abf_slot[sl][:, 0:LC],
                        sb_B[:, boff:boff + LC], ALU.add),
                        waits=[("tt", tt_atr), ("gd", gb_done)] + wexp_extra,
                        inc="tv")
                    vb1_hist[(lc, j)] = P.at("tv")
                    slot_free[sl] = ("tv", P.at("tv"))
                    P.add("vector", lambda e, p=pj, sl=sl: e.scalar_tensor_tensor(
                        sb_Ss[:, LC * sl:LC * sl + LC], p, 1.0,
                        sb_Ss[:, LC * sl:LC * sl + LC], ALU.mult, ALU.add),
                        waits=[("tt", tt_scores)], inc="tv")
                else:
                    P.add("vector", lambda e, p=pj, boff=boff, sl=sl:
                          e.scalar_tensor_tensor(
                        sb_Ss[:, LC * sl:LC * sl + LC], p, 1.0,
                        sb_B[:, boff:boff + LC],
                        ALU.mult, ALU.add),
                        waits=[("tt", tt_scores), ("gd", gb_done)] + wexp_extra,
                        inc="tv")
                    vb1_hist[(lc, j)] = P.at("tv")
                stt_hist[(lc, j)] = P.at("tv")
                if lc == 1:
                    emit_b_pair(h, 1, j, ("tv", vb1_hist[(1, j)]))
                    bgd[(2, j)] = P.at("gd")
                if lc == 3 and h + 1 < H:
                    emit_b_pair(h + 1, 0, j, ("tv", vb1_hist[(3, j)]))
                    bgd_carry[(0, j)] = P.at("gd")
                    if j in (1, 3, 5, 7):
                        emit_a_gather(h + 1, 0, (j - 1) // 2,
                                      [("tt", atr_lc2_end)])
                    if j == 7:
                        ga0_carry = P.at("gd")
                if debug and h == 0 and lc == 0 and j == 0:
                    tap("Bt", lambda: sb_B[:, 0:LC], wait_sem="gd")
                    tap("scr", lambda: sb_Ss[:, 0:LC], wait_sem="tv")
                P.add("scalar", lambda e, sl=sl, x4=j % 3, h=h: e.activation(
                    sb_X[:, LC * x4:LC * x4 + LC],
                    sb_Ss[:, LC * sl:LC * sl + LC], AF.Exp,
                    bias=sb_negB[:, h:h + 1], scale=0.125),
                    waits=[("tv", stt_hist[(lc, j)]), ("tv", negb_tv)], inc="ts")
                exp_hist[(lc, j, "exp")] = P.at("ts")
                if debug and h == 0 and lc == 0 and j == 0:
                    tap("X", lambda: sb_X[:, 0:LC], wait_sem="ts")

                def emit_pv(jp):
                    wpv = [("ts", exp_hist[(lc, jp, "exp")])]
                    if jp == 0:
                        wpv.append(("tv", P.at("tv")))
                    P.add("tensor", lambda e, jp=jp, h=h: e.matmul(
                        ps_c[0:65, :],
                        sb_v65[:, 1040 * jp + 65 * h:1040 * jp + 65 * h + 65],
                        sb_X[:, LC * (jp % 3):LC * (jp % 3) + LC],
                        start=(jp == 0), stop=(jp == NT - 1)),
                        waits=wpv, inc="tt")
                    exp_hist[(lc, jp, "pv")] = P.at("tt")
                if j >= 2:
                    emit_pv(j - 2)
                gi = lc * NT + j
                if h + 1 < H and gi % 2 == 0 and gi // 2 < len(stg_order):
                    kind2, t2 = stg_order[gi // 2]
                    emit_stage_tile(h + 1, kind2, t2)
                if j == NT - 1:
                    emit_pv(NT - 2)
                    emit_pv(NT - 1)
            if lc == 2:
                atr_lc2_end = P.at("tt")
            P.add("vector", lambda e: e.tensor_copy(
                sb_ctv[0:65, :], ps_c[0:65, :]),
                waits=[("tt", P.at("tt")), ("sd", P.at("sd"))], inc="tv")
            for i in range(4):
                wtr2 = [("tv", P.at("tv"))] if i == 0 else []
                P.add("tensor", lambda e, i=i: e.transpose(
                    ps_d[:, 65 * i:65 * i + 65],
                    sb_ctv[0:65, 128 * i:128 * i + 128],
                    sb_idenf[0:65, 0:65]), waits=wtr2, inc="tt")
            P.add("vector", lambda e: e.reciprocal(
                sb_zr[:, 0:4], bass.AP(ps_d, 64, [[512, 128], [65, 4]])),
                waits=[("tt", P.at("tt"))], inc="tv", hz=True)
            P.add("vector", lambda e: e.tensor_tensor(
                bass.AP(sb_ctp, 0, [[256, 128], [64, 4], [1, 64]]),
                bass.AP(ps_d, 0, [[512, 128], [65, 4], [1, 64]]),
                bass.AP(sb_zr, 0, [[4, 128], [1, 4], [0, 64]]),
                ALU.mult), inc="tv")
            def ctx_store(e, h=h, lc=lc):
                dst = bass.AP(d_ctx.tensor, h * S * D + LC * lc * D,
                              [[D, 128], [128 * D, 4], [1, D]])
                src = bass.AP(sb_ctp, 0, [[4 * D, 128], [D, 4], [1, D]])
                return e.dma_start(out=dst, in_=src)
            P.add("sync", ctx_store, waits=[("tv", P.at("tv"))], inc="sd", amt=16)
    attn_sd = P.at("sd")

    M.free("sb_X", "sb_B", "sb_Ss", "sb_A", "sb_stg", "sb_etr",
           "sb_etf", "sb_v65", "sb_kT", "sb_qT")
    sb_xpa = M.alloc("sb_xpa", [128, NT * E], F32)
    sb_xd = M.alloc("sb_xd", [128, NT * E], BF16)
    sb_wd = M.alloc("sb_wd", [128, NCH * E], BF16)
    sb_ctl = M.alloc("sb_ctl", [128, E], F32)
    sb_ffw = M.alloc("sb_ffw", [128, E], F32)
    sb_q1 = M.alloc("sb_q1", [128, E], F32)
    sb_q2 = M.alloc("sb_q2", [128, E], F32)
    sb_q3 = M.alloc("sb_q3", [128, E], F32)
    sb_sc = M.alloc("sb_sc", [128, 32], F32)
    sb_scr = M.alloc("sb_scr", [128, 32], F32)
    sb_dump = M.alloc("sb_dump", [128, E], F32)

    sb_xdT = M.alloc("sb_xdT", [128, NCH * S], BF16)
    sb_wdT = M.alloc("sb_wdT", [128, NCH * E], BF16)
    sb_yt = M.alloc("sb_yt", [128, E], F32)

    def emit_quant(dst_ap_fn, src_sb):
        def a3(sb, off=0):
            return bass.AP(sb, off, [[E, 128], [32, 32], [1, 32]])

        def scbc(sb):
            return bass.AP(sb, 0, [[32, 128], [1, 32], [0, 32]])
        P.add("vector", lambda e: e.tensor_reduce(
            sb_sc[:], a3(src_sb), axis=AX, op=ALU.max,
            apply_absolute_value=True), inc="tv", hz=True)
        P.add("vector", lambda e: e.tensor_scalar(
            sb_sc[:], sb_sc[:], 1.0 / 7.5, 1e-30, ALU.mult, ALU.max), inc="tv")
        P.add("vector", lambda e: e.reciprocal(sb_scr[:], sb_sc[:]), inc="tv", hz=True)
        P.add("vector", lambda e: e.tensor_tensor(
            a3(sb_q1), a3(src_sb), scbc(sb_scr), ALU.mult), inc="tv")
        P.add("scalar", lambda e: e.activation(
            sb_q2[:], sb_q1[:], AF.Square),
            waits=[("tv", P.at("tv"))], inc="ts")
        P.add("vector", lambda e: e.tensor_scalar(
            sb_q3[:], sb_q2[:], 4.0, None, ALU.is_ge),
            waits=[("ts", P.at("ts"))], inc="tv")
        P.add("vector", lambda e: e.tensor_scalar(
            sb_q2[:], sb_q2[:], 16.0, None, ALU.is_ge), inc="tv")
        P.add("vector", lambda e: e.tensor_scalar(
            sb_q3[:], sb_q3[:], -4.0, 8.0, ALU.mult, ALU.add), inc="tv")
        P.add("vector", lambda e: e.scalar_tensor_tensor(
            sb_q3[:], sb_q2[:], -2.0, sb_q3[:], ALU.mult, ALU.add), inc="tv")
        P.add("vector", lambda e: e.reciprocal(
            sb_dump[:], sb_q3[:]), inc="tv", hz=True)
        P.add("vector", lambda e: e.tensor_mul(sb_q1[:], sb_q1[:], sb_q3[:]), inc="tv")
        P.add("vector", lambda e: e.tensor_copy(
            sb_q2.bitcast(I32)[:], sb_q1[:]), inc="tv")
        P.add("scalar", lambda e: e.copy(
            sb_q1[:], sb_q2.bitcast(I32)[:]),
            waits=[("tv", P.at("tv"))], inc="ts")
        P.add("vector", lambda e: e.tensor_mul(sb_q1[:], sb_q1[:], sb_dump[:]),
              waits=[("ts", P.at("ts"))], inc="tv")
        P.add("vector", lambda e: e.tensor_tensor(
            dst_ap_fn(), a3(sb_q1), scbc(sb_sc), ALU.mult), inc="tv")

    tp_i = 0
    tpev = {}

    def emit_tr_block(src_sb, src_off, dst_fn, n=NCH):
        nonlocal tp_i
        for kc in range(n):
            pb = psA_bf if tp_i % 2 == 0 else psB_bf
            w = [("tv", P.at("tv"))] if kc == 0 else []
            if tp_i >= 2:
                w.append(("ts", tpev[tp_i - 2]))
            P.add("tensor", lambda e, kc=kc, p=pb, src_sb=src_sb,
                  src_off=src_off: e.transpose(
                p[:, 0:128], src_sb[:, src_off + 128 * kc:src_off + 128 * kc + 128],
                sb_iden[:]), waits=w, inc="tt")
            P.add("scalar", lambda e, kc=kc, p=pb: e.copy(
                dst_fn(kc), p[:, 0:128]),
                waits=[("tt", P.at("tt"))], inc="ts")
            tpev[tp_i] = P.at("ts")
            tp_i += 1

    for fc in range(NCH):
        P.add("sync", lambda e, fc=fc: e.dma_start(
            out=sb_ffw[:], in_=d_ffw[128 * fc:128 * fc + 128, :]),
            waits=[("tv", P.at("tv"))], inc="sd", amt=16)
        P.barrier("vector", [("sd", P.at("sd"))])
        emit_quant(lambda fc=fc: bass.AP(sb_wd, E * fc,
                                         [[NCH * E, 128], [32, 32], [1, 32]]),
                   sb_ffw)
        emit_tr_block(sb_wd, E * fc, lambda kc, fc=fc:
                      sb_wdT[:, E * kc + 128 * fc:E * kc + 128 * fc + 128])
        if fc == 0:
            tap("wd", lambda: sb_wd[:, 0:E])

    y_sd = {}
    ffn_tt = {}
    addv = {}
    for t in range(NT):
        buf = sb_xt if t % 2 == 0 else sb_xt2
        w = [("tv", P.at("tv"))] if t >= 2 else []
        P.add("sync", lambda e, t=t, b=buf: e.dma_start(
            out=b[:], in_=d_x[128 * t:128 * t + 128, :]), waits=w, inc="sd", amt=16)

        def ctx_ld(e, t=t):
            src = bass.AP(d_ctx.tensor, 128 * t * D, [[D, 128], [S * D, H], [1, D]])
            dst = bass.AP(sb_ctl, 0, [[E, 128], [D, H], [1, D]])
            return e.dma_start(out=dst, in_=src)
        wc = [("sd", attn_sd)]
        if t >= 1:
            wc.append(("tv", addv[t - 1]))
        P.add("sync", ctx_ld, waits=wc, inc="sd", amt=16)
        P.add("vector", lambda e, t=t, b=buf: e.tensor_add(
            bass.AP(sb_xpa, E * t, [[NT * E, 128], [1, E]]), b[:], sb_ctl[:]),
            waits=[("sd", P.at("sd"))], inc="tv")
        addv[t] = P.at("tv")
        emit_ln(lambda: sb_ffw[:],
                lambda t=t: bass.AP(sb_xpa, E * t, [[NT * E, 128], [1, E]]))
        emit_quant(lambda t=t: bass.AP(sb_xd, E * t,
                                       [[NT * E, 128], [32, 32], [1, 32]]),
                   sb_ffw)
        if t == 0:
            tap("xpa", lambda: sb_xpa[:, 0:E])
            tap("xd", lambda: sb_xd[:, 0:E])
        if t >= 1:
            wv = [("tt", ffn_tt[t - 1])]
            if t >= 2:
                wv.append(("sd", y_sd[t - 2]))
            P.add("vector", lambda e, t=t: e.tensor_add(
                sb_yt[:, 0:E], ps_big[:, 0:1024],
                sb_xpa[:, E * (t - 1):E * (t - 1) + E]), waits=wv, inc="tv")
            P.add("sync", lambda e, t=t: e.dma_start(
                out=d_y[128 * (t - 1):128 * (t - 1) + 128, :],
                in_=sb_yt[:, 0:E]), waits=[("tv", P.at("tv"))], inc="sd", amt=16)
            y_sd[t - 1] = P.at("sd")
        yadd_tv = P.at("tv")
        emit_tr_block(sb_xd, E * t, lambda kc, t=t:
                      sb_xdT[:, S * kc + 128 * t:S * kc + 128 * t + 128])
        for kc in range(NCH):
            for ns in range(2):
                wm = [("ts", tpev[tp_i - 1]), ("tv", yadd_tv)] \
                    if (kc == 0 and ns == 0) else []
                P.add("tensor", lambda e, t=t, kc=kc, ns=ns: e.matmul(
                    ps_big[:, 512 * ns:512 * ns + 512],
                    sb_xdT[:, S * kc + 128 * t:S * kc + 128 * t + 128],
                    sb_wdT[:, E * kc + 512 * ns:E * kc + 512 * ns + 512],
                    start=(kc == 0), stop=False), waits=wm, inc="tt")
        for ns in range(2):
            P.add("tensor", lambda e, ns=ns: e.matmul(
                ps_big[:, 512 * ns:512 * ns + 512],
                sb_ones[0:1, 0:128], sb_ffb[0:1, 512 * ns:512 * ns + 512],
                start=False, stop=True), inc="tt")
        ffn_tt[t] = P.at("tt")
    P.add("vector", lambda e: e.tensor_add(
        sb_yt[:, 0:E], ps_big[:, 0:1024], sb_xpa[:, E * 15:E * 15 + E]),
        waits=[("tt", ffn_tt[NT - 1]), ("sd", y_sd[NT - 2])], inc="tv")
    P.add("sync", lambda e: e.dma_start(
        out=d_y[128 * 15:128 * 15 + 128, :], in_=sb_yt[:, 0:E]),
        waits=[("tv", P.at("tv"))], inc="sd", amt=16)

    P.add("sync", lambda e: None, waits=[("sd", P.at("sd"))])

    def replay(name, eng):
        for fn, waits, inc, amt in P.items[name]:
            for s_, v in waits:
                eng.wait_ge(sem_h[s_], v)
            inst = fn(eng)
            if inc is not None:
                inst.then_inc(sem_h[inc], amt)

    @block.sync
    def _(sync):
        replay("sync", sync)

    @block.vector
    def _(vector):
        replay("vector", vector)

    @block.scalar
    def _(scalar):
        replay("scalar", scalar)

    @block.tensor
    def _(tensor):
        replay("tensor", tensor)

    @block.gpsimd
    def _(gpsimd):
        replay("gpsimd", gpsimd)

    block_cm.__exit__(None, None, None)
    M.free_all()
    for cm in reversed(psum_cms):
        cm.__exit__(None, None, None)
    for cm in reversed(sem_cms):
        cm.__exit__(None, None, None)
    return nc


_NC_CACHE = [None]


def _host_prep(inputs):
    x = np.ascontiguousarray(np.asarray(inputs["L0_attn_hidden_states"], dtype=np.float32))
    Et = np.asarray(inputs["L0_attn_dist_emb"], dtype=np.float32)
    E_ext = np.concatenate([
        np.broadcast_to(Et[0:1], (1024, D)), Et,
        np.broadcast_to(Et[-1:], (1025, D))], axis=0)
    p = np.arange(128, dtype=np.int32).reshape(128, 1)
    base = {
        "wqT": np.ascontiguousarray(np.asarray(inputs["L0_attn_wq"], np.float32).T),
        "wkT": np.ascontiguousarray(np.asarray(inputs["L0_attn_wk"], np.float32).T),
        "wvT": np.ascontiguousarray(np.asarray(inputs["L0_attn_wv"], np.float32).T),
        "bq": np.ascontiguousarray(np.asarray(inputs["L0_attn_bq"], np.float32).reshape(1, E)),
        "bk": np.ascontiguousarray(np.asarray(inputs["L0_attn_bk"], np.float32).reshape(1, E)),
        "bv": np.ascontiguousarray(np.asarray(inputs["L0_attn_bv"], np.float32).reshape(1, E)),
        "etf": np.ascontiguousarray(E_ext.T),
        "etr": np.ascontiguousarray(E_ext[::-1].T),
        "ffw": np.ascontiguousarray(np.asarray(inputs["L0_ffn_weight"], np.float32)),
        "ffb": np.ascontiguousarray(np.asarray(inputs["L0_ffn_bias"], np.float32).reshape(1, E)),
        "idxA": (TW - 1) * p + 128,
        "idxB": (TW - 1) * p + 127,
        "idn": np.ascontiguousarray(np.eye(128, dtype=np.float32)),
    }
    return x, base


def debug_run(inputs, core=0):
    x, base = _host_prep(inputs)
    nc = build_program(debug=True)
    im = dict(base)
    im["x"] = np.ascontiguousarray(x[core])
    res = run_bass_kernel_spmd(nc, [im], core_ids=[0])
    return res.results[0]


def kernel(L0_attn_hidden_states, L0_attn_wq, L0_attn_bq, L0_attn_wk,
           L0_attn_bk, L0_attn_wv, L0_attn_bv, L0_attn_dist_emb,
           L0_ffn_weight, L0_ffn_bias):
    x = np.ascontiguousarray(np.asarray(L0_attn_hidden_states, dtype=np.float32))
    wq = np.asarray(L0_attn_wq, dtype=np.float32)
    wk = np.asarray(L0_attn_wk, dtype=np.float32)
    wv = np.asarray(L0_attn_wv, dtype=np.float32)
    bq = np.ascontiguousarray(np.asarray(L0_attn_bq, dtype=np.float32).reshape(1, E))
    bk = np.ascontiguousarray(np.asarray(L0_attn_bk, dtype=np.float32).reshape(1, E))
    bv = np.ascontiguousarray(np.asarray(L0_attn_bv, dtype=np.float32).reshape(1, E))
    Et = np.asarray(L0_attn_dist_emb, dtype=np.float32)
    ffw = np.ascontiguousarray(np.asarray(L0_ffn_weight, dtype=np.float32))
    ffb = np.ascontiguousarray(np.asarray(L0_ffn_bias, dtype=np.float32).reshape(1, E))

    E_ext = np.concatenate([
        np.broadcast_to(Et[0:1], (1024, D)),
        Et,
        np.broadcast_to(Et[-1:], (1025, D)),
    ], axis=0)
    etf = np.ascontiguousarray(E_ext.T)
    etr = np.ascontiguousarray(E_ext[::-1].T)

    wqT = np.ascontiguousarray(wq.T)
    wkT = np.ascontiguousarray(wk.T)
    wvT = np.ascontiguousarray(wv.T)

    p = np.arange(128, dtype=np.int32).reshape(128, 1)
    idxA = (TW - 1) * p + 128
    idxB = (TW - 1) * p + 127
    iden = np.ascontiguousarray(np.eye(128, dtype=np.float32))

    if _NC_CACHE[0] is None:
        _NC_CACHE[0] = build_program()
    nc = _NC_CACHE[0]

    in_maps = []
    for c in range(8):
        in_maps.append({
            "x": np.ascontiguousarray(x[c]),
            "wqT": wqT, "wkT": wkT, "wvT": wvT,
            "bq": bq, "bk": bk, "bv": bv,
            "etf": etf, "etr": etr,
            "ffw": ffw, "ffb": ffb,
            "idxA": idxA, "idxB": idxB, "idn": iden,
        })
    res = run_bass_kernel_spmd(nc, in_maps, core_ids=list(range(8)))
    return np.stack([r["y"] for r in res.results], axis=0)
